# version 12
# baseline (speedup 1.0000x reference)
from contextlib import ExitStack

import numpy as np
import ml_dtypes

import concourse.bass as bass
import concourse.mybir as mybir
from concourse.alu_op_type import AluOpType
from concourse.bass_utils import run_bass_kernel_spmd
from concourse.replica_groups import maybe_share_collective_output_space

F32 = mybir.dt.float32
BF16 = mybir.dt.bfloat16
I32 = mybir.dt.int32
U32 = mybir.dt.uint32
AF = mybir.ActivationFunctionType
OP = AluOpType

BF = ml_dtypes.bfloat16

D_MODEL = 1024
NB = 2
T_FULL = 8192
N_CORES = 8
TPC = 2048

K_CONV = 4
MEPS = 1e-6
LNEPS = 1e-5
LNC16 = 2.772588722239781
EPS16 = 1e-6 / 16.0
QS = 1.0 / 127.0

RG4 = [[0, 1, 2, 3], [4, 5, 6, 7]]
RG8 = [[0, 1, 2, 3, 4, 5, 6, 7]]


def f32(x):
    return np.ascontiguousarray(np.asarray(x), dtype=np.float32)


class P:

    def __init__(self, nc, stack, names):
        self.nc = nc
        self.sems = {}
        self.cnt = {}
        for name in names:
            self.sems[name] = stack.enter_context(nc.semaphore(name))
            self.cnt[name] = 0

    def inc(self, instr, name, n=1):
        instr.then_inc(self.sems[name], n)
        self.cnt[name] += n
        return instr

    def wait(self, eng, *names):
        for name in names:
            eng.wait_ge(self.sems[name], self.cnt[name])

    def acx(self, eng, instr):
        self.inc(instr, "ac")
        eng.drain()
        return instr

    def dvx(self, eng, instr):
        self.inc(instr, "dv")
        eng.drain()
        return instr

    def gpx(self, eng, instr):
        self.inc(instr, "gp")
        eng.drain()
        return instr

    def at(self, name):
        return self.cnt[name]

    def wait_at(self, eng, name, val):
        if val > 0:
            eng.wait_ge(self.sems[name], val)

    def barrier(self, engines):
        for eng in engines:
            for name in self.sems:
                self.wait_at(eng, name, self.cnt[name])


def _consts():
    return dict(
        onesr=np.ones((1, 128), np.float32),
        trilc=np.triu(np.ones((128, 128), np.float32)),
        eyeb=np.eye(128, dtype=BF),
        eyef=np.eye(128, dtype=np.float32),
        iota=np.arange(128, dtype=np.float32).reshape(128, 1),
        triu1=np.triu(np.ones((128, 128), np.float32), 1),
    )


class NS:
    pass


def emit_A(nc, p, C, ppool, t):
    TT = 512
    NTILE = T_FULL // TT
    NCH = 4
    HALO = 3
    NCK = TT // 128

    xb_r = t.xb.rearrange("(c p) t -> p c t", p=128)
    yD_r = t.yD.rearrange("(j c p) t -> p (j c) t", p=128, c=4)

    with ExitStack() as st:
        sb = lambda name, shape, dt=F32: st.enter_context(nc.sbuf_tensor("A" + name, shape, dt))
        sWzx = sb("sWzx", [128, 8 * 1024], BF16)
        sWbc = sb("sWbc", [128, 8 * 256], BF16)
        sWdt = sb("sWdt", [128, 8 * 8], BF16)
        sCw = sb("sCw", [128, NCH * K_CONV])
        sCb = sb("sCb", [128, NCH])
        sAlog = sb("sAlog", [8, 1])
        snegA = sb("snegA", [8, 1])
        sDcol = sb("sDcol", [128, 4])
        sDtb = sb("sDtb", [8, 1])
        sNw = sb("sNw", [128, 8])
        sEpsR = sb("sEpsR", [1, 1])
        sH = sb("sH", [128, 512])
        sHbf = sb("sHbf", [128, 512], BF16)
        sClCar = sb("sClCar", [8, 1])
        sXraw = sb("sXraw", [128, 8 * TT])
        sSq = sb("sSq", [128, 8 * TT])
        sRrms = sb("sRrms", [1, TT])
        sRrms2 = sb("sRrms2", [1, TT])
        sTmpR3 = sb("sTmpR3", [8, TT])
        sXn = sb("sXn", [128, 8 * TT], BF16)
        sZ = sb("sZ", [128, NCH * TT], BF16)
        sSiluZ = sb("sSiluZ", [128, NCH * TT], BF16)
        sXs = sb("sXs", [128, NCH * (TT + HALO)], BF16)
        sCacc0 = sb("sCacc0", [128, TT], BF16)
        sCacc1 = sb("sCacc1", [128, TT], BF16)
        sXh = sb("sXh", [128, NCH * TT], BF16)
        sB = sb("sB", [128, TT], BF16)
        sC = sb("sC", [128, TT], BF16)
        sDtr = sb("sDtr", [8, TT])
        sTmpR = sb("sTmpR", [8, TT])
        sTmpR2 = sb("sTmpR2", [8, TT])
        sClrel = sb("sClrel", [8, TT])
        sWs = sb("sWs", [8, TT])
        sDtsp = sb("sDtsp", [8, TT])
        sCl = sb("sCl", [8, TT])
        sDAc = sb("sDAc", [8, 1])
        sSelc = sb("sSelc", [8, 1024])
        sColT_all = sb("sColT_all", [128, NCK * 24])
        sTrilG_all = sb("sTrilG_all", [128, NCK * 128], BF16)
        sBsm_all = sb("sBsm_all", [128, NCK * 128], BF16)
        sXhT_all = sb("sXhT_all", [128, NCK * 512], BF16)
        sEmA_all = sb("sEmA_all", [128, NCK * 1024], BF16)
        sCdbA_all = sb("sCdbA_all", [128, NCK * 1024], BF16)
        sDAf_all = sb("sDAf_all", [128, NCK * 8])
        sMaskA_all = sb("sMaskA_all", [128, NCK * 1024], BF16)
        sCpA_all = sb("sCpA_all", [128, NCK * 1024], BF16)
        sUA_all = sb("sUA_all", [128, NCK * 512], BF16)
        sUpA_all = sb("sUpA_all", [128, NCK * 512], BF16)
        sDxh = sb("sDxh", [128, NCH * TT], BF16)
        sHtmp = sb("sHtmp", [128, 512])
        sY = sb("sY", [128, NCH * TT], BF16)
        sYevT = sb("sYevT", [128, 512])

        def bank(i, w=512, off=0):
            return ppool[:, i * 512 + off: i * 512 + off + w]

        def bankp(i, p0, prows, w, off=0):
            return ppool[p0:p0 + prows, i * 512 + off: i * 512 + off + w]

        sy, gp, pe, dv, ac = nc.sync, nc.gpsimd, nc.tensor, nc.vector, nc.scalar

        p.inc(sy.dma_start(out=sCw[:, :].rearrange("p (c k) -> p c k", c=NCH),
                           in_=t.cw.rearrange("(c p) k -> p c k", p=128)), "ld", 16)
        p.inc(sy.dma_start(out=sCb[:, :], in_=t.cb[:, :]), "ld", 16)
        p.inc(sy.dma_start(out=sAlog[:, :], in_=t.alog[:, :]), "ld", 16)
        p.inc(sy.dma_start(out=sDcol[:, :], in_=t.dcol[:, :]), "ld", 16)
        p.inc(sy.dma_start(out=sDtb[:, :], in_=t.dtb[:, :]), "ld", 16)
        p.inc(sy.dma_start(out=sNw[:, :], in_=t.nw[:, :]), "ld", 16)
        p.inc(sy.dma_start(out=sSelc[:, :], in_=t.selc[:, :]), "ld", 16)
        p.inc(gp.dma_start(out=sWzx[:, :].rearrange("p (c m) -> p c m", c=8),
                           in_=t.wzx.rearrange("(c p) m -> p c m", p=128)), "gld", 16)
        p.inc(gp.dma_start(out=sWbc[:, :].rearrange("p (c m) -> p c m", c=8),
                           in_=t.wbc.rearrange("(c p) m -> p c m", p=128)), "gld", 16)
        p.inc(gp.dma_start(out=sWdt[:, :].rearrange("p (c m) -> p c m", c=8),
                           in_=t.wdt.rearrange("(c p) m -> p c m", p=128)), "gld", 16)

        p.wait(dv, "ld", "gld")
        p.dvx(dv, dv.memset(sH[:, :], 0.0))
        p.dvx(dv, dv.memset(sHbf[:, :], 0.0))
        p.dvx(dv, dv.memset(sClCar[:, :], 0.0))
        p.dvx(dv, dv.memset(sXs[:, :], 0.0))
        p.dvx(dv, dv.memset(sEpsR[:, :], MEPS))
        p.wait(ac, "dv")
        p.acx(ac, ac.activation(out=sDAc[:, :], in_=sAlog[:, :], func=AF.Exp))
        p.wait(dv, "ac")
        p.dvx(dv, dv.tensor_scalar(snegA[:, :], sDAc[:, :], -1.0, None, OP.mult))

        sqmark = 0
        xnmark = 0
        for tt in range(NTILE):
            tsl = slice(tt * TT, (tt + 1) * TT)
            if tt == 0:
                p.wait(sy, "pe", "ac", "dv")
            else:
                p.wait_at(sy, "ac", sqmark)
                p.wait_at(sy, "dv", xnmark)
            p.inc(sy.dma_start(out=sXraw[:, :].rearrange("p (c t) -> p c t", c=8),
                               in_=xb_r[:, :, tsl]), "ld", 16)
            p.wait(dv, "st")
            if tt > 0:
                for m in range(NCH):
                    base = m * (TT + HALO)
                    p.dvx(dv, dv.tensor_copy(sXs[:, base: base + HALO],
                                         sXs[:, base + TT: base + TT + HALO]))
            p.wait(ac, "ld", "pe")
            p.acx(ac, ac.activation(out=sSq[:, :], in_=sXraw[:, :], func=AF.Square))
            sqmark = p.at("ac")
            p.wait(pe, "ac")
            for c in range(8):
                p.inc(pe.matmul(bankp(0, 0, 1, TT), C.sOnesCol[:, :],
                                sSq[:, c * TT:(c + 1) * TT],
                                start=(c == 0), stop=(c == 7)), "pe")
            p.wait(ac, "pe")
            p.acx(ac, ac.activation(out=sRrms2[:, :], in_=bankp(0, 0, 1, TT), func=AF.Ln,
                                scale=1.0 / D_MODEL, bias=sEpsR[:, :]))
            p.acx(ac, ac.activation(out=sRrms[:, :], in_=sRrms2[:, :], func=AF.Exp,
                                scale=-0.5))
            p.wait(pe, "ac")
            p.inc(pe.matmul(bank(1, TT), C.sOnes[:, :], sRrms[:, :], start=True, stop=True), "pe")
            p.wait(dv, "pe", "ld")
            for c in range(8):
                p.dvx(dv, dv.scalar_tensor_tensor(sXn[:, c * TT:(c + 1) * TT],
                                              sXraw[:, c * TT:(c + 1) * TT],
                                              sNw[:, c:c + 1], bank(1, TT),
                                              OP.mult, OP.mult))
            xnmark = p.at("dv")
            p.wait(pe, "dv", "ac")
            for m in range(4):
                for k in range(8):
                    p.inc(pe.matmul(bank(m), sWzx[:, k * 1024 + m * 128: k * 1024 + (m + 1) * 128],
                                    sXn[:, k * TT:(k + 1) * TT],
                                    start=(k == 0), stop=(k == 7)), "pe")
            for m in range(4):
                for k in range(8):
                    p.inc(pe.matmul(bank(4 + m),
                                    sWzx[:, k * 1024 + 512 + m * 128: k * 1024 + 512 + (m + 1) * 128],
                                    sXn[:, k * TT:(k + 1) * TT],
                                    start=(k == 0), stop=(k == 7)), "pe")
            p.wait(dv, "pe")
            for m in range(4):
                p.dvx(dv, dv.tensor_copy(sZ[:, m * TT:(m + 1) * TT], bank(m)))
            for m in range(4):
                p.dvx(dv, dv.tensor_copy(sXs[:, m * (TT + HALO) + HALO: (m + 1) * (TT + HALO)],
                                     bank(4 + m)))
            p.wait(ac, "dv")
            for m in range(4):
                p.acx(ac, ac.activation(out=sSiluZ[:, m * TT:(m + 1) * TT],
                                    in_=sZ[:, m * TT:(m + 1) * TT], func=AF.Silu))
            p.wait(pe, "dv")
            for k in range(8):
                p.inc(pe.matmul(bank(0), sWbc[:, k * 256: k * 256 + 128],
                                sXn[:, k * TT:(k + 1) * TT], start=(k == 0), stop=(k == 7)), "pe")
            for k in range(8):
                p.inc(pe.matmul(bank(1), sWbc[:, k * 256 + 128: (k + 1) * 256],
                                sXn[:, k * TT:(k + 1) * TT], start=(k == 0), stop=(k == 7)), "pe")
            for k in range(8):
                p.inc(pe.matmul(bankp(2, 0, 8, TT), sWdt[:, k * 8:(k + 1) * 8],
                                sXn[:, k * TT:(k + 1) * TT], start=(k == 0), stop=(k == 7)), "pe")
            p.wait(dv, "pe")
            p.dvx(dv, dv.tensor_copy(sB[:, :], bank(0)))
            p.dvx(dv, dv.tensor_copy(sC[:, :], bank(1)))
            p.dvx(dv, dv.tensor_copy(sDtr[:, :], bankp(2, 0, 8, TT)))

            for m in range(NCH):
                base = m * (TT + HALO)
                p.wait(dv, "ac")
                p.dvx(dv, dv.tensor_scalar(sCacc0[:, :], sXs[:, base: base + TT],
                                       sCw[:, m * K_CONV: m * K_CONV + 1],
                                       sCb[:, m:m + 1], OP.mult, OP.add))
                accs = [sCacc0, sCacc1]
                for k in range(1, K_CONV):
                    src, dst = accs[(k - 1) % 2], accs[k % 2]
                    p.dvx(dv, dv.scalar_tensor_tensor(dst[:, :], sXs[:, base + k: base + k + TT],
                                                  sCw[:, m * K_CONV + k: m * K_CONV + k + 1],
                                                  src[:, :], OP.mult, OP.add))
                final = accs[(K_CONV - 1) % 2]
                p.wait(ac, "dv")
                p.acx(ac, ac.activation(out=sXh[:, m * TT:(m + 1) * TT], in_=final[:, :],
                                    func=AF.Silu))
            p.wait(dv, "ac")
            for m in range(NCH):
                p.dvx(dv, dv.tensor_scalar(sDxh[:, m * TT:(m + 1) * TT],
                                           sXh[:, m * TT:(m + 1) * TT],
                                           sDcol[:, m:m + 1], None, OP.mult))
            p.dvx(dv, dv.tensor_scalar(sTmpR[:, :], sDtr[:, :], sDtb[:, :], None, OP.add))
            p.wait(ac, "dv")
            p.acx(ac, ac.activation(out=sTmpR2[:, :], in_=sTmpR[:, :], func=AF.Abs))
            p.acx(ac, ac.activation(out=sTmpR3[:, :], in_=sTmpR2[:, :], func=AF.Exp, scale=-1.0))
            p.acx(ac, ac.activation(out=sTmpR2[:, :], in_=sTmpR3[:, :], func=AF.Ln, bias=1.0))
            p.wait(dv, "ac")
            p.dvx(dv, dv.tensor_scalar(sTmpR3[:, :], sTmpR[:, :], 0.0, None, OP.max))
            p.dvx(dv, dv.tensor_tensor(sDtsp[:, :], sTmpR3[:, :], sTmpR2[:, :], OP.add))
            p.dvx(dv, dv.tensor_scalar(sTmpR[:, :], sDtsp[:, :], snegA[:, :], None, OP.mult))
            p.dvx(dv, dv.tensor_tensor_scan(sCl[:, :], sTmpR[:, :], sTmpR[:, :], sClCar[:, :],
                                        OP.add, OP.bypass))
            for c in range(NCK):
                prevc = sClCar[:, :] if c == 0 else sCl[:, c * 128 - 1: c * 128]
                csl = slice(c * 128, (c + 1) * 128)
                p.dvx(dv, dv.tensor_scalar(sClrel[:, csl], sCl[:, csl], prevc, None,
                                       OP.subtract))
                endc = sClrel[:, c * 128 + 127: c * 128 + 128]
                p.dvx(dv, dv.tensor_scalar(sTmpR2[:, csl], sClrel[:, csl], endc, None,
                                       OP.subtract))
            p.dvx(dv, dv.tensor_copy(sClCar[:, :], sCl[:, TT - 1: TT]))
            p.wait(ac, "dv")
            p.acx(ac, ac.activation(out=sTmpR3[:, :], in_=sTmpR2[:, :], func=AF.Exp, scale=-1.0))
            p.wait(dv, "ac")
            p.dvx(dv, dv.tensor_tensor(sWs[:, :], sTmpR3[:, :], sDtsp[:, :], OP.mult))

            clbs = [ppool[:, 5 * 512: 5 * 512 + 1024], ppool[:, 1 * 512: 1 * 512 + 1024]]
            rbank = [3, 4]
            pemark = [0] * NCK
            evdv = [0] * NCK
            evac = [0] * NCK
            for c in range(NCK):
                csl = slice(c * 128, (c + 1) * 128)
                s = c % 2
                R = rbank[s]
                clb = clbs[s]
                if c == 0:
                    p.wait(pe, "dv", "ac")
                elif c >= 2:
                    p.wait_at(pe, "dv", evdv[c - 2])
                    p.wait_at(pe, "ac", evac[c - 2])
                p.inc(pe.transpose(bankp(R, 0, 128, 8, off=0), sClrel[:, csl], C.sEyeF[0:8, 0:8]), "pe")
                p.inc(pe.transpose(bankp(R, 0, 128, 8, off=8), sWs[:, csl], C.sEyeF[0:8, 0:8]), "pe")
                p.inc(pe.transpose(bankp(R, 0, 128, 8, off=16), sDtsp[:, csl], C.sEyeF[0:8, 0:8]), "pe")
                p.inc(pe.matmul(bankp(R, 0, 128, 128, off=24), sB[:, csl], sC[:, csl],
                                start=True, stop=True), "pe")
                p.inc(pe.transpose(bankp(R, 0, 128, 64, off=152).bitcast(BF16)[:, 0:128],
                                   sB[:, csl], C.sEyeB[:, :]), "pe")
                for m in range(NCH):
                    p.inc(pe.transpose(
                        bankp(R, 0, 128, 64, off=216 + m * 64).bitcast(BF16)[:, 0:128],
                        sXh[:, m * TT + c * 128: m * TT + (c + 1) * 128], C.sEyeB[:, :]), "pe")
                for h in range(8):
                    p.inc(pe.matmul(clb[:, h * 128:(h + 1) * 128],
                                    sSelc[:, h * 128:(h + 1) * 128], sClrel[:, csl],
                                    start=True, stop=True), "pe")
                pemark[c] = p.at("pe")
                p.wait_at(dv, "pe", pemark[c])
                p.dvx(dv, dv.tensor_copy(sColT_all[:, c * 24:(c + 1) * 24], bankp(R, 0, 128, 24, off=0)))
                p.dvx(dv, dv.tensor_tensor(sTrilG_all[:, c * 128:(c + 1) * 128],
                                           bankp(R, 0, 128, 128, off=24), C.sTril[:, :], OP.mult))
                p.dvx(dv, dv.tensor_copy(sBsm_all[:, c * 128:(c + 1) * 128],
                                         bankp(R, 0, 128, 64, off=152).bitcast(BF16)[:, 0:128]))
                p.dvx(dv, dv.tensor_copy(sXhT_all[:, c * 512:(c + 1) * 512],
                                         bankp(R, 0, 128, 256, off=216).bitcast(BF16)[:, 0:512]))
                clcol_exp = sColT_all[:, c * 24: c * 24 + 8].unsqueeze(2).broadcast_to([128, 8, 128])
                miav = sSq[:, c * 1024:(c + 1) * 1024]
                p.dvx(dv, dv.tensor_tensor(miav.rearrange("p (h t) -> p h t", h=8),
                                           clb.rearrange("p (h t) -> p h t", h=8),
                                           clcol_exp, OP.subtract))
                p.dvx(dv, dv.tensor_scalar(miav, miav, 0.0, None, OP.min))
                evdv[c] = p.at("dv")
                p.wait_at(ac, "pe", pemark[c])
                p.acx(ac, ac.activation(out=sCdbA_all[:, c * 1024:(c + 1) * 1024], in_=clb,
                                        func=AF.Exp))
                p.acx(ac, ac.activation(out=sDAf_all[:, c * 8:(c + 1) * 8],
                                        in_=clb[:, 127:1024:128], func=AF.Exp))
                evac[c] = p.at("ac")

            p.wait(ac, "dv")
            p.acx(ac, ac.activation(out=sEmA_all[:, :], in_=sSq[:, 0:NCK * 1024], func=AF.Exp))
            p.wait(gp, "ac", "pe")
            for c in range(NCK):
                cs1 = slice(c * 1024, (c + 1) * 1024)
                trilg_exp = sTrilG_all[:, c * 128:(c + 1) * 128].unsqueeze(1).broadcast_to([128, 8, 128])
                p.gpx(gp, gp.tensor_tensor(sMaskA_all[:, cs1].rearrange("p (h t) -> p h t", h=8),
                                           sEmA_all[:, cs1].rearrange("p (h t) -> p h t", h=8),
                                           trilg_exp, OP.mult))
                c_exp = sC[:, c * 128:(c + 1) * 128].unsqueeze(1).broadcast_to([128, 8, 128])
                p.gpx(gp, gp.tensor_tensor(sCpA_all[:, cs1].rearrange("p (h t) -> p h t", h=8),
                                           sCdbA_all[:, cs1].rearrange("p (h t) -> p h t", h=8),
                                           c_exp, OP.mult))
            for c in range(NCK):
                cs5 = slice(c * 512, (c + 1) * 512)
                dt_exp = sColT_all[:, c * 24 + 16: c * 24 + 24].unsqueeze(2).broadcast_to([128, 8, 64])
                p.dvx(dv, dv.tensor_tensor(sUA_all[:, cs5].rearrange("p (h q) -> p h q", h=8),
                                           sXhT_all[:, cs5].rearrange("p (h q) -> p h q", h=8),
                                           dt_exp, OP.mult))
                ws_exp = sColT_all[:, c * 24 + 8: c * 24 + 16].unsqueeze(2).broadcast_to([128, 8, 64])
                p.dvx(dv, dv.tensor_tensor(sUpA_all[:, cs5].rearrange("p (h q) -> p h q", h=8),
                                           sXhT_all[:, cs5].rearrange("p (h q) -> p h q", h=8),
                                           ws_exp, OP.mult))
            p2gp = p.at("gp")
            p2dv = p.at("dv")
            p2ac = p.at("ac")

            st_mark = 0
            yev_mark = [0, 0]
            for c in range(NCK):
                csl = slice(c * 128, (c + 1) * 128)
                par3 = c % 2
                yb_ev = 7 if par3 == 0 else 5
                yb_od = 2 if par3 == 0 else 6
                hlb = 0 if par3 == 0 else 4
                if c == 0:
                    p.wait_at(pe, "gp", p2gp)
                    p.wait_at(pe, "dv", p2dv)
                else:
                    p.wait_at(pe, "dv", st_mark)
                    if c >= 2:
                        p.wait_at(pe, "dv", yev_mark[par3])
                for h in range(8):
                    ybank = yb_ev if h % 2 == 0 else yb_od
                    rows = slice((h % 2) * 64, (h % 2) * 64 + 64)
                    yp = ppool[rows, ybank * 512 + (h // 2) * 128: ybank * 512 + (h // 2) * 128 + 128]
                    first = h in (0, 1)
                    last = h in (6, 7)
                    p.inc(pe.matmul(yp, sUA_all[:, c * 512 + h * 64: c * 512 + (h + 1) * 64],
                                    sMaskA_all[:, c * 1024 + h * 128: c * 1024 + (h + 1) * 128],
                                    start=first, stop=False), "pe")
                    p.inc(pe.matmul(yp, sHbf[:, h * 64:(h + 1) * 64],
                                    sCpA_all[:, c * 1024 + h * 128: c * 1024 + (h + 1) * 128],
                                    start=False, stop=last), "pe")
                    p.inc(pe.matmul(bank(hlb)[:, h * 64:(h + 1) * 64],
                                    sBsm_all[:, c * 128:(c + 1) * 128],
                                    sUpA_all[:, c * 512 + h * 64: c * 512 + (h + 1) * 64],
                                    start=(h == 0), stop=(h == 7)), "pe")
                if c == 0:
                    p.wait_at(dv, "ac", p2ac)
                p.wait(dv, "pe")
                dA_view = sDAf_all[:, c * 8:(c + 1) * 8].unsqueeze(2).broadcast_to([128, 8, 64])
                p.dvx(dv, dv.tensor_tensor(sHtmp[:, :].rearrange("p (h q) -> p h q", h=8),
                                           sH[:, :].rearrange("p (h q) -> p h q", h=8),
                                           dA_view, OP.mult))
                p.dvx(dv, dv.tensor_tensor(sH[:, :], sHtmp[:, :], bank(hlb), OP.add))
                p.dvx(dv, dv.tensor_copy(sHbf[:, :], sH[:, :]))
                st_mark = p.at("dv")
                for par, ybank in ((0, yb_ev), (1, yb_od)):
                    rows = slice(par * 64, par * 64 + 64)
                    ypsv = ppool[rows, ybank * 512: ybank * 512 + 512]
                    dxh_v = sDxh[:, :].rearrange("p (m t) -> p m t", m=NCH)[rows, :, csl]
                    silu_v = sSiluZ[:, :].rearrange("p (m t) -> p m t", m=NCH)[rows, :, csl]
                    y_v = sY[:, :].rearrange("p (m t) -> p m t", m=NCH)[rows, :, csl]
                    p.dvx(dv, dv.tensor_tensor(sYevT[rows, :].rearrange("p (m t) -> p m t", m=4),
                                               ypsv.rearrange("p (m t) -> p m t", m=4),
                                               dxh_v, OP.add))
                    p.dvx(dv, dv.tensor_tensor(y_v, sYevT[rows, :].rearrange("p (m t) -> p m t", m=4),
                                               silu_v, OP.mult))
                yev_mark[par3] = p.at("dv")

            j, s = tt // 4, tt % 4
            p.wait(sy, "dv")
            p.inc(sy.dma_start(out=yD_r[:, 4 * j: 4 * j + 4, 512 * s: 512 * s + 512],
                               in_=sY[:, :].rearrange("p (m t) -> p m t", m=NCH)), "st", 16)

        p.barrier([sy, gp, pe, dv, ac])


def emit_B(nc, p, C, ppool, t, ntile=None, do_scores=True, do_tm=True):
    TT = 512
    NTILE = ntile or (TPC // TT)

    xo_r = t.xo.rearrange("(c p) t -> p c t", p=128)
    x1f_r = t.x1fD.rearrange("(c p) t -> p c t", p=128)
    tm_r = t.tmD.rearrange("(t p) m -> p t m", p=128)

    with ExitStack() as st:
        sb = lambda name, shape, dt=F32: st.enter_context(nc.sbuf_tensor("B" + name, shape, dt))
        sWout = sb("sWout", [128, 16 * 1024], BF16)
        sWg = sb("sWg", [128, 8 * 8])
        sXo = sb("sXo", [128, 8 * TT])
        sRaw = sb("sRaw", [128, 8 * TT])
        sSq = sb("sSq", [128, TT])
        sR1 = sb("sR1", [1, TT])
        sR2 = sb("sR2", [1, TT])
        sR3 = sb("sR3", [1, TT])
        sX1 = sb("sX1", [128, 8 * TT])
        sX1b = sb("sX1b", [128, 8 * TT], BF16)
        sTm = sb("sTm", [128, 4 * 1024], BF16)
        sEpsR = sb("sEpsR", [1, 1])
        sMx4s = sb("sMx4s", [128, 4])
        sSc4 = sb("sSc4", [128, 32])
        sScE4 = sb("sScE4", [128, 32])
        sScT = sb("sScT", [8, TT])
        sRec4 = sb("sRec4", [128, 4])
        sGoffB = sb("sGoffB", [128, 16], U32)
        sYgF = sb("sYgF", [128, 16 * TPC], BF16)

        def bank(i, w=512, off=0):
            return ppool[:, i * 512 + off: i * 512 + off + w]

        def bankp(i, p0, prows, w, off=0):
            return ppool[p0:p0 + prows, i * 512 + off: i * 512 + off + w]

        sy, gp, pe, dv, ac = nc.sync, nc.gpsimd, nc.tensor, nc.vector, nc.scalar
        bcyg = gp.to_reg(8191)

        p.inc(sy.dma_start(out=sWg[:, :].rearrange("p (c m) -> p c m", c=8),
                           in_=t.wg.rearrange("(c p) m -> p c m", p=128)), "ld", 16)
        p.inc(sy.dma_start(out=sGoffB[:, :], in_=t.goffB[:, :]), "ld", 16)
        p.inc(gp.dma_start(out=sWout[:, :].rearrange("p (c m) -> p c m", c=16),
                           in_=t.woutT.rearrange("(c p) m -> p c m", p=128)), "gld", 16)
        p.wait(dv, "ld")
        p.dvx(dv, dv.memset(sEpsR[:, :], LNEPS))
        p.wait(gp, "cgy", "ld")
        for c in range(16):
            p.inc(gp.indirect_dma_start(
                out=sYgF[:, c * TPC:(c + 1) * TPC], out_offset=None,
                in_=t.ygAll[:, :],
                in_offset=bass.IndirectOffsetOnAxis(ap=sGoffB[:, c:c + 1], axis=0),
                bounds_check=bcyg, oob_is_err=False), "gld", 16)

        for tt in range(NTILE):
            tsl = slice(tt * TT, (tt + 1) * TT)
            p.wait(sy, "pe", "dv", "ac")
            p.inc(sy.dma_start(out=sXo[:, :].rearrange("p (c t) -> p c t", c=8),
                               in_=xo_r[:, :, tsl]), "ld", 16)
            p.wait(pe, "dv", "ac", "gld", "ld")
            bomk = [0] * 8
            for m in range(8):
                if m >= 4:
                    p.wait_at(pe, "dv", bomk[m - 4])
                for k in range(16):
                    p.inc(pe.matmul(bank(m % 4), sWout[:, k * 1024 + m * 128: k * 1024 + (m + 1) * 128],
                                    sYgF[:, k * TPC + tt * TT: k * TPC + (tt + 1) * TT],
                                    start=(k == 0), stop=(k == 15)), "pe")
                p.wait(dv, "pe")
                p.dvx(dv, dv.scalar_tensor_tensor(sRaw[:, m * TT:(m + 1) * TT],
                                                  sXo[:, m * TT:(m + 1) * TT], 2.0,
                                                  bank(m % 4), OP.mult, OP.add))
                bomk[m] = p.at("dv")
            p.wait(pe, "dv")
            for c in range(8):
                p.inc(pe.matmul(bankp(0, 0, 1, TT), C.sOnesCol[:, :],
                                sRaw[:, c * TT:(c + 1) * TT],
                                start=(c == 0), stop=(c == 7)), "pe")
            p.wait(ac, "dv", "st")
            p.acx(ac, ac.activation(out=sX1[:, :], in_=sRaw[:, :], func=AF.Square))
            p.wait(pe, "ac")
            for c in range(8):
                p.inc(pe.matmul(bankp(1, 0, 1, TT), C.sOnesCol[:, :],
                                sX1[:, c * TT:(c + 1) * TT],
                                start=(c == 0), stop=(c == 7)), "pe")
            p.wait(dv, "pe")
            p.dvx(dv, dv.tensor_scalar(sR1[:, :], bankp(0, 0, 1, TT), 1.0 / D_MODEL,
                                       None, OP.mult))
            p.dvx(dv, dv.tensor_scalar(sR2[:, :], bankp(1, 0, 1, TT), 1.0 / D_MODEL,
                                       None, OP.mult))
            p.dvx(dv, dv.tensor_tensor(sR3[:, :], sR1[:, :], sR1[:, :], OP.mult))
            p.dvx(dv, dv.tensor_tensor(sR2[:, :], sR2[:, :], sR3[:, :], OP.subtract))
            p.wait(ac, "dv")
            p.acx(ac, ac.activation(out=sR3[:, :], in_=sR2[:, :], func=AF.Ln, bias=sEpsR[:, :]))
            p.acx(ac, ac.activation(out=sR2[:, :], in_=sR3[:, :], func=AF.Exp, scale=-0.5))
            p.wait(dv, "ac")
            p.dvx(dv, dv.scalar_tensor_tensor(sR3[:, :], sR1[:, :], -1.0, sR2[:, :],
                                              OP.mult, OP.mult))
            p.wait(pe, "dv")
            p.inc(pe.matmul(bank(2, TT), C.sOnes[:, :], sR2[:, :], start=True, stop=True), "pe")
            p.inc(pe.matmul(bank(3, TT), C.sOnes[:, :], sR3[:, :], start=True, stop=True), "pe")
            p.wait(dv, "pe", "st")
            for c in range(8):
                csl = slice(c * TT, (c + 1) * TT)
                p.dvx(dv, dv.tensor_tensor(sX1[:, csl], sRaw[:, csl], bank(2, TT), OP.mult))
                p.dvx(dv, dv.tensor_tensor(sX1[:, csl], sX1[:, csl], bank(3, TT), OP.add))
                p.dvx(dv, dv.tensor_copy(sX1b[:, csl], sX1[:, csl]))
            p.wait(sy, "dv")
            p.inc(sy.dma_start(out=x1f_r[:, :, tsl],
                               in_=sX1[:, :].rearrange("p (c t) -> p c t", c=8)), "st", 16)
            if do_scores:
                p.wait(pe, "dv", "ac")
                for tg in range(TT // 128):
                    for k in range(8):
                        p.inc(pe.matmul(bankp(4, 0, 128, 8, off=tg * 8),
                                        sX1[:, k * TT + tg * 128: k * TT + (tg + 1) * 128],
                                        sWg[:, k * 8:(k + 1) * 8],
                                        start=(k == 0), stop=(k == 7)), "pe")
                p.wait(dv, "pe")
                lg = bankp(4, 0, 128, 32)
                p.dvx(dv, dv.tensor_reduce(sMx4s[:, :], lg.rearrange("p (g e) -> p g e", g=4),
                                           axis=mybir.AxisListType.X, op=OP.max))
                mx_e = sMx4s[:, :].unsqueeze(2).broadcast_to([128, 4, 8])
                p.dvx(dv, dv.tensor_tensor(sScE4[:, :].rearrange("p (g e) -> p g e", g=4),
                                           lg.rearrange("p (g e) -> p g e", g=4),
                                           mx_e, OP.subtract))
                p.wait(ac, "dv")
                p.acx(ac, ac.activation(out=sSc4[:, :], in_=sScE4[:, :], func=AF.Exp))
                p.wait(dv, "ac")
                p.dvx(dv, dv.tensor_reduce(sRec4[:, :],
                                           sSc4[:, :].rearrange("p (g e) -> p g e", g=4),
                                           axis=mybir.AxisListType.X, op=OP.add))
                p.dvx(dv, dv.reciprocal(sRec4[:, :], sRec4[:, :]))
                rc_e = sRec4[:, :].unsqueeze(2).broadcast_to([128, 4, 8])
                p.dvx(dv, dv.tensor_tensor(sScE4[:, :].rearrange("p (g e) -> p g e", g=4),
                                           sSc4[:, :].rearrange("p (g e) -> p g e", g=4),
                                           rc_e, OP.mult))
                p.wait(pe, "dv")
                for tg in range(TT // 128):
                    p.inc(pe.transpose(bankp(5, 0, 8, 128, off=tg * 128),
                                       sScE4[:, tg * 8:(tg + 1) * 8], C.sEyeF[:, :]), "pe")
                p.wait(dv, "pe", "st")
                p.dvx(dv, dv.tensor_copy(sScT[0:8, :], bankp(5, 0, 8, 512)))
                p.wait(sy, "dv")
                p.inc(sy.dma_start(out=t.sctD[:, tsl], in_=sScT[0:8, :]), "st", 16)
            for s in range(4 if do_tm else 0):
                p.wait(pe, "dv")
                for c in range(8):
                    p.inc(pe.transpose(
                        bankp(6, 0, 128, 64, off=c * 64).bitcast(BF16)[:, 0:128],
                        sX1b[:, c * TT + s * 128: c * TT + (s + 1) * 128],
                        C.sEyeB[:, :]), "pe")
                p.wait(dv, "pe")
                p.dvx(dv, dv.tensor_copy(sTm[:, s * 1024:(s + 1) * 1024],
                                         bankp(6, 0, 128, 512).bitcast(BF16)[:, 0:1024]))
            if do_tm:
                p.wait(sy, "dv")
                p.inc(sy.dma_start(out=tm_r[:, 4 * tt: 4 * tt + 4, :],
                                   in_=sTm[:, :].rearrange("p (s m) -> p s m", s=4)), "st", 16)

        p.barrier([sy, gp, pe, dv, ac])


def emit_C(nc, p, C, ppool, t):
    NITER = 42
    CAP = 4096.0
    BIG = 60000.0

    with ExitStack() as st:
        sb = lambda name, shape, dt=F32: st.enter_context(nc.sbuf_tensor("C" + name, shape, dt))
        sW1 = sb("sW1", [128, 8 * 2048], BF16)
        sW3 = sb("sW3", [128, 8 * 2048], BF16)
        sW2 = sb("sW2", [128, 16 * 1024], BF16)
        sScR = sb("sScR", [8, 2048])
        sGrid = sb("sGrid", [128, 128])
        sLo = sb("sLo", [128, 1]); sHi = sb("sHi", [128, 1]); sMid = sb("sMid", [128, 1])
        sCnt = sb("sCnt", [128, 1]); sSel = sb("sSel", [128, 1])
        sT1 = sb("sT1", [128, 1]); sT2 = sb("sT2", [128, 1])
        sR11 = sb("sR11", [1, 1])
        sTrash = sb("sTrash", [128, 128], BF16)
        sEq = sb("sEq", [128, 128]); sRk = sb("sRk", [128, 128]); sRk2 = sb("sRk2", [128, 128])
        sMask = sb("sMask", [128, 128]); sGate = sb("sGate", [128, 128])
        sRank = sb("sRank", [128, 128]); sIdxF = sb("sIdxF", [128, 128])
        sIdxU = sb("sIdxU", [128, 128], U32)
        sStg = [sb(f"sStg{i}", [128, 1032], BF16) for i in range(4)]
        sMetaAll = sb("sMetaAll", [128, 3 * 128])
        sIotaRow = sb("sIotaRow", [1, 128])
        sXr4 = sb("sXr4", [128, 4 * 1032], BF16)
        sScf = sb("sScf", [128, 4])
        sTokF = sb("sTokF", [128, 4])
        sTokU = sb("sTokU", [128, 4], U32)
        sRt = sb("sRt", [128, 8 * 512], BF16)
        sGs = sb("sGs", [128, 512], BF16)
        sGs2 = sb("sGs2", [128, 512], BF16)
        sH1 = sb("sH1", [128, 16 * 512], BF16)
        sEoF = sb("sEoF", [128, 8 * 512], BF16)
        sEoT = sb("sEoT", [128, 4 * 1024], BF16)

        def bank(i, w=512, off=0):
            return ppool[:, i * 512 + off: i * 512 + off + w]

        def bankp(i, p0, prows, w, off=0):
            return ppool[p0:p0 + prows, i * 512 + off: i * 512 + off + w]

        sy, gp, pe, dv, ac = nc.sync, nc.gpsimd, nc.tensor, nc.vector, nc.scalar
        bcreg = gp.to_reg(4095)
        bctok = gp.to_reg(16383)

        p.inc(gp.dma_start(out=sW1[:, :].rearrange("p (c m) -> p c m", c=8),
                           in_=t.w1.rearrange("(c p) m -> p c m", p=128)), "gld", 16)
        p.inc(gp.dma_start(out=sW3[:, :].rearrange("p (c m) -> p c m", c=8),
                           in_=t.w3.rearrange("(c p) m -> p c m", p=128)), "gld", 16)
        p.inc(gp.dma_start(out=sW2[:, :].rearrange("p (c m) -> p c m", c=16),
                           in_=t.w2.rearrange("(c p) m -> p c m", p=128)), "gld", 16)

        p.wait(gp, "csc")
        p.inc(gp.indirect_dma_start(
            out=sScR[:, :], out_offset=None,
            in_=t.scAll[:, :],
            in_offset=bass.IndirectOffsetOnAxis(ap=C.sSoff[:, :], axis=0),
            bounds_check=gp.to_reg(63), oob_is_err=False), "sc", 16)
        p.wait(pe, "sc", "dv", "ac")
        for f in range(16):
            p.inc(pe.transpose(bankp(0, 0, 128, 8, off=f * 8),
                               sScR[0:8, f * 128:(f + 1) * 128], C.sEyeF[0:8, 0:8]), "pe")
        p.wait(dv, "pe")
        for f in range(16):
            p.dvx(dv, dv.tensor_copy(
                sGrid[:, :].rearrange("p (r f) -> p f r", r=8)[:, f, :],
                bankp(0, 0, 128, 8, off=f * 8)))

        p.dvx(dv, dv.memset(sLo[:, :], 0.0))
        p.dvx(dv, dv.memset(sHi[:, :], 2.0))
        for b4 in range(4):
            p.dvx(dv, dv.memset(sStg[b4][:, 1027:1032], 0.0))

        for it in range(NITER):
            p.dvx(dv, dv.tensor_scalar(sMid[:, :], sLo[:, :], sHi[:, :], 0.5,
                                       OP.add, OP.mult))
            p.dvx(dv, dv.tensor_scalar(sTrash[:, :], sGrid[:, :], sMid[:, :], 0.0,
                                       OP.is_ge, OP.add, accum_out=sCnt[:, :]))
            p.wait(pe, "dv")
            p.inc(pe.matmul(bankp(1, 0, 128, 1), C.sOnesG[:, :], sCnt[:, :],
                            start=True, stop=True), "pe")
            p.wait(dv, "pe")
            p.dvx(dv, dv.tensor_scalar(sSel[:, :], bankp(1, 0, 128, 1), CAP, None, OP.is_ge))
            p.dvx(dv, dv.tensor_tensor(sT1[:, :], sMid[:, :], sLo[:, :], OP.subtract))
            p.dvx(dv, dv.tensor_tensor(sT2[:, :], sHi[:, :], sMid[:, :], OP.subtract))
            p.dvx(dv, dv.tensor_tensor(sT1[:, :], sT1[:, :], sSel[:, :], OP.mult))
            p.dvx(dv, dv.tensor_tensor(sT2[:, :], sT2[:, :], sSel[:, :], OP.mult))
            p.dvx(dv, dv.tensor_tensor(sLo[:, :], sLo[:, :], sT1[:, :], OP.add))
            p.dvx(dv, dv.tensor_tensor(sHi[:, :], sMid[:, :], sT2[:, :], OP.add))
        p.dvx(dv, dv.tensor_scalar(sMask[:, :], sGrid[:, :], sLo[:, :], 0.0,
                                   OP.is_gt, OP.add, accum_out=sCnt[:, :]))
        p.wait(pe, "dv")
        p.inc(pe.matmul(bankp(0, 0, 1, 1), sCnt[:, :], C.sOnesCol[:, :],
                        start=True, stop=True), "pe")
        p.wait(dv, "pe")
        p.dvx(dv, dv.tensor_scalar(sR11[:, :], bankp(0, 0, 1, 1), -1.0, CAP,
                                   OP.mult, OP.add))
        p.wait(pe, "dv")
        p.inc(pe.matmul(bankp(1, 0, 128, 1), C.sOnes[:, :], sR11[:, :],
                        start=True, stop=True), "pe")
        p.wait(dv, "pe")
        p.dvx(dv, dv.tensor_copy(sT1[:, :], bankp(1, 0, 128, 1)))
        p.dvx(dv, dv.tensor_scalar(sEq[:, :], sGrid[:, :], sLo[:, :], 0.0,
                                   OP.is_equal, OP.add, accum_out=sCnt[:, :]))
        p.dvx(dv, dv.tensor_tensor_scan(sRk[:, :], sEq[:, :], sEq[:, :], 0.0,
                                        OP.add, OP.bypass))
        p.dvx(dv, dv.tensor_tensor(sRk[:, :], sRk[:, :], sEq[:, :], OP.subtract))
        p.wait(pe, "dv")
        p.inc(pe.matmul(bankp(2, 0, 128, 1), C.sTriu1[:, :], sCnt[:, :],
                        start=True, stop=True), "pe")
        p.wait(dv, "pe")
        p.dvx(dv, dv.tensor_scalar(sRk2[:, :], sRk[:, :], bankp(2, 0, 128, 1), None, OP.add))
        p.dvx(dv, dv.tensor_scalar(sRk2[:, :], sRk2[:, :], sT1[:, :], None, OP.is_lt))
        p.dvx(dv, dv.tensor_tensor(sRk2[:, :], sRk2[:, :], sEq[:, :], OP.mult))
        p.dvx(dv, dv.tensor_tensor(sMask[:, :], sMask[:, :], sRk2[:, :], OP.add))
        p.dvx(dv, dv.tensor_tensor(sGate[:, :], sMask[:, :], sGrid[:, :], OP.mult))
        p.dvx(dv, dv.tensor_scalar(sTrash[:, :], sMask[:, :], 1.0, 0.0,
                                   OP.mult, OP.add, accum_out=sCnt[:, :]))
        p.dvx(dv, dv.tensor_tensor_scan(sRk[:, :], sMask[:, :], sMask[:, :], 0.0,
                                        OP.add, OP.bypass))
        p.dvx(dv, dv.tensor_tensor(sRk[:, :], sRk[:, :], sMask[:, :], OP.subtract))
        p.wait(pe, "dv")
        p.inc(pe.matmul(bankp(2, 0, 128, 1), C.sTriu1[:, :], sCnt[:, :],
                        start=True, stop=True), "pe")
        p.wait(dv, "pe")
        p.dvx(dv, dv.tensor_scalar(sRank[:, :], sRk[:, :], bankp(2, 0, 128, 1), None, OP.add))
        p.dvx(dv, dv.tensor_tensor(sIdxF[:, :], sRank[:, :], sMask[:, :], OP.mult))
        p.dvx(dv, dv.tensor_scalar(sRk2[:, :], sMask[:, :], -BIG, BIG, OP.mult, OP.add))
        p.dvx(dv, dv.tensor_tensor(sIdxF[:, :], sIdxF[:, :], sRk2[:, :], OP.add))
        p.dvx(dv, dv.tensor_copy(sIdxU[:, :], sIdxF[:, :]))

        p.wait(sy, "dv")
        p.inc(sy.dma_start(out=sIotaRow[:, :], in_=t.rhiRow[:, :]), "ld", 16)
        p.wait(pe, "dv", "ld")
        p.inc(pe.matmul(bankp(3, 0, 128, 128, off=128), C.sOnes[:, :], sIotaRow[:, :],
                        start=True, stop=True), "pe")
        p.wait(dv, "pe")
        m3 = sMetaAll[:, :].rearrange("p (i k) -> p k i", k=3)
        rowb = bankp(3, 0, 128, 128, off=128)
        p.dvx(dv, dv.tensor_copy(m3[:, 0:1, :], sGate[:, :].unsqueeze(1)))
        p.dvx(dv, dv.tensor_copy(m3[:, 1:2, :], rowb.unsqueeze(1)))
        p.dvx(dv, dv.tensor_scalar(m3[:, 2:3, :], rowb.unsqueeze(1), 0.0,
                                   C.sIota[:, :], OP.mult, OP.add))

        stg_busy = [0, 0, 0, 0]
        p.wait(sy, "cx1")
        p.wait(ac, "cx1")
        for i in range(128):
            par = i % 4
            stg = sStg[par]
            q_eng, q_nm = (sy, "ld") if i % 2 == 0 else (ac, "zz")
            if i >= 4:
                p.wait_at(q_eng, "sc", stg_busy[par])
            p.inc(q_eng.dma_start(out=stg[:, 0:1024],
                                  in_=t.x1all[i * 128:(i + 1) * 128, :]), q_nm, 16)
            ldmark = p.at(q_nm)
            p.wait_at(dv, q_nm, ldmark)
            if i >= 4:
                p.wait_at(dv, "sc", stg_busy[par])
            p.dvx(dv, dv.tensor_copy(stg[:, 1024:1027], sMetaAll[:, 3 * i: 3 * i + 3]))
            p.wait(gp, "dv")
            p.inc(gp.indirect_dma_start(
                out=t.xbuf[:, :], out_offset=bass.IndirectOffsetOnAxis(ap=sIdxU[:, i:i + 1], axis=0),
                in_=stg[:, 0:1032], in_offset=None,
                bounds_check=bcreg, oob_is_err=False), "sc", 16)
            stg_busy[par] = p.at("sc")
        p.wait(gp, "sc")
        p.wait(sy, "sc")
        p.wait(ac, "sc")
        p.wait(gp, "zz")

        p.wait(dv, "gld")
        trfmk = [0] * 8
        ldmkC = [0] * 8
        for til in range(8):
            if til == 0:
                p.wait(sy, "pe", "dv", "ac")
            else:
                p.wait_at(sy, "pe", trfmk[til - 1])
            p.inc(sy.dma_start(out=sXr4[:, :].rearrange("p (j m) -> p j m", j=4),
                               in_=t.xbuf.rearrange("(t p) m -> p t m", p=128)[:, til * 4:(til + 1) * 4, :]),
                  "ld", 16)
            ldmkC[til] = p.at("ld")
            p.wait_at(dv, "ld", ldmkC[til])
            p.wait(dv, "sc")
            for j in range(4):
                p.dvx(dv, dv.tensor_copy(sScf[:, j:j + 1],
                                         sXr4[:, j * 1032 + 1024: j * 1032 + 1025]))
                p.dvx(dv, dv.scalar_tensor_tensor(sTokF[:, j:j + 1],
                                                  sXr4[:, j * 1032 + 1025: j * 1032 + 1026], 128.0,
                                                  sXr4[:, j * 1032 + 1026: j * 1032 + 1027],
                                                  OP.mult, OP.add))
            p.dvx(dv, dv.tensor_copy(sTokU[:, :], sTokF[:, :]))
            for j in range(4):
                p.dvx(dv, dv.tensor_scalar(sXr4[:, j * 1032: j * 1032 + 1024],
                                           sXr4[:, j * 1032: j * 1032 + 1024],
                                           sScf[:, j:j + 1],
                                           None, OP.mult))
            for j in range(4):
                tb = 6 + (j % 2)
                p.wait(pe, "dv")
                for cchunk in range(8):
                    p.inc(pe.transpose(
                        bankp(tb, 0, 128, 64, off=cchunk * 64).bitcast(BF16)[:, 0:128],
                        sXr4[:, j * 1032 + cchunk * 128: j * 1032 + (cchunk + 1) * 128],
                        C.sEyeB[:, :]), "pe")
                p.wait(dv, "pe")
                p.dvx(dv, dv.tensor_copy(
                    sRt[:, :].rearrange("p (c t) -> p c t", c=8)[:, :, j * 128:(j + 1) * 128],
                    bankp(tb, 0, 128, 512).bitcast(BF16)[:, 0:1024].rearrange(
                        "p (c t) -> p c t", c=8)))
            trfmk[til] = p.at("pe")
            consumed_dv = [0] * 16
            consumed_ac = [0] * 16
            for m in range(16):
                gb, ub = (bank(0), bank(1)) if m % 2 == 0 else (bank(4), bank(5))
                if m >= 2:
                    p.wait_at(pe, "dv", consumed_dv[m - 2])
                    p.wait_at(pe, "ac", consumed_ac[m - 2])
                elif m == 0:
                    p.wait(pe, "dv", "ac")
                for k in range(8):
                    p.inc(pe.matmul(gb, sW1[:, k * 2048 + m * 128: k * 2048 + (m + 1) * 128],
                                    sRt[:, k * 512:(k + 1) * 512],
                                    start=(k == 0), stop=(k == 7)), "pe")
                for k in range(8):
                    p.inc(pe.matmul(ub, sW3[:, k * 2048 + m * 128: k * 2048 + (m + 1) * 128],
                                    sRt[:, k * 512:(k + 1) * 512],
                                    start=(k == 0), stop=(k == 7)), "pe")
                p.wait(ac, "pe")
                sgs = sGs if m % 2 == 0 else sGs2
                p.acx(ac, ac.activation(out=sgs[:, :], in_=gb, func=AF.Silu))
                consumed_ac[m] = p.at("ac")
                p.wait(dv, "ac")
                p.dvx(dv, dv.tensor_tensor(sH1[:, m * 512:(m + 1) * 512], sgs[:, :],
                                           ub, OP.mult))
                consumed_dv[m] = p.at("dv")
            dcon = [0] * 8
            for dm in range(8):
                db = bank(2) if dm % 2 == 0 else bank(3)
                if dm >= 2:
                    p.wait_at(pe, "dv", dcon[dm - 2])
                else:
                    p.wait(pe, "dv")
                for k in range(16):
                    p.inc(pe.matmul(db, sW2[:, k * 1024 + dm * 128: k * 1024 + (dm + 1) * 128],
                                    sH1[:, k * 512:(k + 1) * 512],
                                    start=(k == 0), stop=(k == 15)), "pe")
                p.wait(dv, "pe")
                p.dvx(dv, dv.tensor_copy(sEoF[:, dm * 512:(dm + 1) * 512], db))
                dcon[dm] = p.at("dv")
            for j in range(4):
                tb = 6 + (j % 2)
                p.wait(pe, "dv")
                for dm in range(8):
                    p.inc(pe.transpose(
                        bankp(tb, 0, 128, 64, off=dm * 64).bitcast(BF16)[:, 0:128],
                        sEoF[:, dm * 512 + j * 128: dm * 512 + (j + 1) * 128],
                        C.sEyeB[:, :]), "pe")
                p.wait(dv, "pe", "sc")
                p.dvx(dv, dv.tensor_copy(
                    sEoT[:, j * 1024:(j + 1) * 1024],
                    bankp(tb, 0, 128, 512).bitcast(BF16)[:, 0:1024]))
            p.wait(gp, "dv")
            for j in range(4):
                p.inc(gp.indirect_dma_start(
                    out=t.moeD[:, :],
                    out_offset=bass.IndirectOffsetOnAxis(ap=sTokU[:, j:j + 1], axis=0),
                    in_=sEoT[:, j * 1024:(j + 1) * 1024], in_offset=None,
                    bounds_check=bctok, oob_is_err=False), "sc", 16)

        p.barrier([sy, gp, pe, dv, ac])


def emit_D(nc, p, C, ppool, t):
    TT = 512
    NTILE = TPC // TT

    x1_r = t.x1fD.rearrange("(c p) t -> p c t", p=128)
    mo_r = [t.moeRS.rearrange("(t p) m -> p t m", p=128),
            t.moeRS2.rearrange("(t p) m -> p t m", p=128)]
    x2f_r = t.x2fD.rearrange("(c p) t -> p c t", p=128)
    x2b_r = t.x2bD.rearrange("(c p) t -> p c t", p=128)

    with ExitStack() as st:
        sb = lambda name, shape, dt=F32: st.enter_context(nc.sbuf_tensor("D" + name, shape, dt))
        sAcc = sb("sAcc", [128, 8 * TT])
        sMoT = sb("sMoT", [128, 4 * 1024], BF16)
        sRawD = [sb(f"sRaw{i}", [128, 8 * TT]) for i in range(2)]
        sSq = sb("sSq", [128, TT])
        sR1 = sb("sR1", [1, TT]); sR2 = sb("sR2", [1, TT]); sR3 = sb("sR3", [1, TT])
        sX2 = sb("sX2", [128, 8 * TT])
        sX2b = sb("sX2b", [128, 8 * TT], BF16)
        sEpsR = sb("sEpsR", [1, 1])

        def bank(i, w=512, off=0):
            return ppool[:, i * 512 + off: i * 512 + off + w]

        def bankp(i, p0, prows, w, off=0):
            return ppool[p0:p0 + prows, i * 512 + off: i * 512 + off + w]

        sy, gp, pe, dv, ac = nc.sync, nc.gpsimd, nc.tensor, nc.vector, nc.scalar

        p.dvx(dv, dv.memset(sEpsR[:, :], LNEPS))

        trD = [0] * NTILE
        rawdv = [0] * NTILE
        rawac = [0] * NTILE
        prevcp = [0, 0]
        for tt in range(NTILE):
            tsl = slice(tt * TT, (tt + 1) * TT)
            raw = sRawD[tt % 2]
            if tt == 0:
                p.wait(sy, "pe", "dv", "ac")
            else:
                p.wait_at(sy, "pe", trD[tt - 1])
            if tt >= 2:
                p.wait_at(sy, "dv", rawdv[tt - 2])
                p.wait_at(sy, "ac", rawac[tt - 2])
            p.wait(sy, "cmo" if tt < 2 else "cm2")
            p.inc(sy.dma_start(out=raw[:, :].rearrange("p (c t) -> p c t", c=8),
                               in_=x1_r[:, :, tsl]), "ld", 16)
            p.inc(sy.dma_start(out=sMoT[:, :].rearrange("p (j m) -> p j m", j=4),
                               in_=mo_r[tt // 2][:, (tt % 2) * 4:(tt % 2) * 4 + 4, :]), "ld", 16)
            ldm = p.at("ld")
            cpmk = [0] * 4
            for grp in range(4):
                tb = grp % 2
                if grp == 0:
                    p.wait_at(pe, "ld", ldm)
                if grp < 2:
                    p.wait_at(pe, "dv", prevcp[grp])
                else:
                    p.wait_at(pe, "dv", cpmk[grp - 2])
                for c in range(8):
                    p.inc(pe.transpose(
                        bankp(tb, 0, 128, 64, off=c * 64).bitcast(BF16)[:, 0:128],
                        sMoT[:, grp * 1024 + c * 128: grp * 1024 + (c + 1) * 128],
                        C.sEyeB[:, :]), "pe")
                p.wait(dv, "pe")
                p.dvx(dv, dv.tensor_copy(
                    sAcc[:, :].rearrange("p (c t) -> p c t", c=8)[:, :, grp * 128:(grp + 1) * 128],
                    bankp(tb, 0, 128, 512).bitcast(BF16)[:, 0:1024].rearrange(
                        "p (c t) -> p c t", c=8)))
                cpmk[grp] = p.at("dv")
            trD[tt] = p.at("pe")
            prevcp = [cpmk[2], cpmk[3]]
            p.wait_at(dv, "ld", ldm)
            p.dvx(dv, dv.tensor_tensor(raw[:, :], raw[:, :], sAcc[:, :], OP.add))
            p.wait(pe, "dv")
            for c in range(8):
                p.inc(pe.matmul(bankp(4, 0, 1, TT), C.sOnesCol[:, :],
                                raw[:, c * TT:(c + 1) * TT],
                                start=(c == 0), stop=(c == 7)), "pe")
            p.wait(ac, "dv", "st")
            p.acx(ac, ac.activation(out=sX2[:, :], in_=raw[:, :], func=AF.Square))
            p.wait(pe, "ac")
            for c in range(8):
                p.inc(pe.matmul(bankp(5, 0, 1, TT), C.sOnesCol[:, :],
                                sX2[:, c * TT:(c + 1) * TT],
                                start=(c == 0), stop=(c == 7)), "pe")
            rawac[tt] = p.at("ac")
            p.wait(dv, "pe")
            p.dvx(dv, dv.tensor_scalar(sR1[:, :], bankp(4, 0, 1, TT), 1.0 / D_MODEL,
                                       None, OP.mult))
            p.dvx(dv, dv.tensor_scalar(sR2[:, :], bankp(5, 0, 1, TT), 1.0 / D_MODEL,
                                       None, OP.mult))
            p.dvx(dv, dv.tensor_tensor(sR3[:, :], sR1[:, :], sR1[:, :], OP.mult))
            p.dvx(dv, dv.tensor_tensor(sR2[:, :], sR2[:, :], sR3[:, :], OP.subtract))
            p.wait(ac, "dv")
            p.acx(ac, ac.activation(out=sR3[:, :], in_=sR2[:, :], func=AF.Ln, bias=sEpsR[:, :]))
            p.acx(ac, ac.activation(out=sR2[:, :], in_=sR3[:, :], func=AF.Exp, scale=-0.5))
            p.wait(dv, "ac")
            p.dvx(dv, dv.scalar_tensor_tensor(sR3[:, :], sR1[:, :], -1.0, sR2[:, :],
                                              OP.mult, OP.mult))
            p.wait(pe, "dv")
            p.inc(pe.matmul(bank(6, TT), C.sOnes[:, :], sR2[:, :], start=True, stop=True), "pe")
            p.inc(pe.matmul(bank(7, TT), C.sOnes[:, :], sR3[:, :], start=True, stop=True), "pe")
            p.wait(dv, "pe", "st")
            for c in range(8):
                csl = slice(c * TT, (c + 1) * TT)
                p.dvx(dv, dv.tensor_tensor(sX2[:, csl], raw[:, csl], bank(6, TT), OP.mult))
                p.dvx(dv, dv.tensor_tensor(sX2[:, csl], sX2[:, csl], bank(7, TT), OP.add))
                p.dvx(dv, dv.tensor_copy(sX2b[:, csl], sX2[:, csl]))
            rawdv[tt] = p.at("dv")
            p.wait(sy, "dv")
            p.inc(sy.dma_start(out=x2f_r[:, :, tsl],
                               in_=sX2[:, :].rearrange("p (c t) -> p c t", c=8)), "st", 16)
            p.inc(sy.dma_start(out=x2b_r[:, :, tsl],
                               in_=sX2b[:, :].rearrange("p (c t) -> p c t", c=8)), "st", 16)

        p.barrier([sy, gp, pe, dv, ac])



def emit_E(nc, p, C, ppool, t):
    T = TPC
    NH = 16

    x2_r = t.x2bD.rearrange("(c p) t -> p c t", p=128)
    phiq_r = t.phiqD.rearrange("(g p) m -> p g m", p=128)

    with ExitStack() as st:
        sb = lambda name, shape, dt=F32: st.enter_context(nc.sbuf_tensor("E" + name, shape, dt))
        sX2 = sb("sX2", [128, 8 * T], BF16)
        sVt = sb("sVt", [128, 16 * 1024], BF16)
        sWq = sb("sWq", [128, 8 * 1024], BF16)
        sWk = sb("sWk", [128, 8 * 1024], BF16)
        sProj = sb("sProj", [128, 8 * 512], BF16)
        sOnes2 = sb("sOnes2", [128, 2])
        sQh = sb("sQh", [128, T], BF16)
        sKh = sb("sKh", [128, T], BF16)
        sQsq = sb("sQsq", [128, T])
        sPhiQh2 = sb("sPhiQh2", [128, 16 * 512], BF16)
        sMx4 = [sb(f"sMx4_{i}", [128, 4]) for i in range(2)]
        sB4 = [sb(f"sB4_{i}", [128, 4]) for i in range(2)]
        sXwS = [sb(f"sXwS_{i}", [128, 1024]) for i in range(2)]
        sPhi4 = [sb(f"sPhi4_{i}", [128, 1024], BF16) for i in range(2)]
        sKVo = sb("sKVo", [128, 130])
        sOnesColB = sb("sOnesColB", [128, 1], BF16)

        def bank(i, w=512, off=0):
            return ppool[:, i * 512 + off: i * 512 + off + w]

        def bankp(i, p0, prows, w, off=0):
            return ppool[p0:p0 + prows, i * 512 + off: i * 512 + off + w]

        sy, gp, pe, dv, ac = nc.sync, nc.gpsimd, nc.tensor, nc.vector, nc.scalar

        p.inc(sy.dma_start(out=sX2[:, :].rearrange("p (c t) -> p c t", c=8),
                           in_=x2_r[:, :, :]), "ld", 16)
        p.inc(sy.dma_start(out=sOnes2[:, :], in_=t.ones2[:, :]), "ld", 16)
        p.inc(gp.dma_start(out=sProj[:, :], in_=t.projT[:, :]), "gld", 16)
        p.inc(gp.dma_start(out=sWq[:, :].rearrange("p (c m) -> p c m", c=8),
                           in_=t.wvT.rearrange("(c p) m -> p c m", p=128)), "gld", 16)
        p.wait(dv, "ld")
        p.dvx(dv, dv.memset(sOnesColB[:, :], 1.0))
        p.wait(pe, "ld", "gld", "dv")
        vdv = [0] * 16
        for tg in range(16):
            vb = 0 if tg % 2 == 0 else 2
            if tg >= 2:
                p.wait_at(pe, "dv", vdv[tg - 2])
            for n in range(2):
                for k in range(8):
                    p.inc(pe.matmul(bank(vb + n), sX2[:, k * T + tg * 128: k * T + (tg + 1) * 128],
                                    sWq[:, k * 1024 + n * 512:(k * 1024) + (n + 1) * 512],
                                    start=(k == 0), stop=(k == 7)), "pe")
            p.wait(dv, "pe")
            for n in range(2):
                p.dvx(dv, dv.tensor_copy(sVt[:, tg * 1024 + n * 512: tg * 1024 + (n + 1) * 512],
                                         bank(vb + n)))
            vdv[tg] = p.at("dv")
        p.wait(gp, "dv")
        p.inc(gp.dma_start(out=sWq[:, :].rearrange("p (c m) -> p c m", c=8),
                           in_=t.wqT.rearrange("(c p) m -> p c m", p=128)), "gld", 16)
        p.inc(gp.dma_start(out=sWk[:, :].rearrange("p (c m) -> p c m", c=8),
                           in_=t.wkT.rearrange("(c p) m -> p c m", p=128)), "gld", 16)
        p.wait(pe, "gld")

        for hp in range(NH // 2):
            prmk = {}
            p.wait(pe, "dv", "ac")
            for bi, (buf, w) in enumerate(((sQh, sWq), (sKh, sWk))):
                for n in range(4):
                    g8 = bi * 4 + n
                    pb = 3 if g8 % 2 == 0 else 4
                    if g8 >= 2:
                        p.wait_at(pe, "dv", prmk[g8 - 2])
                    for k in range(8):
                        p.inc(pe.matmul(bankp(pb, 0, 128, 512),
                                        w[:, k * 1024 + hp * 128: k * 1024 + (hp + 1) * 128],
                                        sX2[:, k * T + n * 512: k * T + (n + 1) * 512],
                                        start=(k == 0), stop=(k == 7)), "pe")
                    p.wait(dv, "pe")
                    p.dvx(dv, dv.tensor_scalar(buf[:, n * 512:(n + 1) * 512],
                                               bankp(pb, 0, 128, 512), 0.125, None, OP.mult))
                    prmk[g8] = p.at("dv")
            for side in range(2):
                qb = sQh if side == 0 else sKh
                p.wait(ac, "dv", "pe")
                p.acx(ac, ac.activation(out=sQsq[:, :], in_=qb[:, :], func=AF.Square))
                if side == 0:
                    p.wait(dv, "st")
                pemk = [0] * 8
                dvmk = [0] * 8
                acmk = [0] * 8
                for G in range(8):
                    par = G % 2
                    xwb = [3, 5][par]
                    if G == 0:
                        p.wait(pe, "ac")
                    if G >= 2:
                        p.wait_at(pe, "dv", dvmk[G - 2])
                    for j in range(2):
                        tg = G * 2 + j
                        p.inc(pe.matmul(bankp(xwb + j, 0, 128, 512),
                                        qb[:, tg * 128:(tg + 1) * 128],
                                        sProj[:, hp * 512:(hp + 1) * 512],
                                        start=True, stop=True), "pe")
                        p.inc(pe.matmul(bankp(2, 0, 128, 2, off=side * 64 + G * 4 + j * 2),
                                        sQsq[:, tg * 128:(tg + 1) * 128],
                                        sOnes2[:, :], start=True, stop=True), "pe")
                    pemk[G] = p.at("pe")
                    p.wait_at(dv, "pe", pemk[G])
                    if G >= 2:
                        p.wait_at(dv, "ac", acmk[G - 2])
                    xwp = ppool[:, xwb * 512: xwb * 512 + 1024]
                    p.dvx(dv, dv.tensor_reduce(sMx4[par][:, :],
                                               xwp.rearrange("p (j m) -> p j m", j=4),
                                               axis=mybir.AxisListType.X, op=OP.max))
                    p.inc(dv.scalar_tensor_tensor(sB4[par][:, :],
                                                  bankp(2, 0, 128, 4, off=side * 64 + G * 4), 0.5,
                                                  sMx4[par][:, :], OP.mult, OP.add), "dv")
                    dv.drain()
                    p.dvx(dv, dv.tensor_scalar(sB4[par][:, :], sB4[par][:, :], LNC16, -1.0,
                                               OP.add, OP.mult))
                    b4e = sB4[par][:, :].unsqueeze(2).broadcast_to([128, 4, 256])
                    p.dvx(dv, dv.tensor_tensor(sXwS[par][:, :].rearrange("p (j m) -> p j m", j=4),
                                               xwp.rearrange("p (j m) -> p j m", j=4),
                                               b4e, OP.add))
                    dvmk[G] = p.at("dv")
                    p.wait_at(ac, "dv", dvmk[G])
                    p.acx(ac, ac.activation(out=sPhi4[par][:, :], in_=sXwS[par][:, :], func=AF.Exp))
                    acmk[G] = p.at("ac")
                    p.wait_at(dv, "ac", acmk[G])
                    if side == 0:
                        p.dvx(dv, dv.tensor_scalar(
                            sPhiQh2[:, G * 1024:(G + 1) * 1024],
                            sPhi4[par][:, :], EPS16, None, OP.add))
                        dvmk[G] = p.at("dv")
                    else:
                        p.dvx(dv, dv.tensor_scalar(sPhi4[par][:, :], sPhi4[par][:, :], EPS16,
                                                   None, OP.add))
                        dvmk[G] = p.at("dv")
                        p.wait_at(pe, "dv", dvmk[G])
                        for j in range(2):
                            tg = G * 2 + j
                            for h2 in range(2):
                                hof = (hp * 2 + h2) * 64
                                for mc in range(2):
                                    phs = sPhi4[par][:, j * 512 + h2 * 256 + mc * 128:
                                                     j * 512 + h2 * 256 + (mc + 1) * 128]
                                    p.inc(pe.matmul(bankp(0, 0, 128, 64, off=h2 * 128 + mc * 64),
                                                    phs,
                                                    sVt[:, tg * 1024 + hof: tg * 1024 + hof + 64],
                                                    start=(tg == 0), stop=(tg == 15)), "pe")
                                    p.inc(pe.matmul(bankp(1, 0, 128, 1, off=h2 * 2 + mc),
                                                    phs, sOnesColB[:, :],
                                                    start=(tg == 0), stop=(tg == 15)), "pe")
                if side == 0:
                    p.wait(sy, "dv", "st")
                    p.inc(sy.dma_start(out=phiq_r[:, :, hp * 512:(hp + 1) * 512],
                                       in_=sPhiQh2[:, :].rearrange("p (g m) -> p g m", g=16)),
                          "st", 16)
            for h2 in range(2):
                h = hp * 2 + h2
                p.wait(dv, "pe", "st")
                p.dvx(dv, dv.tensor_copy(sKVo[:, 0:64], bankp(0, 0, 128, 64, off=h2 * 128)))
                p.dvx(dv, dv.tensor_copy(sKVo[:, 64:65], bankp(1, 0, 128, 1, off=h2 * 2)))
                p.dvx(dv, dv.tensor_copy(sKVo[:, 65:129], bankp(0, 0, 128, 64, off=h2 * 128 + 64)))
                p.dvx(dv, dv.tensor_copy(sKVo[:, 129:130], bankp(1, 0, 128, 1, off=h2 * 2 + 1)))
                p.wait(sy, "dv")
                p.inc(sy.dma_start(out=t.kvpD[h].rearrange("k p m -> p k m"),
                                   in_=sKVo[:, :].rearrange("p (k m) -> p k m", k=2)), "st", 16)

        p.barrier([sy, gp, pe, dv, ac])



def emit_F(nc, p, C, ppool, t):
    T = TPC
    NH = 16
    TT = 512

    x2_r = t.x2fD.rearrange("(c p) t -> p c t", p=128)
    phiq_r = t.phiqD.rearrange("(g p) m -> p g m", p=128)
    x3f_r = t.x3fD.rearrange("(c p) t -> p c t", p=128)
    x3b_r = t.x3bD.rearrange("(c p) t -> p c t", p=128)

    with ExitStack() as st:
        sb = lambda name, shape, dt=F32: st.enter_context(nc.sbuf_tensor("F" + name, shape, dt))
        sKV = sb("sKV", [128, NH * 2 * 65])
        sKVb = sb("sKVb", [128, NH * 2 * 65], BF16)
        sKsT = sb("sKsT", [32, 128])
        sKsCols = sb("sKsCols", [128, 32])
        sSelB = sb("sSelB", [32, 32 * 128])
        sKbAll = sb("sKbAll", [128, NH * 256], BF16)
        sPhiF = [sb(f"sPhiF{i}", [128, 1024], BF16) for i in range(2)]
        sPq4 = [sb(f"sPq4{i}", [128, 1024], BF16) for i in range(2)]
        sPqm4 = [sb(f"sPqm4{i}", [128, 1024], BF16) for i in range(2)]
        sT4 = [sb(f"sT4{i}", [128, 1024]) for i in range(2)]
        sDen4 = [sb(f"sDen4{i}", [128, 4]) for i in range(2)]
        sMerged = sb("sMerged", [128, 8 * T], BF16)
        sWo = sb("sWo", [128, 8 * 1024], BF16)
        sBo = sb("sBo", [128, 8])
        sXo = sb("sXo", [128, 8 * TT])
        sRaw = sb("sRaw", [128, 8 * TT])
        sSq = sb("sSq", [128, TT])
        sR1 = sb("sR1", [1, TT])
        sR2 = sb("sR2", [1, TT])
        sR3 = sb("sR3", [1, TT])
        sX3 = sb("sX3", [128, 8 * TT])
        sX3b = sb("sX3b", [128, 8 * TT], BF16)
        sEpsR = sb("sEpsR", [1, 1])

        def bank(i, w=512, off=0):
            return ppool[:, i * 512 + off: i * 512 + off + w]

        def bankp(i, p0, prows, w, off=0):
            return ppool[p0:p0 + prows, i * 512 + off: i * 512 + off + w]

        sy, gp, pe, dv, ac = nc.sync, nc.gpsimd, nc.tensor, nc.vector, nc.scalar

        p.wait(sy, "ckv")
        p.inc(sy.dma_start(out=sBo[:, :], in_=t.bo[:, :]), "ld", 16)
        p.inc(sy.dma_start(out=sSelB[:, :], in_=t.selB[:, :]), "ld", 16)
        p.inc(sy.dma_start(out=sKV[:, :].rearrange("p (h k m) -> p h k m", h=NH, k=2),
                           in_=t.kvpS.rearrange("h k p m -> p h k m")), "ld", 16)
        p.inc(gp.dma_start(out=sWo[:, :].rearrange("p (c m) -> p c m", c=8),
                           in_=t.woT.rearrange("(c p) m -> p c m", p=128)), "gld", 16)
        p.wait(dv, "ld")
        p.dvx(dv, dv.memset(sEpsR[:, :], LNEPS))
        p.dvx(dv, dv.tensor_copy(sKVb[:, :], sKV[:, :]))

        p.dvx(dv, dv.tensor_copy(sKsCols[:, :], sKV[:, 64:2080:65]))
        p.wait(pe, "dv", "ac")
        p.inc(pe.transpose(bankp(7, 0, 32, 128), sKsCols[:, :], C.sEyeF[:, :]), "pe")
        p.wait(dv, "pe")
        p.dvx(dv, dv.tensor_copy(sKsT[:, :], bankp(7, 0, 32, 128)))
        kbmk = [0] * 4
        for r in range(4):
            bks = [5, 6] if r % 2 == 0 else [3, 4]
            if r >= 2:
                p.wait_at(pe, "dv", kbmk[r - 2])
            elif r == 0:
                p.wait(pe, "dv")
            for jj in range(8):
                j = r * 8 + jj
                bkt = bks[jj // 4]
                off = (jj % 4) * 128
                p.inc(pe.matmul(bankp(bkt, 0, 128, 128, off=off),
                                sSelB[:, j * 128:(j + 1) * 128], sKsT[:, :],
                                start=True, stop=True), "pe")
            p.wait(dv, "pe")
            p.dvx(dv, dv.tensor_copy(sKbAll[:, r * 1024:(r + 1) * 1024],
                                     ppool[:, bks[0] * 512: bks[0] * 512 + 1024]))
            kbmk[r] = p.at("dv")
        p.wait(gp, "dv")

        ldmk = [0] * 64
        gpmk = [0] * 64
        dv1mk = [0] * 64
        pe1mk = [0] * 64
        dv2mk = [0] * 64
        pe2mk = [0] * 64
        dv3mk = [0] * 64
        for h in range(NH):
            hb = h * 130
            rows = slice((h % 2) * 64, (h % 2) * 64 + 64)
            for g in range(4):
                idx = h * 4 + g
                par = idx % 2
                trb = 5 if par == 0 else 3
                nmb = 7 if par == 0 else 6
                if idx >= 2:
                    p.wait_at(sy, "dv", dv1mk[idx - 2])
                p.inc(sy.dma_start(out=sPhiF[par][:, :].rearrange("p (j m) -> p j m", j=4),
                                   in_=phiq_r[:, g * 4:(g + 1) * 4, h * 256:(h + 1) * 256]),
                      "ld", 16)
                ldmk[idx] = p.at("ld")
                p.wait_at(gp, "ld", ldmk[idx])
                if idx >= 2:
                    p.wait_at(gp, "dv", dv1mk[idx - 2])
                kb_e = sKbAll[:, h * 256:(h + 1) * 256].unsqueeze(1).broadcast_to([128, 4, 256])
                p.gpx(gp, gp.tensor_tensor(sT4[par][:, :].rearrange("p (j m) -> p j m", j=4),
                                           sPhiF[par][:, :].rearrange("p (j m) -> p j m", j=4),
                                           kb_e, OP.mult))
                gpmk[idx] = p.at("gp")
                p.wait_at(dv, "gp", gpmk[idx])
                if idx >= 2:
                    p.wait_at(dv, "pe", pe1mk[idx - 2])
                p.dvx(dv, dv.tensor_reduce(sDen4[par][:, :],
                                           sT4[par][:, :].rearrange("p (j m) -> p j m", j=4),
                                           axis=mybir.AxisListType.X, op=OP.add))
                p.dvx(dv, dv.tensor_scalar(sDen4[par][:, :], sDen4[par][:, :], 1e-6, None, OP.add))
                p.dvx(dv, dv.reciprocal(sDen4[par][:, :], sDen4[par][:, :]))
                rd_e = sDen4[par][:, :].unsqueeze(2).broadcast_to([128, 4, 256])
                p.dvx(dv, dv.tensor_tensor(sPq4[par][:, :].rearrange("p (j m) -> p j m", j=4),
                                           sPhiF[par][:, :].rearrange("p (j m) -> p j m", j=4),
                                           rd_e, OP.mult))
                dv1mk[idx] = p.at("dv")
                p.wait_at(pe, "dv", dv1mk[idx])
                if idx >= 2:
                    p.wait_at(pe, "dv", dv2mk[idx - 2])
                for q in range(8):
                    p.inc(pe.transpose(
                        bankp(trb, 0, 128, 64, off=q * 64).bitcast(BF16)[:, 0:128],
                        sPq4[par][:, q * 128:(q + 1) * 128], C.sEyeB[:, :]), "pe")
                pe1mk[idx] = p.at("pe")
                p.wait_at(dv, "pe", pe1mk[idx])
                p.dvx(dv, dv.tensor_copy(sPqm4[par][:, :],
                                         bankp(trb, 0, 128, 512).bitcast(BF16)[:, 0:1024]))
                dv2mk[idx] = p.at("dv")
                p.wait_at(pe, "dv", dv2mk[idx])
                if idx >= 2:
                    p.wait_at(pe, "dv", dv3mk[idx - 2])
                for j in range(4):
                    for mc in range(2):
                        p.inc(pe.matmul(
                            ppool[rows, nmb * 512 + j * 128: nmb * 512 + (j + 1) * 128],
                            sKVb[:, hb + mc * 65: hb + mc * 65 + 64],
                            sPqm4[par][:, j * 256 + mc * 128: j * 256 + (mc + 1) * 128],
                            start=(mc == 0), stop=(mc == 1)), "pe")
                pe2mk[idx] = p.at("pe")
                p.wait_at(dv, "pe", pe2mk[idx])
                p.dvx(dv, dv.tensor_copy(
                    sMerged[rows, (h // 2) * T + g * 512:(h // 2) * T + (g + 1) * 512],
                    ppool[rows, nmb * 512: nmb * 512 + 512]))
                dv3mk[idx] = p.at("dv")

        for tt in range(T // TT):
            tsl = slice(tt * TT, (tt + 1) * TT)
            p.wait(sy, "pe", "dv", "ac")
            p.inc(sy.dma_start(out=sXo[:, :].rearrange("p (c t) -> p c t", c=8),
                               in_=x2_r[:, :, tsl]), "ld", 16)
            p.wait(pe, "dv", "ac")
            for m in range(8):
                for k in range(8):
                    p.inc(pe.matmul(bank(m % 4), sWo[:, k * 1024 + m * 128:(k * 1024) + (m + 1) * 128],
                                    sMerged[:, k * T + tt * TT: k * T + (tt + 1) * TT],
                                    start=(k == 0), stop=(k == 7)), "pe")
                p.wait(dv, "pe", "ld")
                p.dvx(dv, dv.scalar_tensor_tensor(sRaw[:, m * TT:(m + 1) * TT],
                                                  bank(m % 4), sBo[:, m:m + 1],
                                                  sXo[:, m * TT:(m + 1) * TT],
                                                  OP.add, OP.add))
                p.wait(pe, "dv")
            p.wait(pe, "dv")
            for c in range(8):
                p.inc(pe.matmul(bankp(4, 0, 1, TT), C.sOnesCol[:, :],
                                sRaw[:, c * TT:(c + 1) * TT],
                                start=(c == 0), stop=(c == 7)), "pe")
            p.wait(ac, "dv", "st")
            p.acx(ac, ac.activation(out=sX3[:, :], in_=sRaw[:, :], func=AF.Square))
            p.wait(pe, "ac")
            for c in range(8):
                p.inc(pe.matmul(bankp(5, 0, 1, TT), C.sOnesCol[:, :],
                                sX3[:, c * TT:(c + 1) * TT],
                                start=(c == 0), stop=(c == 7)), "pe")
            p.wait(dv, "pe")
            p.dvx(dv, dv.tensor_scalar(sR1[:, :], bankp(4, 0, 1, TT), 1.0 / D_MODEL,
                                       None, OP.mult))
            p.dvx(dv, dv.tensor_scalar(sR2[:, :], bankp(5, 0, 1, TT), 1.0 / D_MODEL,
                                       None, OP.mult))
            p.dvx(dv, dv.tensor_tensor(sR3[:, :], sR1[:, :], sR1[:, :], OP.mult))
            p.dvx(dv, dv.tensor_tensor(sR2[:, :], sR2[:, :], sR3[:, :], OP.subtract))
            p.wait(ac, "dv")
            p.acx(ac, ac.activation(out=sR3[:, :], in_=sR2[:, :], func=AF.Ln, bias=sEpsR[:, :]))
            p.acx(ac, ac.activation(out=sR2[:, :], in_=sR3[:, :], func=AF.Exp, scale=-0.5))
            p.wait(dv, "ac")
            p.dvx(dv, dv.scalar_tensor_tensor(sR3[:, :], sR1[:, :], -1.0, sR2[:, :],
                                              OP.mult, OP.mult))
            p.wait(pe, "dv")
            p.inc(pe.matmul(bank(6, TT), C.sOnes[:, :], sR2[:, :], start=True, stop=True), "pe")
            p.inc(pe.matmul(bank(7, TT), C.sOnes[:, :], sR3[:, :], start=True, stop=True), "pe")
            p.wait(dv, "pe", "st")
            for c in range(8):
                csl = slice(c * TT, (c + 1) * TT)
                p.dvx(dv, dv.tensor_tensor(sX3[:, csl], sRaw[:, csl], bank(6, TT), OP.mult))
                p.dvx(dv, dv.tensor_tensor(sX3[:, csl], sX3[:, csl], bank(7, TT), OP.add))
                p.dvx(dv, dv.tensor_copy(sX3b[:, csl], sX3[:, csl]))
            p.wait(sy, "dv")
            p.inc(sy.dma_start(out=x3f_r[:, :, tsl],
                               in_=sX3[:, :].rearrange("p (c t) -> p c t", c=8)), "st", 16)
            p.inc(sy.dma_start(out=x3b_r[:, :, tsl],
                               in_=sX3b[:, :].rearrange("p (c t) -> p c t", c=8)), "st", 16)

        p.barrier([sy, gp, pe, dv, ac])



def emit_G(nc, p, C, ppool, t):
    T = TPC
    TT = 512
    NHID = 3072

    x3f_r = t.x3fD.rearrange("(c p) t -> p c t", p=128)
    x3b_r = t.x3bD.rearrange("(c p) t -> p c t", p=128)
    out_r = t.outf.rearrange("(c p) t -> p c t", p=128)

    with ExitStack() as st:
        sb = lambda name, shape, dt=F32: st.enter_context(nc.sbuf_tensor("G" + name, shape, dt))
        sWg = sb("sWg", [128, 8 * NHID], BF16)
        sWu = sb("sWu", [128, 8 * NHID], BF16)
        sWdc = sb("sWdc", [128, 24 * 128], BF16)
        sWdc2 = sb("sWdc2", [128, 24 * 128], BF16)
        sXb = sb("sXb", [128, 8 * TT], BF16)
        sXf = sb("sXf", [128, 8 * TT])
        sGs = sb("sGs", [128, TT], BF16)
        sGs2 = sb("sGs2", [128, TT], BF16)
        sH1 = sb("sH1", [128, 24 * TT], BF16)
        sRaw = sb("sRaw", [128, 8 * TT])
        sSq = sb("sSq", [128, TT])
        sR1 = sb("sR1", [1, TT])
        sR2 = sb("sR2", [1, TT])
        sR3 = sb("sR3", [1, TT])
        sOut = sb("sOut", [128, 8 * TT])
        sEpsR = sb("sEpsR", [1, 1])

        def bank(i, w=512, off=0):
            return ppool[:, i * 512 + off: i * 512 + off + w]

        def bankp(i, p0, prows, w, off=0):
            return ppool[p0:p0 + prows, i * 512 + off: i * 512 + off + w]

        sy, gp, pe, dv, ac = nc.sync, nc.gpsimd, nc.tensor, nc.vector, nc.scalar

        p.wait(sy, "wdp")
        p.inc(sy.dma_start(out=sWg[:, :].rearrange("p (c m) -> p c m", c=8),
                           in_=t.wgB.rearrange("(c p) m -> p c m", p=128)), "ld", 16)
        p.inc(sy.dma_start(out=sWu[:, :].rearrange("p (c m) -> p c m", c=8),
                           in_=t.wuB.rearrange("(c p) m -> p c m", p=128)), "ld", 16)
        p.wait(dv, "ld")
        p.dvx(dv, dv.memset(sEpsR[:, :], LNEPS))
        p.wait(pe, "ld", "dv")
        dprev_pe = {}
        dprev_dv = {}

        for tt in range(T // TT):
            tsl = slice(tt * TT, (tt + 1) * TT)
            p.wait(sy, "pe", "dv", "ac")
            p.inc(sy.dma_start(out=sXb[:, :].rearrange("p (c t) -> p c t", c=8),
                               in_=x3b_r[:, :, tsl]), "ld", 16)
            p.inc(sy.dma_start(out=sXf[:, :].rearrange("p (c t) -> p c t", c=8),
                               in_=x3f_r[:, :, tsl]), "ld", 16)
            p.wait(pe, "ld")
            gcon_dv = [0] * 24
            gcon_ac = [0] * 24
            for m in range(24):
                gb, ub = (bank(0), bank(1)) if m % 2 == 0 else (bank(4), bank(5))
                if m >= 2:
                    p.wait_at(pe, "dv", gcon_dv[m - 2])
                    p.wait_at(pe, "ac", gcon_ac[m - 2])
                elif m == 0:
                    p.wait(pe, "dv", "ac")
                for k in range(8):
                    p.inc(pe.matmul(gb, sWg[:, k * NHID + m * 128: k * NHID + (m + 1) * 128],
                                    sXb[:, k * TT:(k + 1) * TT],
                                    start=(k == 0), stop=(k == 7)), "pe")
                for k in range(8):
                    p.inc(pe.matmul(ub, sWu[:, k * NHID + m * 128: k * NHID + (m + 1) * 128],
                                    sXb[:, k * TT:(k + 1) * TT],
                                    start=(k == 0), stop=(k == 7)), "pe")
                p.wait(ac, "pe")
                sgs = sGs if m % 2 == 0 else sGs2
                p.acx(ac, ac.activation(out=sgs[:, :], in_=gb, func=AF.Silu, scale=QS))
                gcon_ac[m] = p.at("ac")
                p.wait(dv, "ac")
                p.dvx(dv, dv.scalar_tensor_tensor(sH1[:, m * TT:(m + 1) * TT],
                                                  ub, QS, sgs[:, :],
                                                  OP.mult, OP.mult))
                gcon_dv[m] = p.at("dv")
            for dm in range(8):
                par = dm % 2
                wb = sWdc if par == 0 else sWdc2
                db = bank(2) if par == 0 else bank(6)
                if par in dprev_pe:
                    p.wait_at(gp, "pe", dprev_pe[par])
                p.inc(gp.dma_start(out=wb[:, :].rearrange("p (c m) -> p c m", c=24),
                                   in_=t.wdB[dm].rearrange("(c p) m -> p c m", p=128)), "gld", 16)
                if par in dprev_dv:
                    p.wait_at(pe, "dv", dprev_dv[par])
                if dm == 0:
                    p.wait(pe, "dv")
                p.wait(pe, "gld")
                for k in range(24):
                    p.inc(pe.matmul(db, wb[:, k * 128:(k + 1) * 128],
                                    sH1[:, k * TT:(k + 1) * TT],
                                    start=(k == 0), stop=(k == 23)), "pe")
                dprev_pe[par] = p.at("pe")
                p.wait(dv, "pe", "ld")
                p.dvx(dv, dv.scalar_tensor_tensor(sRaw[:, dm * TT:(dm + 1) * TT],
                                                  db, QS, sXf[:, dm * TT:(dm + 1) * TT],
                                                  OP.mult, OP.add))
                dprev_dv[par] = p.at("dv")
            p.wait(pe, "dv")
            for c in range(8):
                p.inc(pe.matmul(bankp(4, 0, 1, TT), C.sOnesCol[:, :],
                                sRaw[:, c * TT:(c + 1) * TT],
                                start=(c == 0), stop=(c == 7)), "pe")
            p.wait(ac, "dv", "st")
            p.acx(ac, ac.activation(out=sOut[:, :], in_=sRaw[:, :], func=AF.Square))
            p.wait(pe, "ac")
            for c in range(8):
                p.inc(pe.matmul(bankp(5, 0, 1, TT), C.sOnesCol[:, :],
                                sOut[:, c * TT:(c + 1) * TT],
                                start=(c == 0), stop=(c == 7)), "pe")
            p.wait(dv, "pe")
            p.dvx(dv, dv.tensor_scalar(sR1[:, :], bankp(4, 0, 1, TT), 1.0 / D_MODEL,
                                       None, OP.mult))
            p.dvx(dv, dv.tensor_scalar(sR2[:, :], bankp(5, 0, 1, TT), 1.0 / D_MODEL,
                                       None, OP.mult))
            p.dvx(dv, dv.tensor_tensor(sR3[:, :], sR1[:, :], sR1[:, :], OP.mult))
            p.dvx(dv, dv.tensor_tensor(sR2[:, :], sR2[:, :], sR3[:, :], OP.subtract))
            p.wait(ac, "dv")
            p.acx(ac, ac.activation(out=sR3[:, :], in_=sR2[:, :], func=AF.Ln, bias=sEpsR[:, :]))
            p.acx(ac, ac.activation(out=sR2[:, :], in_=sR3[:, :], func=AF.Exp, scale=-0.5))
            p.wait(dv, "ac")
            p.dvx(dv, dv.scalar_tensor_tensor(sR3[:, :], sR1[:, :], -1.0, sR2[:, :],
                                              OP.mult, OP.mult))
            p.wait(pe, "dv")
            p.inc(pe.matmul(bank(6, TT), C.sOnes[:, :], sR2[:, :], start=True, stop=True), "pe")
            p.inc(pe.matmul(bank(7, TT), C.sOnes[:, :], sR3[:, :], start=True, stop=True), "pe")
            p.wait(dv, "pe", "st")
            for c in range(8):
                csl = slice(c * TT, (c + 1) * TT)
                p.dvx(dv, dv.tensor_tensor(sOut[:, csl], sRaw[:, csl], bank(6, TT), OP.mult))
                p.dvx(dv, dv.tensor_tensor(sOut[:, csl], sOut[:, csl], bank(7, TT), OP.add))
            p.wait(sy, "dv")
            p.inc(sy.dma_start(out=out_r[:, :, tsl],
                               in_=sOut[:, :].rearrange("p (c t) -> p c t", c=8)), "st", 16)

        p.barrier([sy, gp, pe, dv, ac])


SEM_NAMES = ["ld", "gld", "pe", "dv", "ac", "st", "sc", "zz", "gp", "wdp",
             "cgy", "cx1", "csc", "cmo", "cm2", "ckv"]


def build_mega(upto="G", dumps=()):
    nc = bass.Bass("TRN2", target_bir_lowering=False)
    t = NS()
    dumps = set(dumps)

    def ein(name, shape, dt=F32):
        setattr(t, name, nc.dram_tensor(name, shape, dt, kind="ExternalInput").ap())

    def intl(name, shape, dt, addr_space="Local"):
        kind = "ExternalOutput" if name in dumps else "Internal"
        setattr(t, name, nc.dram_tensor(name, shape, dt, kind=kind,
                                        addr_space=addr_space).ap())

    ein("xb", [D_MODEL, T_FULL]); ein("wzx", [D_MODEL, 1024]); ein("wbc", [D_MODEL, 256])
    ein("wdt", [D_MODEL, 8]); ein("cw", [512, K_CONV]); ein("cb", [128, 4])
    ein("alog", [8, 1]); ein("dcol", [128, 4]); ein("dtb", [8, 1]); ein("nw", [128, 8])
    ein("selc", [8, 1024])
    ein("xo", [D_MODEL, TPC]); ein("wg", [D_MODEL, 8]); ein("woutT", [2048, D_MODEL])
    ein("goffB", [128, 16], U32)
    ein("w1", [D_MODEL, 2048]); ein("w3", [D_MODEL, 2048]); ein("w2", [2048, D_MODEL])
    ein("soff", [8, 1], U32); ein("rhiRow", [1, 128])
    ein("wqT", [D_MODEL, D_MODEL]); ein("wkT", [D_MODEL, D_MODEL]); ein("wvT", [D_MODEL, D_MODEL])
    ein("projT", [128, 8 * 512]); ein("ones2", [128, 2])
    ein("woT", [D_MODEL, D_MODEL]); ein("bo", [128, 8]); ein("selB", [32, 32 * 128])
    ein("wgT", [D_MODEL, 3072], I32); ein("wuT", [D_MODEL, 3072], I32)
    ein("wd3", [8, 3072, 128], I32)
    ein("onesr", [1, 128]); ein("trilc", [128, 128]); ein("eyeb", [128, 128], BF16)
    ein("eyef", [128, 128]); ein("iota", [128, 1]); ein("triu1", [128, 128])

    intl("yD", [2048, 2048], BF16)
    intl("ygAll", [8192, 2048], BF16,
         addr_space=maybe_share_collective_output_space("AllGather", RG4, prefer_local=True))
    intl("x1fD", [D_MODEL, TPC], F32)
    intl("tmD", [TPC, D_MODEL], BF16)
    intl("x1all", [16384, D_MODEL], BF16,
         addr_space=maybe_share_collective_output_space("AllGather", RG8, prefer_local=True))
    intl("sctD", [8, TPC], F32)
    intl("scAll", [64, TPC], F32,
         addr_space=maybe_share_collective_output_space("AllGather", RG8, prefer_local=True))
    intl("xbuf", [4096, 1032], BF16)
    intl("moeD", [16384, D_MODEL], BF16)
    intl("moeRS", [1024, D_MODEL], BF16,
         addr_space=maybe_share_collective_output_space("ReduceScatter", RG8, prefer_local=True))
    intl("moeRS2", [1024, D_MODEL], BF16,
         addr_space=maybe_share_collective_output_space("ReduceScatter", RG8, prefer_local=True))
    intl("x2fD", [D_MODEL, TPC], F32)
    intl("x2bD", [D_MODEL, TPC], BF16)
    intl("phiqD", [TPC, 16 * 256], BF16)
    intl("kvpD", [16, 2, 128, 65], F32)
    intl("kvpS", [16, 2, 128, 65], F32,
         addr_space=maybe_share_collective_output_space("AllReduce", RG4, prefer_local=True))
    intl("x3fD", [D_MODEL, TPC], F32)
    intl("x3bD", [D_MODEL, TPC], BF16)
    intl("wgB", [D_MODEL, 3072], BF16)
    intl("wuB", [D_MODEL, 3072], BF16)
    intl("wdB", [8, 3072, 128], BF16)
    t.outf = nc.dram_tensor("outf", [D_MODEL, TPC], F32, kind="ExternalOutput").ap()

    with ExitStack() as outer:
        p = P(nc, outer, SEM_NAMES)
        C = NS()
        C.sOnes = outer.enter_context(nc.sbuf_tensor("cOnes", [1, 128], F32))
        C.sOnesCol = outer.enter_context(nc.sbuf_tensor("cOnesCol", [128, 1], F32))
        C.sTril = outer.enter_context(nc.sbuf_tensor("cTril", [128, 128], F32))
        C.sTriu1 = outer.enter_context(nc.sbuf_tensor("cTriu1", [128, 128], F32))
        C.sEyeB = outer.enter_context(nc.sbuf_tensor("cEyeB", [128, 128], BF16))
        C.sEyeF = outer.enter_context(nc.sbuf_tensor("cEyeF", [128, 128], F32))
        C.sIota = outer.enter_context(nc.sbuf_tensor("cIota", [128, 1], F32))
        C.sZero = outer.enter_context(nc.sbuf_tensor("cZero", [128, 1024], BF16))
        C.sSoff = outer.enter_context(nc.sbuf_tensor("cSoff", [8, 1], U32))
        C.sOnesG = outer.enter_context(nc.sbuf_tensor("cOnesG", [128, 128], F32))
        ppool = outer.enter_context(nc.psum_tensor("ppool", [128, 4096], F32))

        sy, gp, pe, dv, ac = nc.sync, nc.gpsimd, nc.tensor, nc.vector, nc.scalar

        p.inc(sy.dma_start(out=C.sOnes[:, :], in_=t.onesr[:, :]), "ld", 16)
        p.inc(sy.dma_start(out=C.sTril[:, :], in_=t.trilc[:, :]), "ld", 16)
        p.inc(sy.dma_start(out=C.sTriu1[:, :], in_=t.triu1[:, :]), "ld", 16)
        p.inc(sy.dma_start(out=C.sEyeB[:, :], in_=t.eyeb[:, :]), "ld", 16)
        p.inc(sy.dma_start(out=C.sEyeF[:, :], in_=t.eyef[:, :]), "ld", 16)
        p.inc(sy.dma_start(out=C.sIota[:, :], in_=t.iota[:, :]), "ld", 16)
        p.inc(sy.dma_start(out=C.sSoff[:, :], in_=t.soff[:, :]), "ld", 16)
        p.wait(dv, "ld")
        p.dvx(dv, dv.memset(C.sOnesCol[:, :], 1.0))
        p.dvx(dv, dv.memset(C.sOnesG[:, :], 1.0))
        p.dvx(dv, dv.memset(C.sZero[:, :], 0.0))
        p.wait(ac, "dv")
        moe_r = t.moeD.rearrange("(r p) m -> p r m", p=128)
        for i in range(128):
            p.inc(ac.dma_start(out=moe_r[:, i, :], in_=C.sZero[:, :]), "zz", 16)

        emit_A(nc, p, C, ppool, t)
        p.inc(gp.dma_start(out=t.wgB[:, :], in_=t.wgT[:, :]), "wdp", 16)
        p.inc(gp.dma_start(out=t.wuB[:, :], in_=t.wuT[:, :]), "wdp", 16)
        p.inc(gp.dma_start(out=t.wdB[:, :, :], in_=t.wd3[:, :, :]), "wdp", 16)
        if upto == "A":
            p.barrier([sy, gp, pe, dv, ac])
            return nc
        p.inc(gp.collective_compute("AllGather", OP.bypass, replica_groups=RG4,
                                    ins=[t.yD[:, :]], outs=[t.ygAll[:, :]]), "cgy")

        emit_B(nc, p, C, ppool, t)
        if upto == "B":
            p.barrier([sy, gp, pe, dv, ac])
            return nc
        p.inc(gp.collective_compute("AllGather", OP.bypass, replica_groups=RG8,
                                    ins=[t.sctD[:, :]], outs=[t.scAll[:, :]]), "csc")
        p.inc(gp.collective_compute("AllGather", OP.bypass, replica_groups=RG8,
                                    ins=[t.tmD[:, :]], outs=[t.x1all[:, :]]), "cx1")

        emit_C(nc, p, C, ppool, t)
        if upto == "C":
            p.barrier([sy, gp, pe, dv, ac])
            return nc
        p.inc(gp.collective_compute("ReduceScatter", OP.add, replica_groups=RG8,
                                    ins=[t.moeD[0:8192, :]], outs=[t.moeRS[:, :]]), "cmo")
        p.inc(gp.collective_compute("ReduceScatter", OP.add, replica_groups=RG8,
                                    ins=[t.moeD[8192:16384, :]], outs=[t.moeRS2[:, :]]), "cm2")

        emit_D(nc, p, C, ppool, t)
        if upto == "D":
            p.barrier([sy, gp, pe, dv, ac])
            return nc

        emit_E(nc, p, C, ppool, t)
        p.inc(gp.collective_compute(
            "AllReduce", OP.add, replica_groups=RG4,
            ins=[t.kvpD.rearrange("h k p m -> (h k p) m")],
            outs=[t.kvpS.rearrange("h k p m -> (h k p) m")]), "ckv")

        emit_F(nc, p, C, ppool, t)
        if upto == "F":
            p.barrier([sy, gp, pe, dv, ac])
            return nc

        emit_G(nc, p, C, ppool, t)
    return nc


def _rhi_row():
    i = np.arange(128)
    rhi = 8 * (i // 16) + (i % 16) % 8 + 64 * ((i % 16) >= 8)
    return rhi.astype(np.float32).reshape(1, 128)


def _proj_blockdiag(proj):
    pj = np.transpose(proj, (0, 2, 1))
    bd = np.zeros((128, 8 * 512), np.float32)
    for hp in range(8):
        bd[0:64, hp * 512: hp * 512 + 256] = pj[2 * hp]
        bd[64:128, hp * 512 + 256: hp * 512 + 512] = pj[2 * hp + 1]
    return bd


def prep_mega_inputs(inputs):
    x = np.asarray(inputs["L0_attn_x"])
    in_proj_w = np.asarray(inputs["L0_attn_in_proj_w"])
    out_proj_w = np.asarray(inputs["L0_attn_out_proj_w"])
    consts = _consts()
    in_maps = []
    for cid in range(N_CORES):
        b, g = cid // 4, cid % 4
        q, e = cid % 4, cid
        hsl = slice(8 * g, 8 * g + 8)
        chsl = slice(512 * g, 512 * g + 512)
        wz = in_proj_w[chsl]
        wx = in_proj_w[2048 + 512 * g: 2048 + 512 * g + 512]
        goffB = np.zeros((128, 16), np.uint32)
        for c in range(16):
            goffB[:, c] = 2048 * (c // 4) + 512 * q + 128 * (c % 4) + np.arange(128)
        m = dict(
            xb=f32(x[b].T),
            wzx=f32(np.concatenate([wz, wx], 0).T),
            wbc=f32(in_proj_w[4096:4352].T),
            wdt=f32(in_proj_w[4352 + 8 * g: 4352 + 8 * g + 8].T),
            cw=f32(inputs["L0_attn_conv_w"][chsl]),
            cb=f32(np.asarray(inputs["L0_attn_conv_b"])[chsl].reshape(4, 128).T),
            alog=f32(np.asarray(inputs["L0_attn_A_log"])[hsl][:, None]),
            dcol=f32(np.repeat(np.asarray(inputs["L0_attn_D"])[hsl], 64).reshape(4, 128).T),
            dtb=f32(np.asarray(inputs["L0_attn_dt_bias"])[hsl][:, None]),
            nw=f32(np.asarray(inputs["L0_attn_norm_w"]).reshape(8, 128).T),
            selc=f32(np.repeat(np.eye(8, dtype=np.float32), 128, axis=1)),
            xo=f32(x[b].T[:, 2048 * q: 2048 * q + 2048]),
            wg=f32(inputs["L0_ffn_wg"]),
            woutT=f32(out_proj_w.T),
            goffB=goffB,
            w1=f32(inputs["L0_ffn_w1"][e]),
            w3=f32(inputs["L0_ffn_w3"][e]),
            w2=f32(inputs["L0_ffn_w2"][e]),
            soff=(8 * np.arange(8, dtype=np.uint32) + e).reshape(8, 1),
            rhiRow=_rhi_row(),
            wqT=f32(np.asarray(inputs["L1_attn_Wq"]).T),
            wkT=f32(np.asarray(inputs["L1_attn_Wk"]).T),
            wvT=f32(np.asarray(inputs["L1_attn_Wv"]).T),
            projT=_proj_blockdiag(np.asarray(inputs["L1_attn_proj"])),
            ones2=np.concatenate([np.repeat([[1.0, 0.0]], 64, 0),
                                  np.repeat([[0.0, 1.0]], 64, 0)]).astype(np.float32),
            woT=f32(np.asarray(inputs["L1_attn_Wo"]).T),
            bo=f32(np.asarray(inputs["L1_attn_bo"]).reshape(8, 128).T),
            selB=f32(np.repeat(np.eye(32, dtype=np.float32), 128, axis=1)),
            wgT=np.ascontiguousarray(np.asarray(inputs["L1_ffn_w_gate"]).T.astype(np.int32)),
            wuT=np.ascontiguousarray(np.asarray(inputs["L1_ffn_w_up"]).T.astype(np.int32)),
            wd3=np.ascontiguousarray(np.asarray(inputs["L1_ffn_w_down"]).reshape(
                8, 128, 3072).transpose(0, 2, 1).astype(np.int32)),
            **consts,
        )
        in_maps.append(m)
    return in_maps


def kernel_mega(**inputs):
    nc = build_mega()
    res = run_bass_kernel_spmd(nc, prep_mega_inputs(inputs),
                               core_ids=list(range(N_CORES))).results
    out = np.zeros((2, 8192, 1024), np.float32)
    for cid in range(N_CORES):
        b, q = cid // 4, cid % 4
        out[b, 2048 * q: 2048 * q + 2048] = res[cid]["outf"].T
    return out


def kernel(**inputs):
    return kernel_mega(**inputs)


# version 13
# speedup vs baseline: 1.1248x; 1.1248x over previous
from contextlib import ExitStack

import numpy as np
import ml_dtypes

import concourse.bass as bass
import concourse.mybir as mybir
from concourse.alu_op_type import AluOpType
from concourse.bass_utils import run_bass_kernel_spmd
from concourse.replica_groups import maybe_share_collective_output_space

F32 = mybir.dt.float32
BF16 = mybir.dt.bfloat16
I32 = mybir.dt.int32
U32 = mybir.dt.uint32
AF = mybir.ActivationFunctionType
OP = AluOpType

BF = ml_dtypes.bfloat16

D_MODEL = 1024
NB = 2
T_FULL = 8192
N_CORES = 8
TPC = 2048

K_CONV = 4
MEPS = 1e-6
LNEPS = 1e-5
LNC16 = 2.772588722239781
EPS16 = 1e-6 / 16.0
QS = 1.0 / 127.0

RG4 = [[0, 1, 2, 3], [4, 5, 6, 7]]
RG8 = [[0, 1, 2, 3, 4, 5, 6, 7]]


def f32(x):
    return np.ascontiguousarray(np.asarray(x), dtype=np.float32)


class P:

    def __init__(self, nc, stack, names):
        self.nc = nc
        self.sems = {}
        self.cnt = {}
        for name in names:
            self.sems[name] = stack.enter_context(nc.semaphore(name))
            self.cnt[name] = 0

    def inc(self, instr, name, n=1):
        instr.then_inc(self.sems[name], n)
        self.cnt[name] += n
        return instr

    def wait(self, eng, *names):
        for name in names:
            eng.wait_ge(self.sems[name], self.cnt[name])

    def acx(self, eng, instr):
        self.inc(instr, "ac")
        eng.drain()
        return instr

    def dvx(self, eng, instr):
        self.inc(instr, "dv")
        eng.drain()
        return instr

    def gpx(self, eng, instr):
        self.inc(instr, "gp")
        eng.drain()
        return instr

    def at(self, name):
        return self.cnt[name]

    def wait_at(self, eng, name, val):
        if val > 0:
            eng.wait_ge(self.sems[name], val)

    def barrier(self, engines):
        for eng in engines:
            for name in self.sems:
                self.wait_at(eng, name, self.cnt[name])


def _consts():
    return dict(
        onesr=np.ones((1, 128), np.float32),
        trilc=np.triu(np.ones((128, 128), np.float32)),
        eyeb=np.eye(128, dtype=BF),
        eyef=np.eye(128, dtype=np.float32),
        iota=np.arange(128, dtype=np.float32).reshape(128, 1),
        triu1=np.triu(np.ones((128, 128), np.float32), 1),
    )


class NS:
    pass


def emit_A(nc, p, C, ppool, t):
    TT = 512
    NTILE = T_FULL // TT
    NCH = 4
    HALO = 3
    NCK = TT // 128

    xb_r = t.xb.rearrange("(c p) t -> p c t", p=128)
    yD_r = t.yD.rearrange("(j c p) t -> p (j c) t", p=128, c=4)

    with ExitStack() as st:
        sb = lambda name, shape, dt=F32: st.enter_context(nc.sbuf_tensor("A" + name, shape, dt))
        sWzx = sb("sWzx", [128, 8 * 1024], BF16)
        sWbc = sb("sWbc", [128, 8 * 256], BF16)
        sWdt = sb("sWdt", [128, 8 * 8], BF16)
        sCw = sb("sCw", [128, NCH * K_CONV])
        sCb = sb("sCb", [128, NCH])
        sAlog = sb("sAlog", [8, 1])
        snegA = sb("snegA", [8, 1])
        sDcol = sb("sDcol", [128, 4])
        sDtb = sb("sDtb", [8, 1])
        sNw = sb("sNw", [128, 8])
        sEpsR = sb("sEpsR", [1, 1])
        sH = sb("sH", [128, 512])
        sHbf = sb("sHbf", [128, 512], BF16)
        sClCar = sb("sClCar", [8, 1])
        sXraw = sb("sXraw", [128, 8 * TT])
        sSq = sb("sSq", [128, 8 * TT])
        sRrms = sb("sRrms", [1, TT])
        sRrms2 = sb("sRrms2", [1, TT])
        sTmpR3 = sb("sTmpR3", [8, TT])
        sXn = sb("sXn", [128, 8 * TT], BF16)
        sZ = sb("sZ", [128, NCH * TT], BF16)
        sSiluZ = sb("sSiluZ", [128, NCH * TT], BF16)
        sXs = sb("sXs", [128, NCH * (TT + HALO)], BF16)
        sCacc0 = sb("sCacc0", [128, TT], BF16)
        sCacc1 = sb("sCacc1", [128, TT], BF16)
        sXh = sb("sXh", [128, NCH * TT], BF16)
        sB = sb("sB", [128, TT], BF16)
        sC = sb("sC", [128, TT], BF16)
        sDtr = sb("sDtr", [8, TT])
        sTmpR = sb("sTmpR", [8, TT])
        sTmpR2 = sb("sTmpR2", [8, TT])
        sClrel = sb("sClrel", [8, TT])
        sWs = sb("sWs", [8, TT])
        sDtsp = sb("sDtsp", [8, TT])
        sCl = sb("sCl", [8, TT])
        sDAc = sb("sDAc", [8, 1])
        sSelc = sb("sSelc", [8, 1024])
        sColT_all = sb("sColT_all", [128, NCK * 24])
        sTrilG_all = sb("sTrilG_all", [128, NCK * 128], BF16)
        sBsm_all = sb("sBsm_all", [128, NCK * 128], BF16)
        sXhT_all = sb("sXhT_all", [128, NCK * 512], BF16)
        sEmA_all = sb("sEmA_all", [128, NCK * 1024], BF16)
        sCdbA_all = sb("sCdbA_all", [128, NCK * 1024], BF16)
        sDAf_all = sb("sDAf_all", [128, NCK * 8])
        sMaskA_all = sb("sMaskA_all", [128, NCK * 1024], BF16)
        sCpA_all = sb("sCpA_all", [128, NCK * 1024], BF16)
        sUA_all = sb("sUA_all", [128, NCK * 512], BF16)
        sUpA_all = sb("sUpA_all", [128, NCK * 512], BF16)
        sDxh = sb("sDxh", [128, NCH * TT], BF16)
        sHtmp = sb("sHtmp", [128, 512])
        sY = sb("sY", [128, NCH * TT], BF16)
        sYevT = sb("sYevT", [128, 512])

        def bank(i, w=512, off=0):
            return ppool[:, i * 512 + off: i * 512 + off + w]

        def bankp(i, p0, prows, w, off=0):
            return ppool[p0:p0 + prows, i * 512 + off: i * 512 + off + w]

        sy, gp, pe, dv, ac = nc.sync, nc.gpsimd, nc.tensor, nc.vector, nc.scalar

        p.inc(sy.dma_start(out=sCw[:, :].rearrange("p (c k) -> p c k", c=NCH),
                           in_=t.cw.rearrange("(c p) k -> p c k", p=128)), "ld", 16)
        p.inc(sy.dma_start(out=sCb[:, :], in_=t.cb[:, :]), "ld", 16)
        p.inc(sy.dma_start(out=sAlog[:, :], in_=t.alog[:, :]), "ld", 16)
        p.inc(sy.dma_start(out=sDcol[:, :], in_=t.dcol[:, :]), "ld", 16)
        p.inc(sy.dma_start(out=sDtb[:, :], in_=t.dtb[:, :]), "ld", 16)
        p.inc(sy.dma_start(out=sNw[:, :], in_=t.nw[:, :]), "ld", 16)
        p.inc(sy.dma_start(out=sSelc[:, :], in_=t.selc[:, :]), "ld", 16)
        p.inc(gp.dma_start(out=sWzx[:, :].rearrange("p (c m) -> p c m", c=8),
                           in_=t.wzx.rearrange("(c p) m -> p c m", p=128)), "gld", 16)
        p.inc(gp.dma_start(out=sWbc[:, :].rearrange("p (c m) -> p c m", c=8),
                           in_=t.wbc.rearrange("(c p) m -> p c m", p=128)), "gld", 16)
        p.inc(gp.dma_start(out=sWdt[:, :].rearrange("p (c m) -> p c m", c=8),
                           in_=t.wdt.rearrange("(c p) m -> p c m", p=128)), "gld", 16)

        p.wait(dv, "ld", "gld")
        p.dvx(dv, dv.memset(sH[:, :], 0.0))
        p.dvx(dv, dv.memset(sHbf[:, :], 0.0))
        p.dvx(dv, dv.memset(sClCar[:, :], 0.0))
        p.dvx(dv, dv.memset(sXs[:, :], 0.0))
        p.dvx(dv, dv.memset(sEpsR[:, :], MEPS))
        p.wait(ac, "dv")
        p.acx(ac, ac.activation(out=sDAc[:, :], in_=sAlog[:, :], func=AF.Exp))
        p.wait(dv, "ac")
        p.dvx(dv, dv.tensor_scalar(snegA[:, :], sDAc[:, :], -1.0, None, OP.mult))

        sqmark = 0
        xnmark = 0
        for tt in range(NTILE):
            tsl = slice(tt * TT, (tt + 1) * TT)
            if tt == 0:
                p.wait(sy, "pe", "ac", "dv")
            else:
                p.wait_at(sy, "ac", sqmark)
                p.wait_at(sy, "dv", xnmark)
            p.inc(sy.dma_start(out=sXraw[:, :].rearrange("p (c t) -> p c t", c=8),
                               in_=xb_r[:, :, tsl]), "ld", 16)
            p.wait(dv, "st")
            if tt > 0:
                for m in range(NCH):
                    base = m * (TT + HALO)
                    p.dvx(dv, dv.tensor_copy(sXs[:, base: base + HALO],
                                         sXs[:, base + TT: base + TT + HALO]))
            p.wait(ac, "ld", "pe")
            p.acx(ac, ac.activation(out=sSq[:, :], in_=sXraw[:, :], func=AF.Square))
            sqmark = p.at("ac")
            p.wait(pe, "ac")
            for c in range(8):
                p.inc(pe.matmul(bankp(0, 0, 1, TT), C.sOnesCol[:, :],
                                sSq[:, c * TT:(c + 1) * TT],
                                start=(c == 0), stop=(c == 7)), "pe")
            p.wait(ac, "pe")
            p.acx(ac, ac.activation(out=sRrms2[:, :], in_=bankp(0, 0, 1, TT), func=AF.Ln,
                                scale=1.0 / D_MODEL, bias=sEpsR[:, :]))
            p.acx(ac, ac.activation(out=sRrms[:, :], in_=sRrms2[:, :], func=AF.Exp,
                                scale=-0.5))
            p.wait(pe, "ac")
            p.inc(pe.matmul(bank(1, TT), C.sOnes[:, :], sRrms[:, :], start=True, stop=True), "pe")
            p.wait(dv, "pe", "ld")
            for c in range(8):
                p.dvx(dv, dv.scalar_tensor_tensor(sXn[:, c * TT:(c + 1) * TT],
                                              sXraw[:, c * TT:(c + 1) * TT],
                                              sNw[:, c:c + 1], bank(1, TT),
                                              OP.mult, OP.mult))
            xnmark = p.at("dv")
            p.wait(pe, "dv", "ac")
            for m in range(4):
                for k in range(8):
                    p.inc(pe.matmul(bank(m), sWzx[:, k * 1024 + m * 128: k * 1024 + (m + 1) * 128],
                                    sXn[:, k * TT:(k + 1) * TT],
                                    start=(k == 0), stop=(k == 7)), "pe")
            for m in range(4):
                for k in range(8):
                    p.inc(pe.matmul(bank(4 + m),
                                    sWzx[:, k * 1024 + 512 + m * 128: k * 1024 + 512 + (m + 1) * 128],
                                    sXn[:, k * TT:(k + 1) * TT],
                                    start=(k == 0), stop=(k == 7)), "pe")
            p.wait(dv, "pe")
            for m in range(4):
                p.dvx(dv, dv.tensor_copy(sZ[:, m * TT:(m + 1) * TT], bank(m)))
            for m in range(4):
                p.dvx(dv, dv.tensor_copy(sXs[:, m * (TT + HALO) + HALO: (m + 1) * (TT + HALO)],
                                     bank(4 + m)))
            p.wait(ac, "dv")
            for m in range(4):
                p.acx(ac, ac.activation(out=sSiluZ[:, m * TT:(m + 1) * TT],
                                    in_=sZ[:, m * TT:(m + 1) * TT], func=AF.Silu))
            p.wait(pe, "dv")
            for k in range(8):
                p.inc(pe.matmul(bank(0), sWbc[:, k * 256: k * 256 + 128],
                                sXn[:, k * TT:(k + 1) * TT], start=(k == 0), stop=(k == 7)), "pe")
            for k in range(8):
                p.inc(pe.matmul(bank(1), sWbc[:, k * 256 + 128: (k + 1) * 256],
                                sXn[:, k * TT:(k + 1) * TT], start=(k == 0), stop=(k == 7)), "pe")
            for k in range(8):
                p.inc(pe.matmul(bankp(2, 0, 8, TT), sWdt[:, k * 8:(k + 1) * 8],
                                sXn[:, k * TT:(k + 1) * TT], start=(k == 0), stop=(k == 7)), "pe")
            p.wait(dv, "pe")
            p.dvx(dv, dv.tensor_copy(sB[:, :], bank(0)))
            p.dvx(dv, dv.tensor_copy(sC[:, :], bank(1)))
            p.dvx(dv, dv.tensor_copy(sDtr[:, :], bankp(2, 0, 8, TT)))

            for m in range(NCH):
                base = m * (TT + HALO)
                p.wait(dv, "ac")
                p.dvx(dv, dv.tensor_scalar(sCacc0[:, :], sXs[:, base: base + TT],
                                       sCw[:, m * K_CONV: m * K_CONV + 1],
                                       sCb[:, m:m + 1], OP.mult, OP.add))
                accs = [sCacc0, sCacc1]
                for k in range(1, K_CONV):
                    src, dst = accs[(k - 1) % 2], accs[k % 2]
                    p.dvx(dv, dv.scalar_tensor_tensor(dst[:, :], sXs[:, base + k: base + k + TT],
                                                  sCw[:, m * K_CONV + k: m * K_CONV + k + 1],
                                                  src[:, :], OP.mult, OP.add))
                final = accs[(K_CONV - 1) % 2]
                p.wait(ac, "dv")
                p.acx(ac, ac.activation(out=sXh[:, m * TT:(m + 1) * TT], in_=final[:, :],
                                    func=AF.Silu))
            p.wait(dv, "ac")
            for m in range(NCH):
                p.dvx(dv, dv.tensor_scalar(sDxh[:, m * TT:(m + 1) * TT],
                                           sXh[:, m * TT:(m + 1) * TT],
                                           sDcol[:, m:m + 1], None, OP.mult))
            p.dvx(dv, dv.tensor_scalar(sTmpR[:, :], sDtr[:, :], sDtb[:, :], None, OP.add))
            p.wait(ac, "dv")
            p.acx(ac, ac.activation(out=sTmpR2[:, :], in_=sTmpR[:, :], func=AF.Abs))
            p.acx(ac, ac.activation(out=sTmpR3[:, :], in_=sTmpR2[:, :], func=AF.Exp, scale=-1.0))
            p.acx(ac, ac.activation(out=sTmpR2[:, :], in_=sTmpR3[:, :], func=AF.Ln, bias=1.0))
            p.wait(dv, "ac")
            p.dvx(dv, dv.tensor_scalar(sTmpR3[:, :], sTmpR[:, :], 0.0, None, OP.max))
            p.dvx(dv, dv.tensor_tensor(sDtsp[:, :], sTmpR3[:, :], sTmpR2[:, :], OP.add))
            p.dvx(dv, dv.tensor_scalar(sTmpR[:, :], sDtsp[:, :], snegA[:, :], None, OP.mult))
            p.dvx(dv, dv.tensor_tensor_scan(sCl[:, :], sTmpR[:, :], sTmpR[:, :], sClCar[:, :],
                                        OP.add, OP.bypass))
            for c in range(NCK):
                prevc = sClCar[:, :] if c == 0 else sCl[:, c * 128 - 1: c * 128]
                csl = slice(c * 128, (c + 1) * 128)
                p.dvx(dv, dv.tensor_scalar(sClrel[:, csl], sCl[:, csl], prevc, None,
                                       OP.subtract))
                endc = sClrel[:, c * 128 + 127: c * 128 + 128]
                p.dvx(dv, dv.tensor_scalar(sTmpR2[:, csl], sClrel[:, csl], endc, None,
                                       OP.subtract))
            p.dvx(dv, dv.tensor_copy(sClCar[:, :], sCl[:, TT - 1: TT]))
            p.wait(ac, "dv")
            p.acx(ac, ac.activation(out=sTmpR3[:, :], in_=sTmpR2[:, :], func=AF.Exp, scale=-1.0))
            p.wait(dv, "ac")
            p.dvx(dv, dv.tensor_tensor(sWs[:, :], sTmpR3[:, :], sDtsp[:, :], OP.mult))

            clbs = [ppool[:, 5 * 512: 5 * 512 + 1024], ppool[:, 1 * 512: 1 * 512 + 1024]]
            rbank = [3, 4]
            pemark = [0] * NCK
            evdv = [0] * NCK
            evac = [0] * NCK
            for c in range(NCK):
                csl = slice(c * 128, (c + 1) * 128)
                s = c % 2
                R = rbank[s]
                clb = clbs[s]
                if c == 0:
                    p.wait(pe, "dv", "ac")
                elif c >= 2:
                    p.wait_at(pe, "dv", evdv[c - 2])
                    p.wait_at(pe, "ac", evac[c - 2])
                p.inc(pe.transpose(bankp(R, 0, 128, 8, off=0), sClrel[:, csl], C.sEyeF[0:8, 0:8]), "pe")
                p.inc(pe.transpose(bankp(R, 0, 128, 8, off=8), sWs[:, csl], C.sEyeF[0:8, 0:8]), "pe")
                p.inc(pe.transpose(bankp(R, 0, 128, 8, off=16), sDtsp[:, csl], C.sEyeF[0:8, 0:8]), "pe")
                p.inc(pe.matmul(bankp(R, 0, 128, 128, off=24), sB[:, csl], sC[:, csl],
                                start=True, stop=True), "pe")
                p.inc(pe.transpose(bankp(R, 0, 128, 64, off=152).bitcast(BF16)[:, 0:128],
                                   sB[:, csl], C.sEyeB[:, :]), "pe")
                for m in range(NCH):
                    p.inc(pe.transpose(
                        bankp(R, 0, 128, 64, off=216 + m * 64).bitcast(BF16)[:, 0:128],
                        sXh[:, m * TT + c * 128: m * TT + (c + 1) * 128], C.sEyeB[:, :]), "pe")
                for h in range(8):
                    p.inc(pe.matmul(clb[:, h * 128:(h + 1) * 128],
                                    sSelc[:, h * 128:(h + 1) * 128], sClrel[:, csl],
                                    start=True, stop=True), "pe")
                pemark[c] = p.at("pe")
                p.wait_at(dv, "pe", pemark[c])
                p.dvx(dv, dv.tensor_copy(sColT_all[:, c * 24:(c + 1) * 24], bankp(R, 0, 128, 24, off=0)))
                p.dvx(dv, dv.tensor_tensor(sTrilG_all[:, c * 128:(c + 1) * 128],
                                           bankp(R, 0, 128, 128, off=24), C.sTril[:, :], OP.mult))
                p.dvx(dv, dv.tensor_copy(sBsm_all[:, c * 128:(c + 1) * 128],
                                         bankp(R, 0, 128, 64, off=152).bitcast(BF16)[:, 0:128]))
                p.dvx(dv, dv.tensor_copy(sXhT_all[:, c * 512:(c + 1) * 512],
                                         bankp(R, 0, 128, 256, off=216).bitcast(BF16)[:, 0:512]))
                clcol_exp = sColT_all[:, c * 24: c * 24 + 8].unsqueeze(2).broadcast_to([128, 8, 128])
                miav = sSq[:, c * 1024:(c + 1) * 1024]
                p.dvx(dv, dv.tensor_tensor(miav.rearrange("p (h t) -> p h t", h=8),
                                           clb.rearrange("p (h t) -> p h t", h=8),
                                           clcol_exp, OP.subtract))
                p.dvx(dv, dv.tensor_scalar(miav, miav, 0.0, None, OP.min))
                evdv[c] = p.at("dv")
                p.wait_at(ac, "pe", pemark[c])
                p.acx(ac, ac.activation(out=sCdbA_all[:, c * 1024:(c + 1) * 1024], in_=clb,
                                        func=AF.Exp))
                p.acx(ac, ac.activation(out=sDAf_all[:, c * 8:(c + 1) * 8],
                                        in_=clb[:, 127:1024:128], func=AF.Exp))
                evac[c] = p.at("ac")

            p.wait(ac, "dv")
            p.acx(ac, ac.activation(out=sEmA_all[:, :], in_=sSq[:, 0:NCK * 1024], func=AF.Exp))
            p.wait(gp, "ac", "pe")
            for c in range(NCK):
                cs1 = slice(c * 1024, (c + 1) * 1024)
                trilg_exp = sTrilG_all[:, c * 128:(c + 1) * 128].unsqueeze(1).broadcast_to([128, 8, 128])
                p.gpx(gp, gp.tensor_tensor(sMaskA_all[:, cs1].rearrange("p (h t) -> p h t", h=8),
                                           sEmA_all[:, cs1].rearrange("p (h t) -> p h t", h=8),
                                           trilg_exp, OP.mult))
                c_exp = sC[:, c * 128:(c + 1) * 128].unsqueeze(1).broadcast_to([128, 8, 128])
                p.gpx(gp, gp.tensor_tensor(sCpA_all[:, cs1].rearrange("p (h t) -> p h t", h=8),
                                           sCdbA_all[:, cs1].rearrange("p (h t) -> p h t", h=8),
                                           c_exp, OP.mult))
            for c in range(NCK):
                cs5 = slice(c * 512, (c + 1) * 512)
                dt_exp = sColT_all[:, c * 24 + 16: c * 24 + 24].unsqueeze(2).broadcast_to([128, 8, 64])
                p.dvx(dv, dv.tensor_tensor(sUA_all[:, cs5].rearrange("p (h q) -> p h q", h=8),
                                           sXhT_all[:, cs5].rearrange("p (h q) -> p h q", h=8),
                                           dt_exp, OP.mult))
                ws_exp = sColT_all[:, c * 24 + 8: c * 24 + 16].unsqueeze(2).broadcast_to([128, 8, 64])
                p.dvx(dv, dv.tensor_tensor(sUpA_all[:, cs5].rearrange("p (h q) -> p h q", h=8),
                                           sXhT_all[:, cs5].rearrange("p (h q) -> p h q", h=8),
                                           ws_exp, OP.mult))
            p2gp = p.at("gp")
            p2dv = p.at("dv")
            p2ac = p.at("ac")

            st_mark = 0
            yev_mark = [0, 0]
            for c in range(NCK):
                csl = slice(c * 128, (c + 1) * 128)
                par3 = c % 2
                yb_ev = 7 if par3 == 0 else 5
                yb_od = 2 if par3 == 0 else 6
                hlb = 0 if par3 == 0 else 4
                if c == 0:
                    p.wait_at(pe, "gp", p2gp)
                    p.wait_at(pe, "dv", p2dv)
                else:
                    p.wait_at(pe, "dv", st_mark)
                    if c >= 2:
                        p.wait_at(pe, "dv", yev_mark[par3])
                for h in range(8):
                    ybank = yb_ev if h % 2 == 0 else yb_od
                    rows = slice((h % 2) * 64, (h % 2) * 64 + 64)
                    yp = ppool[rows, ybank * 512 + (h // 2) * 128: ybank * 512 + (h // 2) * 128 + 128]
                    first = h in (0, 1)
                    last = h in (6, 7)
                    p.inc(pe.matmul(yp, sUA_all[:, c * 512 + h * 64: c * 512 + (h + 1) * 64],
                                    sMaskA_all[:, c * 1024 + h * 128: c * 1024 + (h + 1) * 128],
                                    start=first, stop=False), "pe")
                    p.inc(pe.matmul(yp, sHbf[:, h * 64:(h + 1) * 64],
                                    sCpA_all[:, c * 1024 + h * 128: c * 1024 + (h + 1) * 128],
                                    start=False, stop=last), "pe")
                    p.inc(pe.matmul(bank(hlb)[:, h * 64:(h + 1) * 64],
                                    sBsm_all[:, c * 128:(c + 1) * 128],
                                    sUpA_all[:, c * 512 + h * 64: c * 512 + (h + 1) * 64],
                                    start=(h == 0), stop=(h == 7)), "pe")
                if c == 0:
                    p.wait_at(dv, "ac", p2ac)
                p.wait(dv, "pe")
                dA_view = sDAf_all[:, c * 8:(c + 1) * 8].unsqueeze(2).broadcast_to([128, 8, 64])
                p.dvx(dv, dv.tensor_tensor(sHtmp[:, :].rearrange("p (h q) -> p h q", h=8),
                                           sH[:, :].rearrange("p (h q) -> p h q", h=8),
                                           dA_view, OP.mult))
                p.dvx(dv, dv.tensor_tensor(sH[:, :], sHtmp[:, :], bank(hlb), OP.add))
                p.dvx(dv, dv.tensor_copy(sHbf[:, :], sH[:, :]))
                st_mark = p.at("dv")
                for par, ybank in ((0, yb_ev), (1, yb_od)):
                    rows = slice(par * 64, par * 64 + 64)
                    ypsv = ppool[rows, ybank * 512: ybank * 512 + 512]
                    dxh_v = sDxh[:, :].rearrange("p (m t) -> p m t", m=NCH)[rows, :, csl]
                    silu_v = sSiluZ[:, :].rearrange("p (m t) -> p m t", m=NCH)[rows, :, csl]
                    y_v = sY[:, :].rearrange("p (m t) -> p m t", m=NCH)[rows, :, csl]
                    p.dvx(dv, dv.tensor_tensor(sYevT[rows, :].rearrange("p (m t) -> p m t", m=4),
                                               ypsv.rearrange("p (m t) -> p m t", m=4),
                                               dxh_v, OP.add))
                    p.dvx(dv, dv.tensor_tensor(y_v, sYevT[rows, :].rearrange("p (m t) -> p m t", m=4),
                                               silu_v, OP.mult))
                yev_mark[par3] = p.at("dv")

            j, s = tt // 4, tt % 4
            p.wait(sy, "dv")
            p.inc(sy.dma_start(out=yD_r[:, 4 * j: 4 * j + 4, 512 * s: 512 * s + 512],
                               in_=sY[:, :].rearrange("p (m t) -> p m t", m=NCH)), "st", 16)

        p.barrier([sy, gp, pe, dv, ac])


def emit_B(nc, p, C, ppool, t, ntile=None, do_scores=True, do_tm=True):
    TT = 512
    NTILE = ntile or (TPC // TT)

    xo_r = t.xo.rearrange("(c p) t -> p c t", p=128)
    x1f_r = t.x1fD.rearrange("(c p) t -> p c t", p=128)
    tm_r = t.tmD.rearrange("(t p) m -> p t m", p=128)

    with ExitStack() as st:
        sb = lambda name, shape, dt=F32: st.enter_context(nc.sbuf_tensor("B" + name, shape, dt))
        sWout = sb("sWout", [128, 16 * 1024], BF16)
        sWg = sb("sWg", [128, 8 * 8])
        sXo = sb("sXo", [128, 8 * TT])
        sRaw = sb("sRaw", [128, 8 * TT])
        sSq = sb("sSq", [128, TT])
        sR1 = sb("sR1", [1, TT])
        sR2 = sb("sR2", [1, TT])
        sR3 = sb("sR3", [1, TT])
        sX1 = sb("sX1", [128, 8 * TT])
        sX1b = sb("sX1b", [128, 8 * TT], BF16)
        sTm = sb("sTm", [128, 4 * 1024], BF16)
        sEpsR = sb("sEpsR", [1, 1])
        sMx4s = sb("sMx4s", [128, 4])
        sLnA = sb("sLnA", [128, TT])
        sLnB = sb("sLnB", [128, TT])
        sSc4 = sb("sSc4", [128, 32])
        sScE4 = sb("sScE4", [128, 32])
        sScT = sb("sScT", [8, TT])
        sRec4 = sb("sRec4", [128, 4])
        sGoffB = sb("sGoffB", [128, 16], U32)
        sYgF = sb("sYgF", [128, 16 * TPC], BF16)

        def bank(i, w=512, off=0):
            return ppool[:, i * 512 + off: i * 512 + off + w]

        def bankp(i, p0, prows, w, off=0):
            return ppool[p0:p0 + prows, i * 512 + off: i * 512 + off + w]

        sy, gp, pe, dv, ac = nc.sync, nc.gpsimd, nc.tensor, nc.vector, nc.scalar
        bcyg = gp.to_reg(8191)

        p.inc(sy.dma_start(out=sWg[:, :].rearrange("p (c m) -> p c m", c=8),
                           in_=t.wg.rearrange("(c p) m -> p c m", p=128)), "ld", 16)
        p.inc(sy.dma_start(out=sGoffB[:, :], in_=t.goffB[:, :]), "ld", 16)
        p.inc(gp.dma_start(out=sWout[:, :].rearrange("p (c m) -> p c m", c=16),
                           in_=t.woutT.rearrange("(c p) m -> p c m", p=128)), "gld", 16)
        p.wait(dv, "ld")
        p.dvx(dv, dv.memset(sEpsR[:, :], LNEPS))
        p.wait(gp, "cgy", "ld")
        for c in range(16):
            p.inc(gp.indirect_dma_start(
                out=sYgF[:, c * TPC:(c + 1) * TPC], out_offset=None,
                in_=t.ygAll[:, :],
                in_offset=bass.IndirectOffsetOnAxis(ap=sGoffB[:, c:c + 1], axis=0),
                bounds_check=bcyg, oob_is_err=False), "gld", 16)

        for tt in range(NTILE):
            tsl = slice(tt * TT, (tt + 1) * TT)
            p.wait(sy, "pe", "dv", "ac")
            p.inc(sy.dma_start(out=sXo[:, :].rearrange("p (c t) -> p c t", c=8),
                               in_=xo_r[:, :, tsl]), "ld", 16)
            p.wait(pe, "dv", "ac", "gld", "ld")
            bomk = [0] * 8
            for m in range(8):
                if m >= 4:
                    p.wait_at(pe, "dv", bomk[m - 4])
                for k in range(16):
                    p.inc(pe.matmul(bank(m % 4), sWout[:, k * 1024 + m * 128: k * 1024 + (m + 1) * 128],
                                    sYgF[:, k * TPC + tt * TT: k * TPC + (tt + 1) * TT],
                                    start=(k == 0), stop=(k == 15)), "pe")
                p.wait(dv, "pe")
                p.dvx(dv, dv.scalar_tensor_tensor(sRaw[:, m * TT:(m + 1) * TT],
                                                  sXo[:, m * TT:(m + 1) * TT], 2.0,
                                                  bank(m % 4), OP.mult, OP.add))
                bomk[m] = p.at("dv")
            p.wait(pe, "dv")
            for c in range(8):
                p.inc(pe.matmul(bankp(0, 0, 1, TT), C.sOnesCol[:, :],
                                sRaw[:, c * TT:(c + 1) * TT],
                                start=(c == 0), stop=(c == 7)), "pe")
            p.wait(ac, "dv", "st")
            p.acx(ac, ac.activation(out=sX1[:, :], in_=sRaw[:, :], func=AF.Square))
            p.wait(pe, "ac")
            for c in range(8):
                p.inc(pe.matmul(bankp(1, 0, 1, TT), C.sOnesCol[:, :],
                                sX1[:, c * TT:(c + 1) * TT],
                                start=(c == 0), stop=(c == 7)), "pe")
            p.wait(dv, "pe")
            p.dvx(dv, dv.tensor_scalar(sR1[:, :], bankp(0, 0, 1, TT), 1.0 / D_MODEL,
                                       None, OP.mult))
            p.dvx(dv, dv.tensor_scalar(sR2[:, :], bankp(1, 0, 1, TT), 1.0 / D_MODEL,
                                       None, OP.mult))
            p.dvx(dv, dv.tensor_tensor(sR3[:, :], sR1[:, :], sR1[:, :], OP.mult))
            p.dvx(dv, dv.tensor_tensor(sR2[:, :], sR2[:, :], sR3[:, :], OP.subtract))
            p.wait(ac, "dv")
            p.acx(ac, ac.activation(out=sR3[:, :], in_=sR2[:, :], func=AF.Ln, bias=sEpsR[:, :]))
            p.acx(ac, ac.activation(out=sR2[:, :], in_=sR3[:, :], func=AF.Exp, scale=-0.5))
            p.wait(dv, "ac")
            p.dvx(dv, dv.scalar_tensor_tensor(sR3[:, :], sR1[:, :], -1.0, sR2[:, :],
                                              OP.mult, OP.mult))
            p.wait(pe, "dv")
            p.inc(pe.matmul(bank(2, TT), C.sOnes[:, :], sR2[:, :], start=True, stop=True), "pe")
            p.inc(pe.matmul(bank(3, TT), C.sOnes[:, :], sR3[:, :], start=True, stop=True), "pe")
            p.wait(dv, "pe", "st")
            p.dvx(dv, dv.tensor_copy(sLnA[:, :], bank(2, TT)))
            p.dvx(dv, dv.tensor_copy(sLnB[:, :], bank(3, TT)))
            rb_ = sLnA[:, :].unsqueeze(1).broadcast_to([128, 8, TT])
            mb_ = sLnB[:, :].unsqueeze(1).broadcast_to([128, 8, TT])
            p.dvx(dv, dv.tensor_tensor(sX1[:, :].rearrange("p (c t) -> p c t", c=8),
                                       sRaw[:, :].rearrange("p (c t) -> p c t", c=8),
                                       rb_, OP.mult))
            p.dvx(dv, dv.tensor_tensor(sX1[:, :].rearrange("p (c t) -> p c t", c=8),
                                       sX1[:, :].rearrange("p (c t) -> p c t", c=8),
                                       mb_, OP.add))
            p.dvx(dv, dv.tensor_copy(sX1b[:, :], sX1[:, :]))
            p.wait(sy, "dv")
            p.inc(sy.dma_start(out=x1f_r[:, :, tsl],
                               in_=sX1[:, :].rearrange("p (c t) -> p c t", c=8)), "st", 16)
            if do_scores:
                p.wait(pe, "dv", "ac")
                for tg in range(TT // 128):
                    for k in range(8):
                        p.inc(pe.matmul(bankp(4, 0, 128, 8, off=tg * 8),
                                        sX1[:, k * TT + tg * 128: k * TT + (tg + 1) * 128],
                                        sWg[:, k * 8:(k + 1) * 8],
                                        start=(k == 0), stop=(k == 7)), "pe")
                p.wait(dv, "pe")
                lg = bankp(4, 0, 128, 32)
                p.dvx(dv, dv.tensor_reduce(sMx4s[:, :], lg.rearrange("p (g e) -> p g e", g=4),
                                           axis=mybir.AxisListType.X, op=OP.max))
                mx_e = sMx4s[:, :].unsqueeze(2).broadcast_to([128, 4, 8])
                p.dvx(dv, dv.tensor_tensor(sScE4[:, :].rearrange("p (g e) -> p g e", g=4),
                                           lg.rearrange("p (g e) -> p g e", g=4),
                                           mx_e, OP.subtract))
                p.wait(ac, "dv")
                p.acx(ac, ac.activation(out=sSc4[:, :], in_=sScE4[:, :], func=AF.Exp))
                p.wait(dv, "ac")
                p.dvx(dv, dv.tensor_reduce(sRec4[:, :],
                                           sSc4[:, :].rearrange("p (g e) -> p g e", g=4),
                                           axis=mybir.AxisListType.X, op=OP.add))
                p.dvx(dv, dv.reciprocal(sRec4[:, :], sRec4[:, :]))
                rc_e = sRec4[:, :].unsqueeze(2).broadcast_to([128, 4, 8])
                p.dvx(dv, dv.tensor_tensor(sScE4[:, :].rearrange("p (g e) -> p g e", g=4),
                                           sSc4[:, :].rearrange("p (g e) -> p g e", g=4),
                                           rc_e, OP.mult))
                p.wait(pe, "dv")
                for tg in range(TT // 128):
                    p.inc(pe.transpose(bankp(5, 0, 8, 128, off=tg * 128),
                                       sScE4[:, tg * 8:(tg + 1) * 8], C.sEyeF[:, :]), "pe")
                p.wait(dv, "pe", "st")
                p.dvx(dv, dv.tensor_copy(sScT[0:8, :], bankp(5, 0, 8, 512)))
                p.wait(sy, "dv")
                p.inc(sy.dma_start(out=t.sctD[:, tsl], in_=sScT[0:8, :]), "st", 16)
            for s in range(4 if do_tm else 0):
                p.wait(pe, "dv")
                for c in range(8):
                    p.inc(pe.transpose(
                        bankp(6, 0, 128, 64, off=c * 64).bitcast(BF16)[:, 0:128],
                        sX1b[:, c * TT + s * 128: c * TT + (s + 1) * 128],
                        C.sEyeB[:, :]), "pe")
                p.wait(dv, "pe")
                p.dvx(dv, dv.tensor_copy(sTm[:, s * 1024:(s + 1) * 1024],
                                         bankp(6, 0, 128, 512).bitcast(BF16)[:, 0:1024]))
            if do_tm:
                p.wait(sy, "dv")
                p.inc(sy.dma_start(out=tm_r[:, 4 * tt: 4 * tt + 4, :],
                                   in_=sTm[:, :].rearrange("p (s m) -> p s m", s=4)), "st", 16)

        p.barrier([sy, gp, pe, dv, ac])


def emit_C(nc, p, C, ppool, t):
    NITER = 42
    CAP = 4096.0
    BIG = 60000.0

    with ExitStack() as st:
        sb = lambda name, shape, dt=F32: st.enter_context(nc.sbuf_tensor("C" + name, shape, dt))
        sW1 = sb("sW1", [128, 8 * 2048], BF16)
        sW3 = sb("sW3", [128, 8 * 2048], BF16)
        sW2 = sb("sW2", [128, 16 * 1024], BF16)
        sScR = sb("sScR", [8, 2048])
        sGrid = sb("sGrid", [128, 128])
        sLo = sb("sLo", [128, 1]); sHi = sb("sHi", [128, 1]); sMid = sb("sMid", [128, 1])
        sCnt = sb("sCnt", [128, 1]); sSel = sb("sSel", [128, 1])
        sT1 = sb("sT1", [128, 1]); sT2 = sb("sT2", [128, 1])
        sR11 = sb("sR11", [1, 1])
        sTrash = sb("sTrash", [128, 128], BF16)
        sEq = sb("sEq", [128, 128]); sRk = sb("sRk", [128, 128]); sRk2 = sb("sRk2", [128, 128])
        sMask = sb("sMask", [128, 128]); sGate = sb("sGate", [128, 128])
        sRank = sb("sRank", [128, 128]); sIdxF = sb("sIdxF", [128, 128])
        sIdxU = sb("sIdxU", [128, 128], U32)
        sStg = [sb(f"sStg{i}", [128, 1032], BF16) for i in range(4)]
        sMetaAll = sb("sMetaAll", [128, 3 * 128])
        sIotaRow = sb("sIotaRow", [1, 128])
        sXr4 = sb("sXr4", [128, 4 * 1032], BF16)
        sScf = sb("sScf", [128, 4])
        sTokF = sb("sTokF", [128, 4])
        sTokU = sb("sTokU", [128, 4], U32)
        sRt = sb("sRt", [128, 8 * 512], BF16)
        sGs = sb("sGs", [128, 512], BF16)
        sGs2 = sb("sGs2", [128, 512], BF16)
        sH1 = sb("sH1", [128, 16 * 512], BF16)
        sEoF = sb("sEoF", [128, 8 * 512], BF16)
        sEoT = sb("sEoT", [128, 4 * 1024], BF16)

        def bank(i, w=512, off=0):
            return ppool[:, i * 512 + off: i * 512 + off + w]

        def bankp(i, p0, prows, w, off=0):
            return ppool[p0:p0 + prows, i * 512 + off: i * 512 + off + w]

        sy, gp, pe, dv, ac = nc.sync, nc.gpsimd, nc.tensor, nc.vector, nc.scalar
        bcreg = gp.to_reg(4095)
        bctok = gp.to_reg(16383)

        p.inc(gp.dma_start(out=sW1[:, :].rearrange("p (c m) -> p c m", c=8),
                           in_=t.w1.rearrange("(c p) m -> p c m", p=128)), "gld", 16)
        p.inc(gp.dma_start(out=sW3[:, :].rearrange("p (c m) -> p c m", c=8),
                           in_=t.w3.rearrange("(c p) m -> p c m", p=128)), "gld", 16)
        p.inc(gp.dma_start(out=sW2[:, :].rearrange("p (c m) -> p c m", c=16),
                           in_=t.w2.rearrange("(c p) m -> p c m", p=128)), "gld", 16)

        p.wait(gp, "csc")
        p.inc(gp.indirect_dma_start(
            out=sScR[:, :], out_offset=None,
            in_=t.scAll[:, :],
            in_offset=bass.IndirectOffsetOnAxis(ap=C.sSoff[:, :], axis=0),
            bounds_check=gp.to_reg(63), oob_is_err=False), "sc", 16)
        p.wait(pe, "sc", "dv", "ac")
        for f in range(16):
            p.inc(pe.transpose(bankp(0, 0, 128, 8, off=f * 8),
                               sScR[0:8, f * 128:(f + 1) * 128], C.sEyeF[0:8, 0:8]), "pe")
        p.wait(dv, "pe")
        for f in range(16):
            p.dvx(dv, dv.tensor_copy(
                sGrid[:, :].rearrange("p (r f) -> p f r", r=8)[:, f, :],
                bankp(0, 0, 128, 8, off=f * 8)))

        p.dvx(dv, dv.memset(sLo[:, :], 0.0))
        p.dvx(dv, dv.memset(sHi[:, :], 2.0))
        for b4 in range(4):
            p.dvx(dv, dv.memset(sStg[b4][:, 1027:1032], 0.0))

        for it in range(NITER):
            p.dvx(dv, dv.tensor_scalar(sMid[:, :], sLo[:, :], sHi[:, :], 0.5,
                                       OP.add, OP.mult))
            p.dvx(dv, dv.tensor_scalar(sTrash[:, :], sGrid[:, :], sMid[:, :], 0.0,
                                       OP.is_ge, OP.add, accum_out=sCnt[:, :]))
            p.wait(pe, "dv")
            p.inc(pe.matmul(bankp(1, 0, 128, 1), C.sOnesG[:, :], sCnt[:, :],
                            start=True, stop=True), "pe")
            p.wait(dv, "pe")
            p.dvx(dv, dv.tensor_scalar(sSel[:, :], bankp(1, 0, 128, 1), CAP, None, OP.is_ge))
            p.dvx(dv, dv.tensor_tensor(sT1[:, :], sMid[:, :], sLo[:, :], OP.subtract))
            p.dvx(dv, dv.tensor_tensor(sT2[:, :], sHi[:, :], sMid[:, :], OP.subtract))
            p.dvx(dv, dv.tensor_tensor(sT1[:, :], sT1[:, :], sSel[:, :], OP.mult))
            p.dvx(dv, dv.tensor_tensor(sT2[:, :], sT2[:, :], sSel[:, :], OP.mult))
            p.dvx(dv, dv.tensor_tensor(sLo[:, :], sLo[:, :], sT1[:, :], OP.add))
            p.dvx(dv, dv.tensor_tensor(sHi[:, :], sMid[:, :], sT2[:, :], OP.add))
        p.dvx(dv, dv.tensor_scalar(sMask[:, :], sGrid[:, :], sLo[:, :], 0.0,
                                   OP.is_gt, OP.add, accum_out=sCnt[:, :]))
        p.wait(pe, "dv")
        p.inc(pe.matmul(bankp(0, 0, 1, 1), sCnt[:, :], C.sOnesCol[:, :],
                        start=True, stop=True), "pe")
        p.wait(dv, "pe")
        p.dvx(dv, dv.tensor_scalar(sR11[:, :], bankp(0, 0, 1, 1), -1.0, CAP,
                                   OP.mult, OP.add))
        p.wait(pe, "dv")
        p.inc(pe.matmul(bankp(1, 0, 128, 1), C.sOnes[:, :], sR11[:, :],
                        start=True, stop=True), "pe")
        p.wait(dv, "pe")
        p.dvx(dv, dv.tensor_copy(sT1[:, :], bankp(1, 0, 128, 1)))
        p.dvx(dv, dv.tensor_scalar(sEq[:, :], sGrid[:, :], sLo[:, :], 0.0,
                                   OP.is_equal, OP.add, accum_out=sCnt[:, :]))
        p.dvx(dv, dv.tensor_tensor_scan(sRk[:, :], sEq[:, :], sEq[:, :], 0.0,
                                        OP.add, OP.bypass))
        p.dvx(dv, dv.tensor_tensor(sRk[:, :], sRk[:, :], sEq[:, :], OP.subtract))
        p.wait(pe, "dv")
        p.inc(pe.matmul(bankp(2, 0, 128, 1), C.sTriu1[:, :], sCnt[:, :],
                        start=True, stop=True), "pe")
        p.wait(dv, "pe")
        p.dvx(dv, dv.tensor_scalar(sRk2[:, :], sRk[:, :], bankp(2, 0, 128, 1), None, OP.add))
        p.dvx(dv, dv.tensor_scalar(sRk2[:, :], sRk2[:, :], sT1[:, :], None, OP.is_lt))
        p.dvx(dv, dv.tensor_tensor(sRk2[:, :], sRk2[:, :], sEq[:, :], OP.mult))
        p.dvx(dv, dv.tensor_tensor(sMask[:, :], sMask[:, :], sRk2[:, :], OP.add))
        p.dvx(dv, dv.tensor_tensor(sGate[:, :], sMask[:, :], sGrid[:, :], OP.mult))
        p.dvx(dv, dv.tensor_scalar(sTrash[:, :], sMask[:, :], 1.0, 0.0,
                                   OP.mult, OP.add, accum_out=sCnt[:, :]))
        p.dvx(dv, dv.tensor_tensor_scan(sRk[:, :], sMask[:, :], sMask[:, :], 0.0,
                                        OP.add, OP.bypass))
        p.dvx(dv, dv.tensor_tensor(sRk[:, :], sRk[:, :], sMask[:, :], OP.subtract))
        p.wait(pe, "dv")
        p.inc(pe.matmul(bankp(2, 0, 128, 1), C.sTriu1[:, :], sCnt[:, :],
                        start=True, stop=True), "pe")
        p.wait(dv, "pe")
        p.dvx(dv, dv.tensor_scalar(sRank[:, :], sRk[:, :], bankp(2, 0, 128, 1), None, OP.add))
        p.dvx(dv, dv.tensor_tensor(sIdxF[:, :], sRank[:, :], sMask[:, :], OP.mult))
        p.dvx(dv, dv.tensor_scalar(sRk2[:, :], sMask[:, :], -BIG, BIG, OP.mult, OP.add))
        p.dvx(dv, dv.tensor_tensor(sIdxF[:, :], sIdxF[:, :], sRk2[:, :], OP.add))
        p.dvx(dv, dv.tensor_copy(sIdxU[:, :], sIdxF[:, :]))

        p.wait(sy, "dv")
        p.inc(sy.dma_start(out=sIotaRow[:, :], in_=t.rhiRow[:, :]), "ld", 16)
        p.wait(pe, "dv", "ld")
        p.inc(pe.matmul(bankp(3, 0, 128, 128, off=128), C.sOnes[:, :], sIotaRow[:, :],
                        start=True, stop=True), "pe")
        p.wait(dv, "pe")
        m3 = sMetaAll[:, :].rearrange("p (i k) -> p k i", k=3)
        rowb = bankp(3, 0, 128, 128, off=128)
        p.dvx(dv, dv.tensor_copy(m3[:, 0:1, :], sGate[:, :].unsqueeze(1)))
        p.dvx(dv, dv.tensor_copy(m3[:, 1:2, :], rowb.unsqueeze(1)))
        p.dvx(dv, dv.tensor_scalar(m3[:, 2:3, :], rowb.unsqueeze(1), 0.0,
                                   C.sIota[:, :], OP.mult, OP.add))

        stg_busy = [0, 0, 0, 0]
        p.wait(sy, "cx1")
        p.wait(ac, "cx1")
        for i in range(128):
            par = i % 4
            stg = sStg[par]
            q_eng, q_nm = (sy, "ld") if i % 2 == 0 else (ac, "zz")
            if i >= 4:
                p.wait_at(q_eng, "sc", stg_busy[par])
            p.inc(q_eng.dma_start(out=stg[:, 0:1024],
                                  in_=t.x1all[i * 128:(i + 1) * 128, :]), q_nm, 16)
            ldmark = p.at(q_nm)
            p.wait_at(dv, q_nm, ldmark)
            if i >= 4:
                p.wait_at(dv, "sc", stg_busy[par])
            p.dvx(dv, dv.tensor_copy(stg[:, 1024:1027], sMetaAll[:, 3 * i: 3 * i + 3]))
            p.wait(gp, "dv")
            p.inc(gp.indirect_dma_start(
                out=t.xbuf[:, :], out_offset=bass.IndirectOffsetOnAxis(ap=sIdxU[:, i:i + 1], axis=0),
                in_=stg[:, 0:1032], in_offset=None,
                bounds_check=bcreg, oob_is_err=False), "sc", 16)
            stg_busy[par] = p.at("sc")
        p.wait(gp, "sc")
        p.wait(sy, "sc")
        p.wait(ac, "sc")
        p.wait(gp, "zz")

        p.wait(dv, "gld")
        trfmk = [0] * 8
        ldmkC = [0] * 8
        for til in range(8):
            if til == 0:
                p.wait(sy, "pe", "dv", "ac")
            else:
                p.wait_at(sy, "pe", trfmk[til - 1])
            p.inc(sy.dma_start(out=sXr4[:, :].rearrange("p (j m) -> p j m", j=4),
                               in_=t.xbuf.rearrange("(t p) m -> p t m", p=128)[:, til * 4:(til + 1) * 4, :]),
                  "ld", 16)
            ldmkC[til] = p.at("ld")
            p.wait_at(dv, "ld", ldmkC[til])
            p.wait(dv, "sc")
            for j in range(4):
                p.dvx(dv, dv.tensor_copy(sScf[:, j:j + 1],
                                         sXr4[:, j * 1032 + 1024: j * 1032 + 1025]))
                p.dvx(dv, dv.scalar_tensor_tensor(sTokF[:, j:j + 1],
                                                  sXr4[:, j * 1032 + 1025: j * 1032 + 1026], 128.0,
                                                  sXr4[:, j * 1032 + 1026: j * 1032 + 1027],
                                                  OP.mult, OP.add))
            p.dvx(dv, dv.tensor_copy(sTokU[:, :], sTokF[:, :]))
            for j in range(4):
                p.dvx(dv, dv.tensor_scalar(sXr4[:, j * 1032: j * 1032 + 1024],
                                           sXr4[:, j * 1032: j * 1032 + 1024],
                                           sScf[:, j:j + 1],
                                           None, OP.mult))
            for j in range(4):
                tb = 6 + (j % 2)
                p.wait(pe, "dv")
                for cchunk in range(8):
                    p.inc(pe.transpose(
                        bankp(tb, 0, 128, 64, off=cchunk * 64).bitcast(BF16)[:, 0:128],
                        sXr4[:, j * 1032 + cchunk * 128: j * 1032 + (cchunk + 1) * 128],
                        C.sEyeB[:, :]), "pe")
                p.wait(dv, "pe")
                p.dvx(dv, dv.tensor_copy(
                    sRt[:, :].rearrange("p (c t) -> p c t", c=8)[:, :, j * 128:(j + 1) * 128],
                    bankp(tb, 0, 128, 512).bitcast(BF16)[:, 0:1024].rearrange(
                        "p (c t) -> p c t", c=8)))
            trfmk[til] = p.at("pe")
            consumed_dv = [0] * 16
            consumed_ac = [0] * 16
            for m in range(16):
                gb, ub = (bank(0), bank(1)) if m % 2 == 0 else (bank(4), bank(5))
                if m >= 2:
                    p.wait_at(pe, "dv", consumed_dv[m - 2])
                    p.wait_at(pe, "ac", consumed_ac[m - 2])
                elif m == 0:
                    p.wait(pe, "dv", "ac")
                for k in range(8):
                    p.inc(pe.matmul(gb, sW1[:, k * 2048 + m * 128: k * 2048 + (m + 1) * 128],
                                    sRt[:, k * 512:(k + 1) * 512],
                                    start=(k == 0), stop=(k == 7)), "pe")
                for k in range(8):
                    p.inc(pe.matmul(ub, sW3[:, k * 2048 + m * 128: k * 2048 + (m + 1) * 128],
                                    sRt[:, k * 512:(k + 1) * 512],
                                    start=(k == 0), stop=(k == 7)), "pe")
                p.wait(ac, "pe")
                sgs = sGs if m % 2 == 0 else sGs2
                p.acx(ac, ac.activation(out=sgs[:, :], in_=gb, func=AF.Silu))
                consumed_ac[m] = p.at("ac")
                p.wait(dv, "ac")
                p.dvx(dv, dv.tensor_tensor(sH1[:, m * 512:(m + 1) * 512], sgs[:, :],
                                           ub, OP.mult))
                consumed_dv[m] = p.at("dv")
            dcon = [0] * 8
            for dm in range(8):
                db = bank(2) if dm % 2 == 0 else bank(3)
                if dm >= 2:
                    p.wait_at(pe, "dv", dcon[dm - 2])
                else:
                    p.wait(pe, "dv")
                for k in range(16):
                    p.inc(pe.matmul(db, sW2[:, k * 1024 + dm * 128: k * 1024 + (dm + 1) * 128],
                                    sH1[:, k * 512:(k + 1) * 512],
                                    start=(k == 0), stop=(k == 15)), "pe")
                p.wait(dv, "pe")
                p.dvx(dv, dv.tensor_copy(sEoF[:, dm * 512:(dm + 1) * 512], db))
                dcon[dm] = p.at("dv")
            for j in range(4):
                tb = 6 + (j % 2)
                p.wait(pe, "dv")
                for dm in range(8):
                    p.inc(pe.transpose(
                        bankp(tb, 0, 128, 64, off=dm * 64).bitcast(BF16)[:, 0:128],
                        sEoF[:, dm * 512 + j * 128: dm * 512 + (j + 1) * 128],
                        C.sEyeB[:, :]), "pe")
                p.wait(dv, "pe", "sc")
                p.dvx(dv, dv.tensor_copy(
                    sEoT[:, j * 1024:(j + 1) * 1024],
                    bankp(tb, 0, 128, 512).bitcast(BF16)[:, 0:1024]))
            p.wait(gp, "dv")
            for j in range(4):
                p.inc(gp.indirect_dma_start(
                    out=t.moeD[:, :],
                    out_offset=bass.IndirectOffsetOnAxis(ap=sTokU[:, j:j + 1], axis=0),
                    in_=sEoT[:, j * 1024:(j + 1) * 1024], in_offset=None,
                    bounds_check=bctok, oob_is_err=False), "sc", 16)

        p.barrier([sy, gp, pe, dv, ac])


def emit_D(nc, p, C, ppool, t):
    TT = 512
    NTILE = TPC // TT

    x1_r = t.x1fD.rearrange("(c p) t -> p c t", p=128)
    mo_r = [t.moeRS.rearrange("(t p) m -> p t m", p=128),
            t.moeRS2.rearrange("(t p) m -> p t m", p=128)]
    x2f_r = t.x2fD.rearrange("(c p) t -> p c t", p=128)
    x2b_r = t.x2bD.rearrange("(c p) t -> p c t", p=128)

    with ExitStack() as st:
        sb = lambda name, shape, dt=F32: st.enter_context(nc.sbuf_tensor("D" + name, shape, dt))
        sAcc = sb("sAcc", [128, 8 * TT])
        sMoT = sb("sMoT", [128, 4 * 1024], BF16)
        sRawD = [sb(f"sRaw{i}", [128, 8 * TT]) for i in range(2)]
        sSq = sb("sSq", [128, TT])
        sR1 = sb("sR1", [1, TT]); sR2 = sb("sR2", [1, TT]); sR3 = sb("sR3", [1, TT])
        sX2 = sb("sX2", [128, 8 * TT])
        sX2b = sb("sX2b", [128, 8 * TT], BF16)
        sLnA = sb("sLnA", [128, TT])
        sLnB = sb("sLnB", [128, TT])
        sEpsR = sb("sEpsR", [1, 1])

        def bank(i, w=512, off=0):
            return ppool[:, i * 512 + off: i * 512 + off + w]

        def bankp(i, p0, prows, w, off=0):
            return ppool[p0:p0 + prows, i * 512 + off: i * 512 + off + w]

        sy, gp, pe, dv, ac = nc.sync, nc.gpsimd, nc.tensor, nc.vector, nc.scalar

        p.dvx(dv, dv.memset(sEpsR[:, :], LNEPS))

        trD = [0] * NTILE
        rawdv = [0] * NTILE
        rawac = [0] * NTILE
        prevcp = [0, 0]
        for tt in range(NTILE):
            tsl = slice(tt * TT, (tt + 1) * TT)
            raw = sRawD[tt % 2]
            if tt == 0:
                p.wait(sy, "pe", "dv", "ac")
            else:
                p.wait_at(sy, "pe", trD[tt - 1])
            if tt >= 2:
                p.wait_at(sy, "dv", rawdv[tt - 2])
                p.wait_at(sy, "ac", rawac[tt - 2])
            p.wait(sy, "cmo" if tt < 2 else "cm2")
            p.inc(sy.dma_start(out=raw[:, :].rearrange("p (c t) -> p c t", c=8),
                               in_=x1_r[:, :, tsl]), "ld", 16)
            p.inc(sy.dma_start(out=sMoT[:, :].rearrange("p (j m) -> p j m", j=4),
                               in_=mo_r[tt // 2][:, (tt % 2) * 4:(tt % 2) * 4 + 4, :]), "ld", 16)
            ldm = p.at("ld")
            cpmk = [0] * 4
            for grp in range(4):
                tb = grp % 2
                if grp == 0:
                    p.wait_at(pe, "ld", ldm)
                if grp < 2:
                    p.wait_at(pe, "dv", prevcp[grp])
                else:
                    p.wait_at(pe, "dv", cpmk[grp - 2])
                for c in range(8):
                    p.inc(pe.transpose(
                        bankp(tb, 0, 128, 64, off=c * 64).bitcast(BF16)[:, 0:128],
                        sMoT[:, grp * 1024 + c * 128: grp * 1024 + (c + 1) * 128],
                        C.sEyeB[:, :]), "pe")
                p.wait(dv, "pe")
                p.dvx(dv, dv.tensor_copy(
                    sAcc[:, :].rearrange("p (c t) -> p c t", c=8)[:, :, grp * 128:(grp + 1) * 128],
                    bankp(tb, 0, 128, 512).bitcast(BF16)[:, 0:1024].rearrange(
                        "p (c t) -> p c t", c=8)))
                cpmk[grp] = p.at("dv")
            trD[tt] = p.at("pe")
            prevcp = [cpmk[2], cpmk[3]]
            p.wait_at(dv, "ld", ldm)
            p.dvx(dv, dv.tensor_tensor(raw[:, :], raw[:, :], sAcc[:, :], OP.add))
            p.wait(pe, "dv")
            for c in range(8):
                p.inc(pe.matmul(bankp(4, 0, 1, TT), C.sOnesCol[:, :],
                                raw[:, c * TT:(c + 1) * TT],
                                start=(c == 0), stop=(c == 7)), "pe")
            p.wait(ac, "dv", "st")
            p.acx(ac, ac.activation(out=sX2[:, :], in_=raw[:, :], func=AF.Square))
            p.wait(pe, "ac")
            for c in range(8):
                p.inc(pe.matmul(bankp(5, 0, 1, TT), C.sOnesCol[:, :],
                                sX2[:, c * TT:(c + 1) * TT],
                                start=(c == 0), stop=(c == 7)), "pe")
            rawac[tt] = p.at("ac")
            p.wait(dv, "pe")
            p.dvx(dv, dv.tensor_scalar(sR1[:, :], bankp(4, 0, 1, TT), 1.0 / D_MODEL,
                                       None, OP.mult))
            p.dvx(dv, dv.tensor_scalar(sR2[:, :], bankp(5, 0, 1, TT), 1.0 / D_MODEL,
                                       None, OP.mult))
            p.dvx(dv, dv.tensor_tensor(sR3[:, :], sR1[:, :], sR1[:, :], OP.mult))
            p.dvx(dv, dv.tensor_tensor(sR2[:, :], sR2[:, :], sR3[:, :], OP.subtract))
            p.wait(ac, "dv")
            p.acx(ac, ac.activation(out=sR3[:, :], in_=sR2[:, :], func=AF.Ln, bias=sEpsR[:, :]))
            p.acx(ac, ac.activation(out=sR2[:, :], in_=sR3[:, :], func=AF.Exp, scale=-0.5))
            p.wait(dv, "ac")
            p.dvx(dv, dv.scalar_tensor_tensor(sR3[:, :], sR1[:, :], -1.0, sR2[:, :],
                                              OP.mult, OP.mult))
            p.wait(pe, "dv")
            p.inc(pe.matmul(bank(6, TT), C.sOnes[:, :], sR2[:, :], start=True, stop=True), "pe")
            p.inc(pe.matmul(bank(7, TT), C.sOnes[:, :], sR3[:, :], start=True, stop=True), "pe")
            p.wait(dv, "pe", "st")
            p.dvx(dv, dv.tensor_copy(sLnA[:, :], bank(6, TT)))
            p.dvx(dv, dv.tensor_copy(sLnB[:, :], bank(7, TT)))
            rb_ = sLnA[:, :].unsqueeze(1).broadcast_to([128, 8, TT])
            mb_ = sLnB[:, :].unsqueeze(1).broadcast_to([128, 8, TT])
            p.dvx(dv, dv.tensor_tensor(sX2[:, :].rearrange("p (c t) -> p c t", c=8),
                                       raw[:, :].rearrange("p (c t) -> p c t", c=8),
                                       rb_, OP.mult))
            p.dvx(dv, dv.tensor_tensor(sX2[:, :].rearrange("p (c t) -> p c t", c=8),
                                       sX2[:, :].rearrange("p (c t) -> p c t", c=8),
                                       mb_, OP.add))
            p.dvx(dv, dv.tensor_copy(sX2b[:, :], sX2[:, :]))
            rawdv[tt] = p.at("dv")
            p.wait(sy, "dv")
            p.inc(sy.dma_start(out=x2f_r[:, :, tsl],
                               in_=sX2[:, :].rearrange("p (c t) -> p c t", c=8)), "st", 16)
            p.inc(sy.dma_start(out=x2b_r[:, :, tsl],
                               in_=sX2b[:, :].rearrange("p (c t) -> p c t", c=8)), "st", 16)

        p.barrier([sy, gp, pe, dv, ac])



def emit_E(nc, p, C, ppool, t):
    T = TPC
    NH = 16

    x2_r = t.x2bD.rearrange("(c p) t -> p c t", p=128)
    phiq_r = t.phiqD.rearrange("(g p) m -> p g m", p=128)

    with ExitStack() as st:
        sb = lambda name, shape, dt=F32: st.enter_context(nc.sbuf_tensor("E" + name, shape, dt))
        sX2 = sb("sX2", [128, 8 * T], BF16)
        sVt = sb("sVt", [128, 16 * 1024], BF16)
        sWq = sb("sWq", [128, 8 * 1024], BF16)
        sWk = sb("sWk", [128, 8 * 1024], BF16)
        sProj = sb("sProj", [128, 8 * 512], BF16)
        sOnes2 = sb("sOnes2", [128, 2])
        sQh = sb("sQh", [128, T], BF16)
        sKh = sb("sKh", [128, T], BF16)
        sQsq = sb("sQsq", [128, T])
        sPhiQh2 = sb("sPhiQh2", [128, 16 * 512], BF16)
        sMx4 = [sb(f"sMx4_{i}", [128, 4]) for i in range(2)]
        sB4 = [sb(f"sB4_{i}", [128, 4]) for i in range(2)]
        sXwS = [sb(f"sXwS_{i}", [128, 1024]) for i in range(2)]
        sPhi4 = [sb(f"sPhi4_{i}", [128, 1024], BF16) for i in range(2)]
        sKVo = sb("sKVo", [128, 130])
        sOnesColB = sb("sOnesColB", [128, 1], BF16)

        def bank(i, w=512, off=0):
            return ppool[:, i * 512 + off: i * 512 + off + w]

        def bankp(i, p0, prows, w, off=0):
            return ppool[p0:p0 + prows, i * 512 + off: i * 512 + off + w]

        sy, gp, pe, dv, ac = nc.sync, nc.gpsimd, nc.tensor, nc.vector, nc.scalar

        p.inc(sy.dma_start(out=sX2[:, :].rearrange("p (c t) -> p c t", c=8),
                           in_=x2_r[:, :, :]), "ld", 16)
        p.inc(sy.dma_start(out=sOnes2[:, :], in_=t.ones2[:, :]), "ld", 16)
        p.inc(gp.dma_start(out=sProj[:, :], in_=t.projT[:, :]), "gld", 16)
        p.inc(gp.dma_start(out=sWq[:, :].rearrange("p (c m) -> p c m", c=8),
                           in_=t.wvT.rearrange("(c p) m -> p c m", p=128)), "gld", 16)
        p.wait(dv, "ld")
        p.dvx(dv, dv.memset(sOnesColB[:, :], 1.0))
        p.wait(pe, "ld", "gld", "dv")
        vdv = [0] * 16
        for tg in range(16):
            vb = 0 if tg % 2 == 0 else 2
            if tg >= 2:
                p.wait_at(pe, "dv", vdv[tg - 2])
            for n in range(2):
                for k in range(8):
                    p.inc(pe.matmul(bank(vb + n), sX2[:, k * T + tg * 128: k * T + (tg + 1) * 128],
                                    sWq[:, k * 1024 + n * 512:(k * 1024) + (n + 1) * 512],
                                    start=(k == 0), stop=(k == 7)), "pe")
            p.wait(dv, "pe")
            for n in range(2):
                p.dvx(dv, dv.tensor_copy(sVt[:, tg * 1024 + n * 512: tg * 1024 + (n + 1) * 512],
                                         bank(vb + n)))
            vdv[tg] = p.at("dv")
        p.wait(gp, "dv")
        p.inc(gp.dma_start(out=sWq[:, :].rearrange("p (c m) -> p c m", c=8),
                           in_=t.wqT.rearrange("(c p) m -> p c m", p=128)), "gld", 16)
        p.inc(gp.dma_start(out=sWk[:, :].rearrange("p (c m) -> p c m", c=8),
                           in_=t.wkT.rearrange("(c p) m -> p c m", p=128)), "gld", 16)
        p.wait(pe, "gld")

        for hp in range(NH // 2):
            prmk = {}
            p.wait(pe, "dv", "ac")
            for bi, (buf, w) in enumerate(((sQh, sWq), (sKh, sWk))):
                for n in range(4):
                    g8 = bi * 4 + n
                    pb = 3 if g8 % 2 == 0 else 4
                    if g8 >= 2:
                        p.wait_at(pe, "dv", prmk[g8 - 2])
                    for k in range(8):
                        p.inc(pe.matmul(bankp(pb, 0, 128, 512),
                                        w[:, k * 1024 + hp * 128: k * 1024 + (hp + 1) * 128],
                                        sX2[:, k * T + n * 512: k * T + (n + 1) * 512],
                                        start=(k == 0), stop=(k == 7)), "pe")
                    p.wait(dv, "pe")
                    p.dvx(dv, dv.tensor_scalar(buf[:, n * 512:(n + 1) * 512],
                                               bankp(pb, 0, 128, 512), 0.125, None, OP.mult))
                    prmk[g8] = p.at("dv")
            for side in range(2):
                qb = sQh if side == 0 else sKh
                p.wait(ac, "dv", "pe")
                p.acx(ac, ac.activation(out=sQsq[:, :], in_=qb[:, :], func=AF.Square))
                if side == 0:
                    p.wait(dv, "st")
                pemk = [0] * 8
                dvmk = [0] * 8
                acmk = [0] * 8
                for G in range(8):
                    par = G % 2
                    xwb = [3, 5][par]
                    if G == 0:
                        p.wait(pe, "ac")
                    if G >= 2:
                        p.wait_at(pe, "dv", dvmk[G - 2])
                    for j in range(2):
                        tg = G * 2 + j
                        p.inc(pe.matmul(bankp(xwb + j, 0, 128, 512),
                                        qb[:, tg * 128:(tg + 1) * 128],
                                        sProj[:, hp * 512:(hp + 1) * 512],
                                        start=True, stop=True), "pe")
                        p.inc(pe.matmul(bankp(2, 0, 128, 2, off=side * 64 + G * 4 + j * 2),
                                        sQsq[:, tg * 128:(tg + 1) * 128],
                                        sOnes2[:, :], start=True, stop=True), "pe")
                    pemk[G] = p.at("pe")
                    p.wait_at(dv, "pe", pemk[G])
                    if G >= 2:
                        p.wait_at(dv, "ac", acmk[G - 2])
                    xwp = ppool[:, xwb * 512: xwb * 512 + 1024]
                    p.dvx(dv, dv.tensor_reduce(sMx4[par][:, :],
                                               xwp.rearrange("p (j m) -> p j m", j=4),
                                               axis=mybir.AxisListType.X, op=OP.max))
                    p.inc(dv.scalar_tensor_tensor(sB4[par][:, :],
                                                  bankp(2, 0, 128, 4, off=side * 64 + G * 4), 0.5,
                                                  sMx4[par][:, :], OP.mult, OP.add), "dv")
                    dv.drain()
                    p.dvx(dv, dv.tensor_scalar(sB4[par][:, :], sB4[par][:, :], LNC16, -1.0,
                                               OP.add, OP.mult))
                    b4e = sB4[par][:, :].unsqueeze(2).broadcast_to([128, 4, 256])
                    p.dvx(dv, dv.tensor_tensor(sXwS[par][:, :].rearrange("p (j m) -> p j m", j=4),
                                               xwp.rearrange("p (j m) -> p j m", j=4),
                                               b4e, OP.add))
                    dvmk[G] = p.at("dv")
                    p.wait_at(ac, "dv", dvmk[G])
                    p.acx(ac, ac.activation(out=sPhi4[par][:, :], in_=sXwS[par][:, :], func=AF.Exp))
                    acmk[G] = p.at("ac")
                    p.wait_at(dv, "ac", acmk[G])
                    if side == 0:
                        p.dvx(dv, dv.tensor_scalar(
                            sPhiQh2[:, G * 1024:(G + 1) * 1024],
                            sPhi4[par][:, :], EPS16, None, OP.add))
                        dvmk[G] = p.at("dv")
                    else:
                        p.dvx(dv, dv.tensor_scalar(sPhi4[par][:, :], sPhi4[par][:, :], EPS16,
                                                   None, OP.add))
                        dvmk[G] = p.at("dv")
                        p.wait_at(pe, "dv", dvmk[G])
                        for j in range(2):
                            tg = G * 2 + j
                            for h2 in range(2):
                                hof = (hp * 2 + h2) * 64
                                for mc in range(2):
                                    phs = sPhi4[par][:, j * 512 + h2 * 256 + mc * 128:
                                                     j * 512 + h2 * 256 + (mc + 1) * 128]
                                    p.inc(pe.matmul(bankp(0, 0, 128, 64, off=h2 * 128 + mc * 64),
                                                    phs,
                                                    sVt[:, tg * 1024 + hof: tg * 1024 + hof + 64],
                                                    start=(tg == 0), stop=(tg == 15)), "pe")
                                    p.inc(pe.matmul(bankp(1, 0, 128, 1, off=h2 * 2 + mc),
                                                    phs, sOnesColB[:, :],
                                                    start=(tg == 0), stop=(tg == 15)), "pe")
                if side == 0:
                    p.wait(sy, "dv", "st")
                    p.inc(sy.dma_start(out=phiq_r[:, :, hp * 512:(hp + 1) * 512],
                                       in_=sPhiQh2[:, :].rearrange("p (g m) -> p g m", g=16)),
                          "st", 16)
            for h2 in range(2):
                h = hp * 2 + h2
                p.wait(dv, "pe", "st")
                p.dvx(dv, dv.tensor_copy(sKVo[:, 0:64], bankp(0, 0, 128, 64, off=h2 * 128)))
                p.dvx(dv, dv.tensor_copy(sKVo[:, 64:65], bankp(1, 0, 128, 1, off=h2 * 2)))
                p.dvx(dv, dv.tensor_copy(sKVo[:, 65:129], bankp(0, 0, 128, 64, off=h2 * 128 + 64)))
                p.dvx(dv, dv.tensor_copy(sKVo[:, 129:130], bankp(1, 0, 128, 1, off=h2 * 2 + 1)))
                p.wait(sy, "dv")
                p.inc(sy.dma_start(out=t.kvpD[h].rearrange("k p m -> p k m"),
                                   in_=sKVo[:, :].rearrange("p (k m) -> p k m", k=2)), "st", 16)

        p.barrier([sy, gp, pe, dv, ac])



def emit_F(nc, p, C, ppool, t):
    T = TPC
    NH = 16
    TT = 512

    x2_r = t.x2fD.rearrange("(c p) t -> p c t", p=128)
    phiq_r = t.phiqD.rearrange("(g p) m -> p g m", p=128)
    x3f_r = t.x3fD.rearrange("(c p) t -> p c t", p=128)
    x3b_r = t.x3bD.rearrange("(c p) t -> p c t", p=128)

    with ExitStack() as st:
        sb = lambda name, shape, dt=F32: st.enter_context(nc.sbuf_tensor("F" + name, shape, dt))
        sKV = sb("sKV", [128, NH * 2 * 65])
        sKVb = sb("sKVb", [128, NH * 2 * 65], BF16)
        sKsT = sb("sKsT", [32, 128])
        sKsCols = sb("sKsCols", [128, 32])
        sSelB = sb("sSelB", [32, 32 * 128])
        sKbAll = sb("sKbAll", [128, NH * 256], BF16)
        sPhiF = [sb(f"sPhiF{i}", [128, 1024], BF16) for i in range(2)]
        sPq4 = [sb(f"sPq4{i}", [128, 1024], BF16) for i in range(2)]
        sPqm4 = [sb(f"sPqm4{i}", [128, 1024], BF16) for i in range(2)]
        sT4 = [sb(f"sT4{i}", [128, 1024]) for i in range(2)]
        sDen4 = [sb(f"sDen4{i}", [128, 4]) for i in range(2)]
        sMerged = sb("sMerged", [128, 8 * T], BF16)
        sWo = sb("sWo", [128, 8 * 1024], BF16)
        sBo = sb("sBo", [128, 8])
        sXo = sb("sXo", [128, 8 * TT])
        sRaw = sb("sRaw", [128, 8 * TT])
        sSq = sb("sSq", [128, TT])
        sR1 = sb("sR1", [1, TT])
        sR2 = sb("sR2", [1, TT])
        sR3 = sb("sR3", [1, TT])
        sX3 = sb("sX3", [128, 8 * TT])
        sX3b = sb("sX3b", [128, 8 * TT], BF16)
        sLnA = sb("sLnA", [128, TT])
        sLnB = sb("sLnB", [128, TT])
        sEpsR = sb("sEpsR", [1, 1])

        def bank(i, w=512, off=0):
            return ppool[:, i * 512 + off: i * 512 + off + w]

        def bankp(i, p0, prows, w, off=0):
            return ppool[p0:p0 + prows, i * 512 + off: i * 512 + off + w]

        sy, gp, pe, dv, ac = nc.sync, nc.gpsimd, nc.tensor, nc.vector, nc.scalar

        p.wait(sy, "ckv")
        p.inc(sy.dma_start(out=sBo[:, :], in_=t.bo[:, :]), "ld", 16)
        p.inc(sy.dma_start(out=sSelB[:, :], in_=t.selB[:, :]), "ld", 16)
        p.inc(sy.dma_start(out=sKV[:, :].rearrange("p (h k m) -> p h k m", h=NH, k=2),
                           in_=t.kvpS.rearrange("h k p m -> p h k m")), "ld", 16)
        p.inc(gp.dma_start(out=sWo[:, :].rearrange("p (c m) -> p c m", c=8),
                           in_=t.woT.rearrange("(c p) m -> p c m", p=128)), "gld", 16)
        p.wait(dv, "ld")
        p.dvx(dv, dv.memset(sEpsR[:, :], LNEPS))
        p.dvx(dv, dv.tensor_copy(sKVb[:, :], sKV[:, :]))

        p.dvx(dv, dv.tensor_copy(sKsCols[:, :], sKV[:, 64:2080:65]))
        p.wait(pe, "dv", "ac")
        p.inc(pe.transpose(bankp(7, 0, 32, 128), sKsCols[:, :], C.sEyeF[:, :]), "pe")
        p.wait(dv, "pe")
        p.dvx(dv, dv.tensor_copy(sKsT[:, :], bankp(7, 0, 32, 128)))
        kbmk = [0] * 4
        for r in range(4):
            bks = [5, 6] if r % 2 == 0 else [3, 4]
            if r >= 2:
                p.wait_at(pe, "dv", kbmk[r - 2])
            elif r == 0:
                p.wait(pe, "dv")
            for jj in range(8):
                j = r * 8 + jj
                bkt = bks[jj // 4]
                off = (jj % 4) * 128
                p.inc(pe.matmul(bankp(bkt, 0, 128, 128, off=off),
                                sSelB[:, j * 128:(j + 1) * 128], sKsT[:, :],
                                start=True, stop=True), "pe")
            p.wait(dv, "pe")
            p.dvx(dv, dv.tensor_copy(sKbAll[:, r * 1024:(r + 1) * 1024],
                                     ppool[:, bks[0] * 512: bks[0] * 512 + 1024]))
            kbmk[r] = p.at("dv")
        p.wait(gp, "dv")

        ldmk = [0] * 64
        gpmk = [0] * 64
        dv1mk = [0] * 64
        pe1mk = [0] * 64
        dv2mk = [0] * 64
        pe2mk = [0] * 64
        dv3mk = [0] * 64
        for h in range(NH):
            hb = h * 130
            rows = slice((h % 2) * 64, (h % 2) * 64 + 64)
            for g in range(4):
                idx = h * 4 + g
                par = idx % 2
                trb = 5 if par == 0 else 3
                nmb = 7 if par == 0 else 6
                if idx >= 2:
                    p.wait_at(sy, "dv", dv1mk[idx - 2])
                p.inc(sy.dma_start(out=sPhiF[par][:, :].rearrange("p (j m) -> p j m", j=4),
                                   in_=phiq_r[:, g * 4:(g + 1) * 4, h * 256:(h + 1) * 256]),
                      "ld", 16)
                ldmk[idx] = p.at("ld")
                p.wait_at(gp, "ld", ldmk[idx])
                if idx >= 2:
                    p.wait_at(gp, "dv", dv1mk[idx - 2])
                kb_e = sKbAll[:, h * 256:(h + 1) * 256].unsqueeze(1).broadcast_to([128, 4, 256])
                p.gpx(gp, gp.tensor_tensor(sT4[par][:, :].rearrange("p (j m) -> p j m", j=4),
                                           sPhiF[par][:, :].rearrange("p (j m) -> p j m", j=4),
                                           kb_e, OP.mult))
                gpmk[idx] = p.at("gp")
                p.wait_at(dv, "gp", gpmk[idx])
                if idx >= 2:
                    p.wait_at(dv, "pe", pe1mk[idx - 2])
                p.dvx(dv, dv.tensor_reduce(sDen4[par][:, :],
                                           sT4[par][:, :].rearrange("p (j m) -> p j m", j=4),
                                           axis=mybir.AxisListType.X, op=OP.add))
                p.dvx(dv, dv.tensor_scalar(sDen4[par][:, :], sDen4[par][:, :], 1e-6, None, OP.add))
                p.dvx(dv, dv.reciprocal(sDen4[par][:, :], sDen4[par][:, :]))
                rd_e = sDen4[par][:, :].unsqueeze(2).broadcast_to([128, 4, 256])
                p.dvx(dv, dv.tensor_tensor(sPq4[par][:, :].rearrange("p (j m) -> p j m", j=4),
                                           sPhiF[par][:, :].rearrange("p (j m) -> p j m", j=4),
                                           rd_e, OP.mult))
                dv1mk[idx] = p.at("dv")
                p.wait_at(pe, "dv", dv1mk[idx])
                if idx >= 2:
                    p.wait_at(pe, "dv", dv2mk[idx - 2])
                for q in range(8):
                    p.inc(pe.transpose(
                        bankp(trb, 0, 128, 64, off=q * 64).bitcast(BF16)[:, 0:128],
                        sPq4[par][:, q * 128:(q + 1) * 128], C.sEyeB[:, :]), "pe")
                pe1mk[idx] = p.at("pe")
                p.wait_at(dv, "pe", pe1mk[idx])
                p.dvx(dv, dv.tensor_copy(sPqm4[par][:, :],
                                         bankp(trb, 0, 128, 512).bitcast(BF16)[:, 0:1024]))
                dv2mk[idx] = p.at("dv")
                p.wait_at(pe, "dv", dv2mk[idx])
                if idx >= 2:
                    p.wait_at(pe, "dv", dv3mk[idx - 2])
                for j in range(4):
                    for mc in range(2):
                        p.inc(pe.matmul(
                            ppool[rows, nmb * 512 + j * 128: nmb * 512 + (j + 1) * 128],
                            sKVb[:, hb + mc * 65: hb + mc * 65 + 64],
                            sPqm4[par][:, j * 256 + mc * 128: j * 256 + (mc + 1) * 128],
                            start=(mc == 0), stop=(mc == 1)), "pe")
                pe2mk[idx] = p.at("pe")
                p.wait_at(dv, "pe", pe2mk[idx])
                p.dvx(dv, dv.tensor_copy(
                    sMerged[rows, (h // 2) * T + g * 512:(h // 2) * T + (g + 1) * 512],
                    ppool[rows, nmb * 512: nmb * 512 + 512]))
                dv3mk[idx] = p.at("dv")

        for tt in range(T // TT):
            tsl = slice(tt * TT, (tt + 1) * TT)
            p.wait(sy, "pe", "dv", "ac")
            p.inc(sy.dma_start(out=sXo[:, :].rearrange("p (c t) -> p c t", c=8),
                               in_=x2_r[:, :, tsl]), "ld", 16)
            p.wait(pe, "dv", "ac")
            for m in range(8):
                for k in range(8):
                    p.inc(pe.matmul(bank(m % 4), sWo[:, k * 1024 + m * 128:(k * 1024) + (m + 1) * 128],
                                    sMerged[:, k * T + tt * TT: k * T + (tt + 1) * TT],
                                    start=(k == 0), stop=(k == 7)), "pe")
                p.wait(dv, "pe", "ld")
                p.dvx(dv, dv.scalar_tensor_tensor(sRaw[:, m * TT:(m + 1) * TT],
                                                  bank(m % 4), sBo[:, m:m + 1],
                                                  sXo[:, m * TT:(m + 1) * TT],
                                                  OP.add, OP.add))
                p.wait(pe, "dv")
            p.wait(pe, "dv")
            for c in range(8):
                p.inc(pe.matmul(bankp(4, 0, 1, TT), C.sOnesCol[:, :],
                                sRaw[:, c * TT:(c + 1) * TT],
                                start=(c == 0), stop=(c == 7)), "pe")
            p.wait(ac, "dv", "st")
            p.acx(ac, ac.activation(out=sX3[:, :], in_=sRaw[:, :], func=AF.Square))
            p.wait(pe, "ac")
            for c in range(8):
                p.inc(pe.matmul(bankp(5, 0, 1, TT), C.sOnesCol[:, :],
                                sX3[:, c * TT:(c + 1) * TT],
                                start=(c == 0), stop=(c == 7)), "pe")
            p.wait(dv, "pe")
            p.dvx(dv, dv.tensor_scalar(sR1[:, :], bankp(4, 0, 1, TT), 1.0 / D_MODEL,
                                       None, OP.mult))
            p.dvx(dv, dv.tensor_scalar(sR2[:, :], bankp(5, 0, 1, TT), 1.0 / D_MODEL,
                                       None, OP.mult))
            p.dvx(dv, dv.tensor_tensor(sR3[:, :], sR1[:, :], sR1[:, :], OP.mult))
            p.dvx(dv, dv.tensor_tensor(sR2[:, :], sR2[:, :], sR3[:, :], OP.subtract))
            p.wait(ac, "dv")
            p.acx(ac, ac.activation(out=sR3[:, :], in_=sR2[:, :], func=AF.Ln, bias=sEpsR[:, :]))
            p.acx(ac, ac.activation(out=sR2[:, :], in_=sR3[:, :], func=AF.Exp, scale=-0.5))
            p.wait(dv, "ac")
            p.dvx(dv, dv.scalar_tensor_tensor(sR3[:, :], sR1[:, :], -1.0, sR2[:, :],
                                              OP.mult, OP.mult))
            p.wait(pe, "dv")
            p.inc(pe.matmul(bank(6, TT), C.sOnes[:, :], sR2[:, :], start=True, stop=True), "pe")
            p.inc(pe.matmul(bank(7, TT), C.sOnes[:, :], sR3[:, :], start=True, stop=True), "pe")
            p.wait(dv, "pe", "st")
            p.dvx(dv, dv.tensor_copy(sLnA[:, :], bank(6, TT)))
            p.dvx(dv, dv.tensor_copy(sLnB[:, :], bank(7, TT)))
            rb_ = sLnA[:, :].unsqueeze(1).broadcast_to([128, 8, TT])
            mb_ = sLnB[:, :].unsqueeze(1).broadcast_to([128, 8, TT])
            p.dvx(dv, dv.tensor_tensor(sX3[:, :].rearrange("p (c t) -> p c t", c=8),
                                       sRaw[:, :].rearrange("p (c t) -> p c t", c=8),
                                       rb_, OP.mult))
            p.dvx(dv, dv.tensor_tensor(sX3[:, :].rearrange("p (c t) -> p c t", c=8),
                                       sX3[:, :].rearrange("p (c t) -> p c t", c=8),
                                       mb_, OP.add))
            p.dvx(dv, dv.tensor_copy(sX3b[:, :], sX3[:, :]))
            p.wait(sy, "dv")
            p.inc(sy.dma_start(out=x3f_r[:, :, tsl],
                               in_=sX3[:, :].rearrange("p (c t) -> p c t", c=8)), "st", 16)
            p.inc(sy.dma_start(out=x3b_r[:, :, tsl],
                               in_=sX3b[:, :].rearrange("p (c t) -> p c t", c=8)), "st", 16)

        p.barrier([sy, gp, pe, dv, ac])



def emit_G(nc, p, C, ppool, t):
    T = TPC
    TT = 512
    NHID = 3072

    x3f_r = t.x3fD.rearrange("(c p) t -> p c t", p=128)
    x3b_r = t.x3bD.rearrange("(c p) t -> p c t", p=128)
    out_r = t.outf.rearrange("(c p) t -> p c t", p=128)

    with ExitStack() as st:
        sb = lambda name, shape, dt=F32: st.enter_context(nc.sbuf_tensor("G" + name, shape, dt))
        sWg = sb("sWg", [128, 8 * NHID], BF16)
        sWu = sb("sWu", [128, 8 * NHID], BF16)
        sWdc = sb("sWdc", [128, 24 * 128], BF16)
        sWdc2 = sb("sWdc2", [128, 24 * 128], BF16)
        sXb = sb("sXb", [128, 8 * TT], BF16)
        sXf = sb("sXf", [128, 8 * TT])
        sGs = sb("sGs", [128, TT], BF16)
        sGs2 = sb("sGs2", [128, TT], BF16)
        sH1 = sb("sH1", [128, 24 * TT], BF16)
        sRaw = sb("sRaw", [128, 8 * TT])
        sSq = sb("sSq", [128, TT])
        sR1 = sb("sR1", [1, TT])
        sR2 = sb("sR2", [1, TT])
        sR3 = sb("sR3", [1, TT])
        sOut = sb("sOut", [128, 8 * TT])
        sLnA = sb("sLnA", [128, TT])
        sLnB = sb("sLnB", [128, TT])
        sEpsR = sb("sEpsR", [1, 1])

        def bank(i, w=512, off=0):
            return ppool[:, i * 512 + off: i * 512 + off + w]

        def bankp(i, p0, prows, w, off=0):
            return ppool[p0:p0 + prows, i * 512 + off: i * 512 + off + w]

        sy, gp, pe, dv, ac = nc.sync, nc.gpsimd, nc.tensor, nc.vector, nc.scalar

        p.wait(sy, "wdp")
        p.inc(sy.dma_start(out=sWg[:, :].rearrange("p (c m) -> p c m", c=8),
                           in_=t.wgB.rearrange("(c p) m -> p c m", p=128)), "ld", 16)
        p.inc(sy.dma_start(out=sWu[:, :].rearrange("p (c m) -> p c m", c=8),
                           in_=t.wuB.rearrange("(c p) m -> p c m", p=128)), "ld", 16)
        p.wait(dv, "ld")
        p.dvx(dv, dv.memset(sEpsR[:, :], LNEPS))
        p.wait(pe, "ld", "dv")
        dprev_pe = {}
        dprev_dv = {}

        for tt in range(T // TT):
            tsl = slice(tt * TT, (tt + 1) * TT)
            p.wait(sy, "pe", "dv", "ac")
            p.inc(sy.dma_start(out=sXb[:, :].rearrange("p (c t) -> p c t", c=8),
                               in_=x3b_r[:, :, tsl]), "ld", 16)
            p.inc(sy.dma_start(out=sXf[:, :].rearrange("p (c t) -> p c t", c=8),
                               in_=x3f_r[:, :, tsl]), "ld", 16)
            p.wait(pe, "ld")
            gcon_dv = [0] * 24
            gcon_ac = [0] * 24
            for m in range(24):
                gb, ub = (bank(0), bank(1)) if m % 2 == 0 else (bank(4), bank(5))
                if m >= 2:
                    p.wait_at(pe, "dv", gcon_dv[m - 2])
                    p.wait_at(pe, "ac", gcon_ac[m - 2])
                elif m == 0:
                    p.wait(pe, "dv", "ac")
                for k in range(8):
                    p.inc(pe.matmul(gb, sWg[:, k * NHID + m * 128: k * NHID + (m + 1) * 128],
                                    sXb[:, k * TT:(k + 1) * TT],
                                    start=(k == 0), stop=(k == 7)), "pe")
                for k in range(8):
                    p.inc(pe.matmul(ub, sWu[:, k * NHID + m * 128: k * NHID + (m + 1) * 128],
                                    sXb[:, k * TT:(k + 1) * TT],
                                    start=(k == 0), stop=(k == 7)), "pe")
                p.wait(ac, "pe")
                sgs = sGs if m % 2 == 0 else sGs2
                p.acx(ac, ac.activation(out=sgs[:, :], in_=gb, func=AF.Silu, scale=QS))
                gcon_ac[m] = p.at("ac")
                p.wait(dv, "ac")
                p.dvx(dv, dv.scalar_tensor_tensor(sH1[:, m * TT:(m + 1) * TT],
                                                  ub, QS, sgs[:, :],
                                                  OP.mult, OP.mult))
                gcon_dv[m] = p.at("dv")
            for dm in range(8):
                par = dm % 2
                wb = sWdc if par == 0 else sWdc2
                db = bank(2) if par == 0 else bank(6)
                if par in dprev_pe:
                    p.wait_at(gp, "pe", dprev_pe[par])
                p.inc(gp.dma_start(out=wb[:, :].rearrange("p (c m) -> p c m", c=24),
                                   in_=t.wdB[dm].rearrange("(c p) m -> p c m", p=128)), "gld", 16)
                if par in dprev_dv:
                    p.wait_at(pe, "dv", dprev_dv[par])
                if dm == 0:
                    p.wait(pe, "dv")
                p.wait(pe, "gld")
                for k in range(24):
                    p.inc(pe.matmul(db, wb[:, k * 128:(k + 1) * 128],
                                    sH1[:, k * TT:(k + 1) * TT],
                                    start=(k == 0), stop=(k == 23)), "pe")
                dprev_pe[par] = p.at("pe")
                p.wait(dv, "pe", "ld")
                p.dvx(dv, dv.scalar_tensor_tensor(sRaw[:, dm * TT:(dm + 1) * TT],
                                                  db, QS, sXf[:, dm * TT:(dm + 1) * TT],
                                                  OP.mult, OP.add))
                dprev_dv[par] = p.at("dv")
            p.wait(pe, "dv")
            for c in range(8):
                p.inc(pe.matmul(bankp(4, 0, 1, TT), C.sOnesCol[:, :],
                                sRaw[:, c * TT:(c + 1) * TT],
                                start=(c == 0), stop=(c == 7)), "pe")
            p.wait(ac, "dv", "st")
            p.acx(ac, ac.activation(out=sOut[:, :], in_=sRaw[:, :], func=AF.Square))
            p.wait(pe, "ac")
            for c in range(8):
                p.inc(pe.matmul(bankp(5, 0, 1, TT), C.sOnesCol[:, :],
                                sOut[:, c * TT:(c + 1) * TT],
                                start=(c == 0), stop=(c == 7)), "pe")
            p.wait(dv, "pe")
            p.dvx(dv, dv.tensor_scalar(sR1[:, :], bankp(4, 0, 1, TT), 1.0 / D_MODEL,
                                       None, OP.mult))
            p.dvx(dv, dv.tensor_scalar(sR2[:, :], bankp(5, 0, 1, TT), 1.0 / D_MODEL,
                                       None, OP.mult))
            p.dvx(dv, dv.tensor_tensor(sR3[:, :], sR1[:, :], sR1[:, :], OP.mult))
            p.dvx(dv, dv.tensor_tensor(sR2[:, :], sR2[:, :], sR3[:, :], OP.subtract))
            p.wait(ac, "dv")
            p.acx(ac, ac.activation(out=sR3[:, :], in_=sR2[:, :], func=AF.Ln, bias=sEpsR[:, :]))
            p.acx(ac, ac.activation(out=sR2[:, :], in_=sR3[:, :], func=AF.Exp, scale=-0.5))
            p.wait(dv, "ac")
            p.dvx(dv, dv.scalar_tensor_tensor(sR3[:, :], sR1[:, :], -1.0, sR2[:, :],
                                              OP.mult, OP.mult))
            p.wait(pe, "dv")
            p.inc(pe.matmul(bank(6, TT), C.sOnes[:, :], sR2[:, :], start=True, stop=True), "pe")
            p.inc(pe.matmul(bank(7, TT), C.sOnes[:, :], sR3[:, :], start=True, stop=True), "pe")
            p.wait(dv, "pe", "st")
            p.dvx(dv, dv.tensor_copy(sLnA[:, :], bank(6, TT)))
            p.dvx(dv, dv.tensor_copy(sLnB[:, :], bank(7, TT)))
            rb_ = sLnA[:, :].unsqueeze(1).broadcast_to([128, 8, TT])
            mb_ = sLnB[:, :].unsqueeze(1).broadcast_to([128, 8, TT])
            p.dvx(dv, dv.tensor_tensor(sOut[:, :].rearrange("p (c t) -> p c t", c=8),
                                       sRaw[:, :].rearrange("p (c t) -> p c t", c=8),
                                       rb_, OP.mult))
            p.dvx(dv, dv.tensor_tensor(sOut[:, :].rearrange("p (c t) -> p c t", c=8),
                                       sOut[:, :].rearrange("p (c t) -> p c t", c=8),
                                       mb_, OP.add))
            p.wait(sy, "dv")
            p.inc(sy.dma_start(out=out_r[:, :, tsl],
                               in_=sOut[:, :].rearrange("p (c t) -> p c t", c=8)), "st", 16)

        p.barrier([sy, gp, pe, dv, ac])


SEM_NAMES = ["ld", "gld", "pe", "dv", "ac", "st", "sc", "zz", "gp", "wdp",
             "cgy", "cx1", "csc", "cmo", "cm2", "ckv"]


def build_mega(upto="G", dumps=()):
    nc = bass.Bass("TRN2", target_bir_lowering=False)
    t = NS()
    dumps = set(dumps)

    def ein(name, shape, dt=F32):
        setattr(t, name, nc.dram_tensor(name, shape, dt, kind="ExternalInput").ap())

    def intl(name, shape, dt, addr_space="Local"):
        kind = "ExternalOutput" if name in dumps else "Internal"
        setattr(t, name, nc.dram_tensor(name, shape, dt, kind=kind,
                                        addr_space=addr_space).ap())

    ein("xb", [D_MODEL, T_FULL]); ein("wzx", [D_MODEL, 1024]); ein("wbc", [D_MODEL, 256])
    ein("wdt", [D_MODEL, 8]); ein("cw", [512, K_CONV]); ein("cb", [128, 4])
    ein("alog", [8, 1]); ein("dcol", [128, 4]); ein("dtb", [8, 1]); ein("nw", [128, 8])
    ein("selc", [8, 1024])
    ein("xo", [D_MODEL, TPC]); ein("wg", [D_MODEL, 8]); ein("woutT", [2048, D_MODEL])
    ein("goffB", [128, 16], U32)
    ein("w1", [D_MODEL, 2048]); ein("w3", [D_MODEL, 2048]); ein("w2", [2048, D_MODEL])
    ein("soff", [8, 1], U32); ein("rhiRow", [1, 128])
    ein("wqT", [D_MODEL, D_MODEL]); ein("wkT", [D_MODEL, D_MODEL]); ein("wvT", [D_MODEL, D_MODEL])
    ein("projT", [128, 8 * 512]); ein("ones2", [128, 2])
    ein("woT", [D_MODEL, D_MODEL]); ein("bo", [128, 8]); ein("selB", [32, 32 * 128])
    ein("wgT", [D_MODEL, 3072], I32); ein("wuT", [D_MODEL, 3072], I32)
    ein("wd3", [8, 3072, 128], I32)
    ein("onesr", [1, 128]); ein("trilc", [128, 128]); ein("eyeb", [128, 128], BF16)
    ein("eyef", [128, 128]); ein("iota", [128, 1]); ein("triu1", [128, 128])

    intl("yD", [2048, 2048], BF16)
    intl("ygAll", [8192, 2048], BF16,
         addr_space=maybe_share_collective_output_space("AllGather", RG4, prefer_local=True))
    intl("x1fD", [D_MODEL, TPC], F32)
    intl("tmD", [TPC, D_MODEL], BF16)
    intl("x1all", [16384, D_MODEL], BF16,
         addr_space=maybe_share_collective_output_space("AllGather", RG8, prefer_local=True))
    intl("sctD", [8, TPC], F32)
    intl("scAll", [64, TPC], F32,
         addr_space=maybe_share_collective_output_space("AllGather", RG8, prefer_local=True))
    intl("xbuf", [4096, 1032], BF16)
    intl("moeD", [16384, D_MODEL], BF16)
    intl("moeRS", [1024, D_MODEL], BF16,
         addr_space=maybe_share_collective_output_space("ReduceScatter", RG8, prefer_local=True))
    intl("moeRS2", [1024, D_MODEL], BF16,
         addr_space=maybe_share_collective_output_space("ReduceScatter", RG8, prefer_local=True))
    intl("x2fD", [D_MODEL, TPC], F32)
    intl("x2bD", [D_MODEL, TPC], BF16)
    intl("phiqD", [TPC, 16 * 256], BF16)
    intl("kvpD", [16, 2, 128, 65], F32)
    intl("kvpS", [16, 2, 128, 65], F32,
         addr_space=maybe_share_collective_output_space("AllReduce", RG4, prefer_local=True))
    intl("x3fD", [D_MODEL, TPC], F32)
    intl("x3bD", [D_MODEL, TPC], BF16)
    intl("wgB", [D_MODEL, 3072], BF16)
    intl("wuB", [D_MODEL, 3072], BF16)
    intl("wdB", [8, 3072, 128], BF16)
    t.outf = nc.dram_tensor("outf", [D_MODEL, TPC], F32, kind="ExternalOutput").ap()

    with ExitStack() as outer:
        p = P(nc, outer, SEM_NAMES)
        C = NS()
        C.sOnes = outer.enter_context(nc.sbuf_tensor("cOnes", [1, 128], F32))
        C.sOnesCol = outer.enter_context(nc.sbuf_tensor("cOnesCol", [128, 1], F32))
        C.sTril = outer.enter_context(nc.sbuf_tensor("cTril", [128, 128], F32))
        C.sTriu1 = outer.enter_context(nc.sbuf_tensor("cTriu1", [128, 128], F32))
        C.sEyeB = outer.enter_context(nc.sbuf_tensor("cEyeB", [128, 128], BF16))
        C.sEyeF = outer.enter_context(nc.sbuf_tensor("cEyeF", [128, 128], F32))
        C.sIota = outer.enter_context(nc.sbuf_tensor("cIota", [128, 1], F32))
        C.sZero = outer.enter_context(nc.sbuf_tensor("cZero", [128, 1024], BF16))
        C.sSoff = outer.enter_context(nc.sbuf_tensor("cSoff", [8, 1], U32))
        C.sOnesG = outer.enter_context(nc.sbuf_tensor("cOnesG", [128, 128], F32))
        ppool = outer.enter_context(nc.psum_tensor("ppool", [128, 4096], F32))

        sy, gp, pe, dv, ac = nc.sync, nc.gpsimd, nc.tensor, nc.vector, nc.scalar

        p.inc(sy.dma_start(out=C.sOnes[:, :], in_=t.onesr[:, :]), "ld", 16)
        p.inc(sy.dma_start(out=C.sTril[:, :], in_=t.trilc[:, :]), "ld", 16)
        p.inc(sy.dma_start(out=C.sTriu1[:, :], in_=t.triu1[:, :]), "ld", 16)
        p.inc(sy.dma_start(out=C.sEyeB[:, :], in_=t.eyeb[:, :]), "ld", 16)
        p.inc(sy.dma_start(out=C.sEyeF[:, :], in_=t.eyef[:, :]), "ld", 16)
        p.inc(sy.dma_start(out=C.sIota[:, :], in_=t.iota[:, :]), "ld", 16)
        p.inc(sy.dma_start(out=C.sSoff[:, :], in_=t.soff[:, :]), "ld", 16)
        p.wait(dv, "ld")
        p.dvx(dv, dv.memset(C.sOnesCol[:, :], 1.0))
        p.dvx(dv, dv.memset(C.sOnesG[:, :], 1.0))
        p.dvx(dv, dv.memset(C.sZero[:, :], 0.0))
        p.wait(ac, "dv")
        moe_r = t.moeD.rearrange("(r p) m -> p r m", p=128)
        for i in range(128):
            p.inc(ac.dma_start(out=moe_r[:, i, :], in_=C.sZero[:, :]), "zz", 16)

        emit_A(nc, p, C, ppool, t)
        p.inc(gp.dma_start(out=t.wgB[:, :], in_=t.wgT[:, :]), "wdp", 16)
        p.inc(gp.dma_start(out=t.wuB[:, :], in_=t.wuT[:, :]), "wdp", 16)
        p.inc(gp.dma_start(out=t.wdB[:, :, :], in_=t.wd3[:, :, :]), "wdp", 16)
        if upto == "A":
            p.barrier([sy, gp, pe, dv, ac])
            return nc
        p.inc(gp.collective_compute("AllGather", OP.bypass, replica_groups=RG4,
                                    ins=[t.yD[:, :]], outs=[t.ygAll[:, :]]), "cgy")

        emit_B(nc, p, C, ppool, t)
        if upto == "B":
            p.barrier([sy, gp, pe, dv, ac])
            return nc
        p.inc(gp.collective_compute("AllGather", OP.bypass, replica_groups=RG8,
                                    ins=[t.sctD[:, :]], outs=[t.scAll[:, :]]), "csc")
        p.inc(gp.collective_compute("AllGather", OP.bypass, replica_groups=RG8,
                                    ins=[t.tmD[:, :]], outs=[t.x1all[:, :]]), "cx1")

        emit_C(nc, p, C, ppool, t)
        if upto == "C":
            p.barrier([sy, gp, pe, dv, ac])
            return nc
        p.inc(gp.collective_compute("ReduceScatter", OP.add, replica_groups=RG8,
                                    ins=[t.moeD[0:8192, :]], outs=[t.moeRS[:, :]]), "cmo")
        p.inc(gp.collective_compute("ReduceScatter", OP.add, replica_groups=RG8,
                                    ins=[t.moeD[8192:16384, :]], outs=[t.moeRS2[:, :]]), "cm2")

        emit_D(nc, p, C, ppool, t)
        if upto == "D":
            p.barrier([sy, gp, pe, dv, ac])
            return nc

        emit_E(nc, p, C, ppool, t)
        p.inc(gp.collective_compute(
            "AllReduce", OP.add, replica_groups=RG4,
            ins=[t.kvpD.rearrange("h k p m -> (h k p) m")],
            outs=[t.kvpS.rearrange("h k p m -> (h k p) m")]), "ckv")

        emit_F(nc, p, C, ppool, t)
        if upto == "F":
            p.barrier([sy, gp, pe, dv, ac])
            return nc

        emit_G(nc, p, C, ppool, t)
    return nc


def _rhi_row():
    i = np.arange(128)
    rhi = 8 * (i // 16) + (i % 16) % 8 + 64 * ((i % 16) >= 8)
    return rhi.astype(np.float32).reshape(1, 128)


def _proj_blockdiag(proj):
    pj = np.transpose(proj, (0, 2, 1))
    bd = np.zeros((128, 8 * 512), np.float32)
    for hp in range(8):
        bd[0:64, hp * 512: hp * 512 + 256] = pj[2 * hp]
        bd[64:128, hp * 512 + 256: hp * 512 + 512] = pj[2 * hp + 1]
    return bd


def prep_mega_inputs(inputs):
    x = np.asarray(inputs["L0_attn_x"])
    in_proj_w = np.asarray(inputs["L0_attn_in_proj_w"])
    out_proj_w = np.asarray(inputs["L0_attn_out_proj_w"])
    consts = _consts()
    in_maps = []
    for cid in range(N_CORES):
        b, g = cid // 4, cid % 4
        q, e = cid % 4, cid
        hsl = slice(8 * g, 8 * g + 8)
        chsl = slice(512 * g, 512 * g + 512)
        wz = in_proj_w[chsl]
        wx = in_proj_w[2048 + 512 * g: 2048 + 512 * g + 512]
        goffB = np.zeros((128, 16), np.uint32)
        for c in range(16):
            goffB[:, c] = 2048 * (c // 4) + 512 * q + 128 * (c % 4) + np.arange(128)
        m = dict(
            xb=f32(x[b].T),
            wzx=f32(np.concatenate([wz, wx], 0).T),
            wbc=f32(in_proj_w[4096:4352].T),
            wdt=f32(in_proj_w[4352 + 8 * g: 4352 + 8 * g + 8].T),
            cw=f32(inputs["L0_attn_conv_w"][chsl]),
            cb=f32(np.asarray(inputs["L0_attn_conv_b"])[chsl].reshape(4, 128).T),
            alog=f32(np.asarray(inputs["L0_attn_A_log"])[hsl][:, None]),
            dcol=f32(np.repeat(np.asarray(inputs["L0_attn_D"])[hsl], 64).reshape(4, 128).T),
            dtb=f32(np.asarray(inputs["L0_attn_dt_bias"])[hsl][:, None]),
            nw=f32(np.asarray(inputs["L0_attn_norm_w"]).reshape(8, 128).T),
            selc=f32(np.repeat(np.eye(8, dtype=np.float32), 128, axis=1)),
            xo=f32(x[b].T[:, 2048 * q: 2048 * q + 2048]),
            wg=f32(inputs["L0_ffn_wg"]),
            woutT=f32(out_proj_w.T),
            goffB=goffB,
            w1=f32(inputs["L0_ffn_w1"][e]),
            w3=f32(inputs["L0_ffn_w3"][e]),
            w2=f32(inputs["L0_ffn_w2"][e]),
            soff=(8 * np.arange(8, dtype=np.uint32) + e).reshape(8, 1),
            rhiRow=_rhi_row(),
            wqT=f32(np.asarray(inputs["L1_attn_Wq"]).T),
            wkT=f32(np.asarray(inputs["L1_attn_Wk"]).T),
            wvT=f32(np.asarray(inputs["L1_attn_Wv"]).T),
            projT=_proj_blockdiag(np.asarray(inputs["L1_attn_proj"])),
            ones2=np.concatenate([np.repeat([[1.0, 0.0]], 64, 0),
                                  np.repeat([[0.0, 1.0]], 64, 0)]).astype(np.float32),
            woT=f32(np.asarray(inputs["L1_attn_Wo"]).T),
            bo=f32(np.asarray(inputs["L1_attn_bo"]).reshape(8, 128).T),
            selB=f32(np.repeat(np.eye(32, dtype=np.float32), 128, axis=1)),
            wgT=np.ascontiguousarray(np.asarray(inputs["L1_ffn_w_gate"]).T.astype(np.int32)),
            wuT=np.ascontiguousarray(np.asarray(inputs["L1_ffn_w_up"]).T.astype(np.int32)),
            wd3=np.ascontiguousarray(np.asarray(inputs["L1_ffn_w_down"]).reshape(
                8, 128, 3072).transpose(0, 2, 1).astype(np.int32)),
            **consts,
        )
        in_maps.append(m)
    return in_maps


def kernel_mega(**inputs):
    nc = build_mega()
    res = run_bass_kernel_spmd(nc, prep_mega_inputs(inputs),
                               core_ids=list(range(N_CORES))).results
    out = np.zeros((2, 8192, 1024), np.float32)
    for cid in range(N_CORES):
        b, q = cid // 4, cid % 4
        out[b, 2048 * q: 2048 * q + 2048] = res[cid]["outf"].T
    return out


def kernel(**inputs):
    return kernel_mega(**inputs)
